# version 03
# baseline (speedup 1.0000x reference)
import numpy as np
from contextlib import ExitStack

import concourse.bass as bass
import concourse.mybir as mybir
from concourse.bass_utils import run_bass_kernel_spmd

F32 = mybir.dt.float32
BF16 = mybir.dt.bfloat16
U8 = mybir.dt.uint8
AX = mybir.AxisListType
ALU = mybir.AluOpType
ACT = mybir.ActivationFunctionType

D = 2048
S = 2048
NT = 16
NH = 16
NG = 4
EPS = 1e-5
QSCALE = 1.0 / float(np.sqrt(2048.0))
NEG = -1.0e30

DBG = ()
NPHASE = 5

A1 = 0
A2 = 65536
A3 = 131072
WG = 196608
CN = 210944
TOTAL = 212832


def _sz(dt):
    return mybir.dt.size(dt)


class Ctr:
    def __init__(self, stack, nc, name):
        self.h = stack.enter_context(nc.semaphore(name))
        self.n = 0

    def inc(self, ins, k=1):
        ins.then_inc(self.h, k)
        self.n += k
        return self.n


def build():
    nc = bass.Bass("TRN2", target_bir_lowering=False, debug=False)

    x_h = nc.dram_tensor("x", [S, D], F32, kind="ExternalInput").ap()
    qkv_h = nc.dram_tensor("qkv", [D, 3 * D], F32, kind="ExternalInput").ap()
    oproj_h = nc.dram_tensor("o_proj", [D, D], F32, kind="ExternalInput").ap()
    gamma_h = nc.dram_tensor("gamma", [1, D], F32, kind="ExternalInput").ap()
    beta_h = nc.dram_tensor("beta", [1, D], F32, kind="ExternalInput").ap()
    A_h = nc.dram_tensor("A", [D, 1024], F32, kind="ExternalInput").ap()
    B_h = nc.dram_tensor("B", [8 * 2048, 128], F32, kind="ExternalInput").ap()
    W_h = nc.dram_tensor("W", [D, D], F32, kind="ExternalInput").ap()
    bias_h = nc.dram_tensor("bias", [1, D], F32, kind="ExternalInput").ap()

    out_h = nc.dram_tensor("out", [S, D], F32, kind="ExternalOutput").ap()

    qsp_h = nc.dram_tensor("q_sp", [NH * 128, NG * 512], BF16, kind="Internal").ap()
    aosp_h = nc.dram_tensor("ao_sp", [NH * 128, NG * 512], BF16, kind="Internal").ap()
    x1sp_h = nc.dram_tensor("x1_sp", [S, D], F32, kind="Internal").ap()
    y3sp_h = nc.dram_tensor("y3_sp", [NH * 128, NT * 128], BF16, kind="Internal").ap()
    gsp_h = nc.dram_tensor("g_sp", [8 * 128, NT * 128], BF16, kind="Internal").ap()
    wbf_h = nc.dram_tensor("w_bf", [D, D], BF16, kind="Internal").ap()
    bbf_h = nc.dram_tensor("b_bf", [8 * 2048, 128], BF16, kind="Internal").ap()

    dbg_aps = {}
    def _dbg(name, shape, dtype):
        if name in DBG:
            dbg_aps[name] = nc.dram_tensor("dbg_" + name, shape, dtype,
                                           kind="ExternalOutput").ap()
    _dbg("xn", [128, NH * D], BF16)
    _dbg("stats", [128, 2 * NT], F32)
    _dbg("kf", [128, NH * D], BF16)
    _dbg("vt", [128, NT * D], BF16)
    _dbg("lsum", [128, NT], F32)
    _dbg("qsp", [NH * 128, NG * 512], BF16)
    _dbg("ao", [NH * 128, NG * 512], BF16)
    _dbg("x1", [2048, 2048], F32)
    _dbg("y3", [NH * 128, NT * 128], BF16)
    _dbg("gsp", [8 * 128, NT * 128], BF16)
    _dbg("dvec", [128, 2192], F32)

    with ExitStack() as stack:
        arena = stack.enter_context(nc.sbuf_tensor("arena", [128, TOTAL], U8))

        def V(off, cols, dtype, parts=128):
            nb = cols * _sz(dtype)
            assert off % 32 == 0 and off + nb <= TOTAL, (off, nb)
            return arena[0:parts, off:off + nb].bitcast(dtype)

        id_bf = V(CN, 128, BF16)
        mask_f = V(CN + 256, 128, F32)
        is1p1 = V(CN + 768, NT, F32)
        nm1s = V(CN + 832, NT, F32)
        l_sums = V(CN + 896, NT, F32)
        inv_l = V(CN + 960, NT, F32)
        id_f32 = V(CN + 1024, 128, F32)

        xin = [V(A2 + i * 8192, D, F32) for i in range(2)]
        y1f = V(A2 + 16384, D, F32)
        y2f = V(A2 + 24576, D, F32)
        ttmp = V(A2 + 32768, D, F32)
        xnt_bf = V(A2 + 16384, D, BF16)
        grep = V(A2 + 40960, D, F32)
        brep = V(A2 + 49152, D, F32)
        st4 = V(A2 + 57344, 24, F32)
        mv2 = V(A2 + 57440, 2, F32)
        sc1 = V(A2 + 57472, 8, F32)

        xn = V(A3, NH * D, BF16)
        kf = V(A1, NH * D, BF16)
        vt = V(A2, NT * D, BF16)

        wkb = [V(WG + i * 4096, NH * 128, BF16) for i in range(2)]
        qstage = V(WG + 8192, D, BF16)
        wvb = [V(A1 + i * 16384, NH * 512, BF16) for i in range(2)]

        sems = {}
        sems_extra = ("sDBG",)
        for name in ("sSET", "sCON", "sX", "sSq", "sRc", "sV1", "sXN", "sT1",
                     "sE1", "sWV", "sTV", "sEV", "sWK", "sTK", "sEK",
                     "sWQ", "sTQ", "sEQ", "sQS",
                     "sQL", "sTS", "sVM", "sSP", "sTT", "sVT", "sPZ",
                     "sTA", "sEA", "sAO",
                     "sWC", "sOW3", "sAW", "sAL", "sXL", "sTO", "sVX", "sX1W",
                     "sY3W", "sGW", "sRc3", "sSq3", "sSQ3", "sVY", "sTY",
                     "sVYE", "sTVc", "sVG", "sTG", "sVGE", "sVK", "sV3",
                     "sWT", "sBF", "sL3", "sT3", "sVO", "sOW") + sems_extra:
            sems[name] = Ctr(stack, nc, name)
        g = sems

        block = stack.enter_context(nc.Block())

        def setup_sync(eng):
            eng.dma_start(out=grep, in_=gamma_h.partition_broadcast(128)).then_inc(
                g["sCON"].h, 16)
            eng.dma_start(out=brep, in_=beta_h.partition_broadcast(128)).then_inc(
                g["sCON"].h, 16)
            g["sCON"].n += 32

        def setup_gpsimd(eng):
            eng.memset(mask_f, 0.0)
            ins = eng.affine_select(out=mask_f, in_=mask_f, pattern=[[-1, 128]],
                                    compare_op=ALU.is_ge, fill=NEG, base=0,
                                    channel_multiplier=1)
            g["sSET"].inc(ins)
            eng.memset(id_f32, 0.0)
            ins = eng.affine_select(out=id_f32, in_=id_f32, pattern=[[-1, 128]],
                                    compare_op=ALU.not_equal, fill=1.0, base=0,
                                    channel_multiplier=1)
            g["sSET"].inc(ins)

        block.sync(setup_sync)
        block.gpsimd(setup_gpsimd)

        def setup_vector(eng):
            eng.wait_ge(g["sSET"].h, 2)
            ins = eng.tensor_copy(out=id_bf, in_=id_f32)
            g["sSET"].inc(ins)

        block.vector(setup_vector)

        ph1_ps = ExitStack()
        tp_ps = [ph1_ps.enter_context(nc.psum_tensor(f"tp{i}", [128, 1024], BF16))
                 for i in range(2)]

        def ph1_sync(eng):
            for i in range(NT):
                if i >= 2:
                    eng.wait_ge(g["sXN"].h, i - 1)
                eng.dma_start(out=xin[i % 2],
                              in_=x_h[i * 128:(i + 1) * 128, :]).then_inc(
                    g["sX"].h, 16)
            g["sX"].n += NT * 16

        def ph1_vector(eng):
            for i in range(NT):
                xt = xin[i % 2]
                eng.wait_ge(g["sX"].h, 16 * (i + 1))
                if i >= 1:
                    eng.wait_ge(g["sT1"].h, 4 * i)
                for k in range(4):
                    ins = eng.bn_stats(out=st4[:, 6 * k:6 * k + 6],
                                       in_=xt[:, 512 * k:512 * (k + 1)])
                g["sV1"].inc(ins)
                eng.wait_ge(g["sV1"].h, g["sV1"].n)
                ins = eng.bn_aggr(out=mv2, in_=st4)
                g["sV1"].inc(ins)
                eng.wait_ge(g["sV1"].h, g["sV1"].n)
                ins = eng.tensor_scalar(out=sc1[:, 0:1], in0=mv2[:, 1:2],
                                        scalar1=EPS, scalar2=None, op0=ALU.add)
                g["sRc"].inc(ins)
                eng.wait_ge(g["sSq"].h, 2 * i + 1)
                ins = eng.reciprocal(out=sc1[:, 6:7], in_=sc1[:, 4:5])
                g["sV1"].inc(ins)
                eng.wait_ge(g["sV1"].h, g["sV1"].n)
                eng.tensor_scalar(out=is1p1[:, i:i + 1], in0=sc1[:, 6:7],
                                  scalar1=1.0, scalar2=None, op0=ALU.add)
                eng.tensor_scalar(out=nm1s[:, i:i + 1], in0=mv2[:, 0:1],
                                  scalar1=sc1[:, 6:7], scalar2=-1.0,
                                  op0=ALU.mult, op1=ALU.mult)
                ins = eng.tensor_scalar(out=sc1[:, 1:2], in0=mv2[:, 1:2],
                                        scalar1=sc1[:, 6:7], scalar2=sc1[:, 6:7],
                                        op0=ALU.mult, op1=ALU.mult)
                g["sV1"].inc(ins)
                eng.wait_ge(g["sV1"].h, g["sV1"].n)
                ins = eng.tensor_scalar(out=sc1[:, 1:2], in0=sc1[:, 1:2],
                                        scalar1=EPS, scalar2=None, op0=ALU.add)
                g["sRc"].inc(ins)
                eng.wait_ge(g["sSq"].h, 2 * i + 2)
                ins = eng.reciprocal(out=sc1[:, 7:8], in_=sc1[:, 5:6])
                g["sV1"].inc(ins)
                eng.wait_ge(g["sV1"].h, g["sV1"].n)
                eng.tensor_scalar(out=sc1[:, 3:4], in0=sc1[:, 6:7],
                                  scalar1=sc1[:, 7:8], scalar2=None, op0=ALU.mult)
                ins = eng.tensor_scalar(out=sc1[:, 2:3], in0=mv2[:, 0:1],
                                        scalar1=-1.0, scalar2=None, op0=ALU.mult)
                g["sV1"].inc(ins)
                eng.wait_ge(g["sV1"].h, g["sV1"].n)
                if i == 0:
                    eng.wait_ge(g["sCON"].h, 32)
                eng.scalar_tensor_tensor(out=ttmp, in0=xt, scalar=sc1[:, 2:3],
                                         in1=grep, op0=ALU.add, op1=ALU.mult)
                ins = eng.scalar_tensor_tensor(out=xnt_bf, in0=ttmp,
                                               scalar=sc1[:, 3:4], in1=brep,
                                               op0=ALU.mult, op1=ALU.add)
                g["sXN"].inc(ins)

        def ph1_tensor(eng):
            eng.wait_ge(g["sSET"].h, 3)
            for i in range(NT):
                eng.wait_ge(g["sXN"].h, i + 1)
                for gq in range(4):
                    b = (i * 4 + gq) % 2
                    if i * 4 + gq >= 2:
                        eng.wait_ge(g["sE1"].h, i * 4 + gq - 1)
                    for k in range(4):
                        c = gq * 4 + k
                        ins = eng.transpose(tp_ps[b][:, k * 128:(k + 1) * 128],
                                            xnt_bf[:, c * 128:(c + 1) * 128], id_bf)
                    g["sT1"].inc(ins)

        def ph1_scalar(eng):
            for i in range(NT):
                for k in range(2):
                    eng.wait_ge(g["sRc"].h, 2 * i + k + 1)
                    ins = eng.activation(out=sc1[:, 4 + k:5 + k],
                                         in_=sc1[:, k:k + 1], func=ACT.Sqrt)
                    g["sSq"].inc(ins)
                for gq in range(4):
                    b = (i * 4 + gq) % 2
                    eng.wait_ge(g["sT1"].h, i * 4 + gq + 1)
                    dst = xn.rearrange("p (hc t) -> p hc t", hc=NH)[
                        :, 4 * gq:4 * gq + 4, i * 128:(i + 1) * 128]
                    src = tp_ps[b][:, 0:512].rearrange("p (hc t) -> p hc t", hc=4)
                    ins = eng.copy(out=dst, in_=src)
                    g["sE1"].inc(ins)

        block.sync(ph1_sync)
        block.vector(ph1_vector)
        block.tensor(ph1_tensor)
        block.scalar(ph1_scalar)

        if "xn" in dbg_aps or "stats" in dbg_aps:
            def dbg_ph1(eng):
                eng.wait_ge(g["sE1"].h, NT * 4)
                eng.wait_ge(g["sXN"].h, NT)
                if "xn" in dbg_aps:
                    eng.dma_start(out=dbg_aps["xn"], in_=xn).then_inc(g["sCON"].h, 16)
                    g["sCON"].n += 16
                if "stats" in dbg_aps:
                    st = dbg_aps["stats"]
                    eng.dma_start(out=st[:, 0:NT], in_=is1p1).then_inc(g["sCON"].h, 16)
                    eng.dma_start(out=st[:, NT:2 * NT], in_=nm1s).then_inc(
                        g["sCON"].h, 16)
                    g["sCON"].n += 32
                eng.wait_ge(g["sCON"].h, g["sCON"].n)
            block.sync(dbg_ph1)

        ph1_ps.close()
        if NPHASE == 1:
            return nc, dbg_aps

        genv_ps = ExitStack()
        pv = [genv_ps.enter_context(nc.psum_tensor(f"pv{i}", [128, 512], F32))
              for i in range(2)]

        def genv_gpsimd(eng):
            for dvb in range(4):
                if dvb >= 2:
                    eng.wait_ge(g["sTV"].h, 16 * (dvb - 1))
                src = qkv_h[:, 2 * D + dvb * 512:2 * D + (dvb + 1) * 512]
                src = src.rearrange("(hc p) n -> p hc n", p=128)
                dst = wvb[dvb % 2].rearrange("p (hc n) -> p hc n", hc=NH)
                eng.dma_start(out=dst, in_=src).then_inc(g["sWV"].h, 16)
            g["sWV"].n += 4 * 16

        def genv_tensor(eng):
            eng.wait_ge(g["sE1"].h, NT * 4)
            for dvb in range(4):
                eng.wait_ge(g["sWV"].h, 16 * (dvb + 1))
                for tc in range(NT):
                    n = dvb * NT + tc
                    b = n % 2
                    if n >= 2:
                        eng.wait_ge(g["sEV"].h, n - 1)
                    for hc in range(NH):
                        ins = eng.matmul(
                            pv[b][:],
                            xn[:, hc * D + tc * 128:hc * D + (tc + 1) * 128],
                            wvb[dvb % 2][:, hc * 512:(hc + 1) * 512],
                            start=(hc == 0), stop=(hc == NH - 1))
                    g["sTV"].inc(ins)

        def genv_vector(eng):
            for dvb in range(4):
                for tc in range(NT):
                    n = dvb * NT + tc
                    eng.wait_ge(g["sTV"].h, n + 1)
                    ins = eng.tensor_copy(
                        out=vt[:, tc * D + dvb * 512:tc * D + (dvb + 1) * 512],
                        in_=pv[n % 2][:])
                    g["sEV"].inc(ins)

        def wcast_gpsimd(eng):
            eng.dma_start(out=wbf_h, in_=W_h).then_inc(g["sWC"].h, 16)
            eng.dma_start(out=bbf_h, in_=B_h).then_inc(g["sWC"].h, 16)
            g["sWC"].n += 32

        block.gpsimd(genv_gpsimd)
        block.gpsimd(wcast_gpsimd)
        block.tensor(genv_tensor)
        block.vector(genv_vector)
        genv_ps.close()

        genk_ps = ExitStack()
        pg = [genk_ps.enter_context(nc.psum_tensor(f"pg{i}", [128, 2048], F32))
              for i in range(2)]

        def genkq_gpsimd(eng):
            for phase, base in (("k", 1 * D), ("q", 0)):
                for dkc in range(NH):
                    n = (0 if phase == "k" else NH) + dkc
                    if n >= 2:
                        sem = g["sTK"] if n - 2 < NH else g["sTQ"]
                        eng.wait_ge(sem.h, (n - 2) % NH + 1)
                    src = qkv_h[:, base + dkc * 128:base + (dkc + 1) * 128]
                    src = src.rearrange("(hc p) c -> p hc c", p=128)
                    dst = wkb[n % 2].rearrange("p (hc c) -> p hc c", hc=NH)
                    eng.dma_start(out=dst, in_=src).then_inc(g["sWK"].h, 16)
            g["sWK"].n += 2 * NH * 16

        def genkq_tensor(eng):
            eng.wait_ge(g["sEV"].h, 64)
            for dkc in range(NH):
                eng.wait_ge(g["sWK"].h, 16 * (dkc + 1))
                b = dkc % 2
                if dkc >= 2:
                    eng.wait_ge(g["sEK"].h, dkc - 1)
                for tb in range(4):
                    for hc in range(NH):
                        ins = eng.matmul(
                            pg[b][:, tb * 512:(tb + 1) * 512],
                            wkb[dkc % 2][:, hc * 128:(hc + 1) * 128],
                            xn[:, hc * D + tb * 512:hc * D + (tb + 1) * 512],
                            start=(hc == 0), stop=(hc == NH - 1))
                g["sTK"].inc(ins)
            for dkc in range(NH):
                eng.wait_ge(g["sWK"].h, 16 * (NH + dkc + 1))
                b = dkc % 2
                if dkc >= 2:
                    eng.wait_ge(g["sEQ"].h, dkc - 1)
                else:
                    eng.wait_ge(g["sEK"].h, NH)
                for tb in range(4):
                    for hc in range(NH):
                        ins = eng.matmul(
                            pg[b][:, tb * 512:(tb + 1) * 512],
                            wkb[dkc % 2][:, hc * 128:(hc + 1) * 128],
                            xn[:, hc * D + tb * 512:hc * D + (tb + 1) * 512],
                            start=(hc == 0), stop=(hc == NH - 1))
                g["sTQ"].inc(ins)

        def genkq_vector(eng):
            for dkc in range(NH):
                eng.wait_ge(g["sTK"].h, dkc + 1)
                ins = eng.tensor_copy(out=kf[:, dkc * D:(dkc + 1) * D],
                                      in_=pg[dkc % 2][:])
                g["sEK"].inc(ins)
            for dkc in range(NH):
                eng.wait_ge(g["sTQ"].h, dkc + 1)
                if dkc >= 1:
                    eng.wait_ge(g["sQS"].h, 16 * dkc)
                ins = eng.tensor_scalar(out=qstage, in0=pg[dkc % 2][:],
                                        scalar1=QSCALE, scalar2=None,
                                        op0=ALU.mult)
                g["sEQ"].inc(ins)

        def genkq_scalar(eng):
            for dkc in range(NH):
                eng.wait_ge(g["sEQ"].h, dkc + 1)
                eng.dma_start(out=qsp_h[dkc * 128:(dkc + 1) * 128, :],
                              in_=qstage).then_inc(g["sQS"].h, 16)
            g["sQS"].n += NH * 16

        block.gpsimd(genkq_gpsimd)
        block.tensor(genkq_tensor)
        block.vector(genkq_vector)
        block.scalar(genkq_scalar)
        genk_ps.close()

        if dbg_aps:
            def dbg_gen(eng):
                eng.wait_ge(g["sEK"].h, NH)
                eng.wait_ge(g["sEV"].h, 64)
                eng.wait_ge(g["sQS"].h, NH * 16)
                if "kf" in dbg_aps:
                    eng.dma_start(out=dbg_aps["kf"], in_=kf).then_inc(g["sCON"].h, 16)
                    g["sCON"].n += 16
                if "vt" in dbg_aps:
                    eng.dma_start(out=dbg_aps["vt"], in_=vt).then_inc(g["sCON"].h, 16)
                    g["sCON"].n += 16
                if "qsp" in dbg_aps:
                    eng.dma_start(out=dbg_aps["qsp"], in_=qsp_h).then_inc(
                        g["sCON"].h, 16)
                    g["sCON"].n += 16
                eng.wait_ge(g["sCON"].h, g["sCON"].n)
            block.sync(dbg_gen)

        if NPHASE == 2:
            return nc, dbg_aps

        q_sb = V(A3, NH * 512, BF16)
        pT = [V(A3 + 16384 + i * 16384, NT * 512, BF16) for i in range(2)]
        p_sb = [V(A3 + 49152 + i * 4096, D, BF16) for i in range(2)]
        ao_st = [V(A3 + 57344 + i * 1024, 512, BF16) for i in range(2)]
        negm = V(A3 + 59392, 2, F32)
        dum2 = V(A3 + 59424, 1, F32)

        ph2_ps = ExitStack()
        psc = ph2_ps.enter_context(nc.psum_tensor("psc", [128, 2048], F32))
        ptp2 = [ph2_ps.enter_context(nc.psum_tensor(f"ptp2{i}", [128, 1024], BF16))
                for i in range(2)]
        pav = [ph2_ps.enter_context(nc.psum_tensor(f"pav{i}", [128, 512], F32))
               for i in range(2)]

        def _nbt(s_tile):
            return (s_tile + 1 + 3) // 4

        def ph2_sync(eng):
            eng.wait_ge(g["sQS"].h, NH * 16)
            for gI in range(NG):
                if gI >= 1:
                    eng.wait_ge(g["sTS"].h, 4 * gI)
                src = qsp_h[:, gI * 512:(gI + 1) * 512].rearrange(
                    "(dkc p) n -> p dkc n", p=128)
                dst = q_sb.rearrange("p (dkc n) -> p dkc n", dkc=NH)
                eng.dma_start(out=dst, in_=src).then_inc(g["sQL"].h, 16)
            g["sQL"].n += NG * 16

        def ph2_tensor(eng):
            av_idx = 0
            gbt = 0
            for gI in range(NG):
                eng.wait_ge(g["sQL"].h, 16 * (gI + 1))
                for j in range(4):
                    s_tile = 4 * gI + j
                    T = (s_tile + 1) * 128
                    if gI == 0:
                        sc_base = (j % 2) * 512
                        eng.wait_ge(g["sSP"].h, max(0, s_tile - 1))
                    elif gI == 1:
                        sc_base = (j % 2) * 1024
                        eng.wait_ge(g["sSP"].h, 4 if j < 2 else s_tile - 1)
                    else:
                        sc_base = 0
                        eng.wait_ge(g["sSP"].h, s_tile)
                    nb = (T + 511) // 512
                    for tb in range(nb):
                        N = min(512, T - tb * 512)
                        for dkc in range(NH):
                            ins = eng.matmul(
                                psc[:, sc_base + tb * 512:sc_base + tb * 512 + N],
                                q_sb[:, dkc * 512 + j * 128:dkc * 512 + (j + 1) * 128],
                                kf[:, dkc * D + tb * 512:dkc * D + tb * 512 + N],
                                start=(dkc == 0), stop=(dkc == NH - 1))
                    g["sTS"].inc(ins)
                    if gI >= 1:
                        if j == 0:
                            eng.wait_ge(g["sPZ"].h, gI)
                        ntc = 4 * gI
                        for dvc in range(4 * j, 4 * j + 4):
                            b = av_idx % 2
                            if av_idx >= 2:
                                eng.wait_ge(g["sEA"].h, av_idx - 1)
                            for tc in range(ntc):
                                ins = eng.matmul(
                                    pav[b][:],
                                    vt[:, tc * D + dvc * 128:tc * D + (dvc + 1) * 128],
                                    pT[(gI - 1) % 2][:, tc * 512:(tc + 1) * 512],
                                    start=(tc == 0), stop=(tc == ntc - 1))
                            g["sTA"].inc(ins)
                            av_idx += 1
                    eng.wait_ge(g["sSP"].h, s_tile + 1)
                    ntp = s_tile + 1
                    for bt in range(_nbt(s_tile)):
                        b = gbt % 2
                        if gbt >= 2:
                            eng.wait_ge(g["sVT"].h, gbt - 1)
                        for k in range(min(4, ntp - 4 * bt)):
                            tc = 4 * bt + k
                            ins = eng.transpose(
                                ptp2[b][:, k * 128:(k + 1) * 128],
                                p_sb[j % 2][:, tc * 128:(tc + 1) * 128], id_bf)
                        g["sTT"].inc(ins)
                        gbt += 1
            eng.wait_ge(g["sPZ"].h, NG)
            for dvc in range(16):
                b = av_idx % 2
                if av_idx >= 2:
                    eng.wait_ge(g["sEA"].h, av_idx - 1)
                for tc in range(16):
                    ins = eng.matmul(
                        pav[b][:],
                        vt[:, tc * D + dvc * 128:tc * D + (dvc + 1) * 128],
                        pT[1][:, tc * 512:(tc + 1) * 512],
                        start=(tc == 0), stop=(tc == 15))
                g["sTA"].inc(ins)
                av_idx += 1

        def ph2_vector(eng):
            gbt = 0
            for gI in range(NG):
                if gI >= 2:
                    eng.wait_ge(g["sTA"].h, 16 * (gI - 1))
                for j in range(4):
                    s_tile = 4 * gI + j
                    T = (s_tile + 1) * 128
                    sc_base = ((j % 2) * 512 if gI == 0 else
                               (j % 2) * 1024 if gI == 1 else 0)
                    eng.wait_ge(g["sTS"].h, s_tile + 1)
                    eng.tensor_add(psc[:, sc_base + s_tile * 128:sc_base + T],
                                   psc[:, sc_base + s_tile * 128:sc_base + T],
                                   mask_f)
                    ins = eng.reduce_max(out=negm[:, j % 2:j % 2 + 1],
                                         in_=psc[:, sc_base:sc_base + T],
                                         axis=AX.X, negate=True)
                    g["sVM"].inc(ins)
                    for bt in range(_nbt(s_tile)):
                        eng.wait_ge(g["sTT"].h, gbt + 1)
                        nk = min(4, s_tile + 1 - 4 * bt)
                        dst = pT[gI % 2].rearrange("p (tc c) -> p tc c", tc=NT)[
                            :, 4 * bt:4 * bt + nk, j * 128:(j + 1) * 128]
                        src = ptp2[gbt % 2][:, 0:nk * 128].rearrange(
                            "p (tc c) -> p tc c", tc=nk)
                        ins = eng.tensor_copy(out=dst, in_=src)
                        g["sVT"].inc(ins)
                        gbt += 1
                    hi = 4 * (gI + 1)
                    if s_tile + 1 < hi:
                        zv = pT[gI % 2].rearrange("p (tc c) -> p tc c", tc=NT)[
                            :, s_tile + 1:hi, j * 128:(j + 1) * 128]
                        eng.memset(zv, 0.0)
                ins = eng.tensor_copy(out=dum2, in_=negm[:, 0:1])
                g["sPZ"].inc(ins)

        def ph2_scalar(eng):
            av_idx = 0
            for gI in range(NG):
                for j in range(4):
                    s_tile = 4 * gI + j
                    T = (s_tile + 1) * 128
                    sc_base = ((j % 2) * 512 if gI == 0 else
                               (j % 2) * 1024 if gI == 1 else 0)
                    eng.wait_ge(g["sVM"].h, s_tile + 1)
                    ins = eng.activation(out=p_sb[j % 2][:, 0:T],
                                         in_=psc[:, sc_base:sc_base + T],
                                         func=ACT.Exp, bias=negm[:, j % 2:j % 2 + 1],
                                         scale=1.0,
                                         accum_out=l_sums[:, s_tile:s_tile + 1])
                    g["sSP"].inc(ins)
                    if gI >= 1:
                        for dvc in range(4 * j, 4 * j + 4):
                            eng.wait_ge(g["sTA"].h, av_idx + 1)
                            if av_idx >= 2:
                                eng.wait_ge(g["sAO"].h, 16 * (av_idx - 1))
                            b2 = av_idx % 2
                            ins = eng.copy(out=ao_st[b2], in_=pav[b2][:])
                            g["sEA"].inc(ins)
                            eng.wait_ge(g["sEA"].h, g["sEA"].n)
                            eng.dma_start(
                                out=aosp_h[dvc * 128:(dvc + 1) * 128,
                                           (gI - 1) * 512:gI * 512],
                                in_=ao_st[b2]).then_inc(g["sAO"].h, 16)
                            g["sAO"].n += 16
                            av_idx += 1
            for dvc in range(16):
                eng.wait_ge(g["sTA"].h, av_idx + 1)
                if av_idx >= 2:
                    eng.wait_ge(g["sAO"].h, 16 * (av_idx - 1))
                b2 = av_idx % 2
                ins = eng.copy(out=ao_st[b2], in_=pav[b2][:])
                g["sEA"].inc(ins)
                eng.wait_ge(g["sEA"].h, g["sEA"].n)
                eng.dma_start(out=aosp_h[dvc * 128:(dvc + 1) * 128,
                                         3 * 512:4 * 512],
                              in_=ao_st[b2]).then_inc(g["sAO"].h, 16)
                g["sAO"].n += 16
                av_idx += 1

        block.sync(ph2_sync)
        block.tensor(ph2_tensor)
        block.vector(ph2_vector)
        block.scalar(ph2_scalar)
        ph2_ps.close()

        if dbg_aps and ("ao" in dbg_aps or "lsum" in dbg_aps):
            def dbg_ph2(eng):
                eng.wait_ge(g["sAO"].h, g["sAO"].n)
                if "ao" in dbg_aps:
                    eng.dma_start(out=dbg_aps["ao"], in_=aosp_h).then_inc(
                        g["sCON"].h, 16)
                    g["sCON"].n += 16
                if "lsum" in dbg_aps:
                    eng.dma_start(out=dbg_aps["lsum"], in_=l_sums).then_inc(
                        g["sCON"].h, 16)
                    g["sCON"].n += 16
                eng.wait_ge(g["sCON"].h, g["sCON"].n)
            block.sync(dbg_ph2)

        if NPHASE == 3:
            return nc, dbg_aps

        opj_bf = V(A1, NH * D, BF16)
        WT = V(A1, NH * D, BF16)
        Bfl = V(A3 + 32768, 8 * D, BF16)
        A_bf = V(A3, NH * 1024, BF16)
        xi3 = [V(A2 + i * 8192, D, F32) for i in range(2)]
        ao_rd = V(A2 + 16384, NH * 512, BF16)
        x1_sb = V(A2 + 32768, D, F32)
        y3_sb = V(A2 + 40960, D, BF16)
        y3t_sb = V(A2 + 45056, NH * 128, BF16)
        vec_sb = V(A2 + 49152, 1024, F32)
        sq_sb = V(A2 + 53248, 1024, F32)
        G_sb = V(A2 + 57344, 1024, BF16)
        gt_sb = V(A2 + 59392, 8 * 128, BF16)
        arrows = V(A2 + 61440, 64, F32)
        scr64 = V(A2 + 61696, 64, F32)
        mask64 = V(A2 + 61952, 64, F32)
        m8 = V(A2 + 62208, 8, F32)
        v4 = V(A2 + 62240, 16, F32)
        v4s = V(A2 + 62304, 16, F32)
        st4b = V(A2 + 62368, 24, F32)
        mv2b = V(A2 + 62464, 2, F32)
        sc3 = V(A2 + 62496, 8, F32)

        ph3a_ps = ExitStack()
        po = ph3a_ps.enter_context(nc.psum_tensor("po", [128, 2048], F32))
        pvv = ph3a_ps.enter_context(nc.psum_tensor("pvv", [128, 1024], F32))
        ptp3 = [ph3a_ps.enter_context(nc.psum_tensor(f"ptp3{i}", [128, 1024], BF16))
                for i in range(2)]

        def ph3a_gpsimd(eng):
            eng.wait_ge(g["sTS"].h, 16)
            for q4 in range(4):
                src = oproj_h[q4 * 512:(q4 + 1) * 512, :].rearrange(
                    "(dvc p) h -> p dvc h", p=128)
                dst = opj_bf.rearrange("p (dvc h) -> p dvc h", dvc=NH)[
                    :, q4 * 4:(q4 + 1) * 4, :]
                eng.dma_start(out=dst, in_=src).then_inc(g["sOW3"].h, 16)
            g["sOW3"].n += 64
            eng.wait_ge(g["sTA"].h, 64)
            src = A_h.rearrange("(hc p) ar -> p hc ar", p=128)
            eng.dma_start(out=A_bf.rearrange("p (hc ar) -> p hc ar", hc=NH),
                          in_=src).then_inc(g["sAW"].h, 16)
            g["sAW"].n += 16
            for i in range(NT):
                eng.wait_ge(g["sVX"].h, i + 1)
                eng.dma_start(out=x1sp_h[i * 128:(i + 1) * 128, :],
                              in_=x1_sb).then_inc(g["sX1W"].h, 16)
                g["sX1W"].n += 16
                eng.wait_ge(g["sVYE"].h, 4 * (i + 1))
                dst = y3sp_h.rearrange("(hc p) tc -> p hc tc", p=128)[
                    :, :, i * 128:(i + 1) * 128]
                eng.dma_start(out=dst,
                              in_=y3t_sb.rearrange("p (hc c) -> p hc c", hc=NH)
                              ).then_inc(g["sY3W"].h, 16)
                g["sY3W"].n += 16
                eng.wait_ge(g["sVGE"].h, 2 * (i + 1))
                dst = gsp_h.rearrange("(arc p) tc -> p arc tc", p=128)[
                    :, :, i * 128:(i + 1) * 128]
                eng.dma_start(out=dst,
                              in_=gt_sb.rearrange("p (arc c) -> p arc c", arc=8)
                              ).then_inc(g["sGW"].h, 16)
                g["sGW"].n += 16

        def ph3a_sync(eng):
            eng.wait_ge(g["sTA"].h, 64)
            eng.wait_ge(g["sAO"].h, 64 * 16)
            eng.wait_ge(g["sWC"].h, 32)
            eng.wait_ge(g["sTT"].h, 40)
            eng.wait_ge(g["sSP"].h, 16)
            for c in range(8):
                eng.dma_start_transpose(
                    out=Bfl[:, c * D:(c + 1) * D],
                    in_=bbf_h[c * D:(c + 1) * D, :]).then_inc(g["sBF"].h, 16)
            g["sBF"].n += 8 * 16
            for i in range(NT):
                gi = i // 4
                if i % 4 == 0:
                    if gi >= 1:
                        eng.wait_ge(g["sTO"].h, 16 * gi)
                    src = aosp_h[:, gi * 512:(gi + 1) * 512].rearrange(
                        "(dvc p) n -> p dvc n", p=128)
                    eng.dma_start(out=ao_rd.rearrange("p (dvc n) -> p dvc n",
                                                      dvc=NH),
                                  in_=src).then_inc(g["sAL"].h, 16)
                    g["sAL"].n += 16
                if i >= 2:
                    eng.wait_ge(g["sVX"].h, i - 1)
                eng.dma_start(out=xi3[i % 2],
                              in_=x_h[i * 128:(i + 1) * 128, :]).then_inc(
                    g["sXL"].h, 16)
                g["sXL"].n += 16

        def ph3a_tensor(eng):
            eng.wait_ge(g["sEA"].h, 64)
            eng.wait_ge(g["sOW3"].h, 64)
            for i in range(NT):
                gi, j = i // 4, i % 4
                eng.wait_ge(g["sAL"].h, 16 * (gi + 1))
                if i >= 1:
                    eng.wait_ge(g["sVX"].h, i)
                for nb in range(4):
                    for dvc in range(NH):
                        ins = eng.matmul(
                            po[:, nb * 512:(nb + 1) * 512],
                            ao_rd[:, dvc * 512 + j * 128:dvc * 512 + (j + 1) * 128],
                            opj_bf[:, dvc * D + nb * 512:dvc * D + (nb + 1) * 512],
                            start=(dvc == 0), stop=(dvc == NH - 1))
                    g["sTO"].inc(ins)
                eng.wait_ge(g["sVY"].h, i + 1)
                for bt in range(4):
                    b = bt % 2
                    if bt < 2:
                        if i >= 1:
                            eng.wait_ge(g["sVGE"].h, 2 * (i - 1) + bt + 1)
                    else:
                        eng.wait_ge(g["sVYE"].h, 4 * i + bt - 1)
                    for k in range(4):
                        c = bt * 4 + k
                        ins = eng.transpose(ptp3[b][:, k * 128:(k + 1) * 128],
                                            y3_sb[:, c * 128:(c + 1) * 128], id_bf)
                    g["sTY"].inc(ins)
                if i == 0:
                    eng.wait_ge(g["sAW"].h, 16)
                eng.wait_ge(g["sVYE"].h, 4 * (i + 1))
                if i >= 1:
                    eng.wait_ge(g["sSQ3"].h, 2 * i)
                for hc in range(NH):
                    for n2 in range(2):
                        ins = eng.matmul(
                            pvv[:, n2 * 512:(n2 + 1) * 512],
                            y3t_sb[:, hc * 128:(hc + 1) * 128],
                            A_bf[:, hc * 1024 + n2 * 512:hc * 1024 + (n2 + 1) * 512],
                            start=(hc == 0), stop=(hc == NH - 1))
                g["sTVc"].inc(ins)
                eng.wait_ge(g["sVG"].h, i + 1)
                for bt in range(2):
                    b = bt % 2
                    eng.wait_ge(g["sVYE"].h, 4 * i + bt + 3)
                    for k in range(4):
                        c = bt * 4 + k
                        ins = eng.transpose(ptp3[b][:, k * 128:(k + 1) * 128],
                                            G_sb[:, c * 128:(c + 1) * 128], id_bf)
                    g["sTG"].inc(ins)

        def ph3a_vector(eng):
            eng.wait_ge(g["sSP"].h, 16)
            ins = eng.reciprocal(out=inv_l, in_=l_sums)
            g["sV3"].inc(ins)
            eng.wait_ge(g["sV3"].h, g["sV3"].n)
            for i in range(NT):
                eng.wait_ge(g["sXL"].h, 16 * (i + 1))
                eng.wait_ge(g["sTO"].h, 4 * (i + 1))
                if i >= 1:
                    eng.wait_ge(g["sX1W"].h, 16 * i)
                eng.tensor_scalar(out=xi3[i % 2], in0=xi3[i % 2],
                                  scalar1=is1p1[:, i:i + 1],
                                  scalar2=nm1s[:, i:i + 1],
                                  op0=ALU.mult, op1=ALU.add)
                ins = eng.scalar_tensor_tensor(out=x1_sb, in0=po[:],
                                               scalar=inv_l[:, i:i + 1],
                                               in1=xi3[i % 2],
                                               op0=ALU.mult, op1=ALU.add)
                g["sVX"].inc(ins)
                for k in range(4):
                    ins = eng.bn_stats(out=st4b[:, 6 * k:6 * k + 6],
                                       in_=x1_sb[:, 512 * k:512 * (k + 1)])
                g["sV3"].inc(ins)
                eng.wait_ge(g["sV3"].h, g["sV3"].n)
                ins = eng.bn_aggr(out=mv2b, in_=st4b)
                g["sV3"].inc(ins)
                eng.wait_ge(g["sV3"].h, g["sV3"].n)
                ins = eng.tensor_scalar(out=sc3[:, 0:1], in0=mv2b[:, 1:2],
                                        scalar1=EPS, scalar2=None, op0=ALU.add)
                g["sRc3"].inc(ins)
                eng.wait_ge(g["sSq3"].h, i + 1)
                ins = eng.reciprocal(out=sc3[:, 2:3], in_=sc3[:, 1:2])
                g["sV3"].inc(ins)
                eng.wait_ge(g["sV3"].h, g["sV3"].n)
                ins = eng.tensor_scalar(out=sc3[:, 3:4], in0=mv2b[:, 0:1],
                                        scalar1=sc3[:, 2:3], scalar2=-1.0,
                                        op0=ALU.mult, op1=ALU.mult)
                g["sV3"].inc(ins)
                eng.wait_ge(g["sV3"].h, g["sV3"].n)
                if i >= 1:
                    eng.wait_ge(g["sY3W"].h, 16 * i)
                ins = eng.tensor_scalar(out=y3_sb, in0=x1_sb,
                                        scalar1=sc3[:, 2:3], scalar2=sc3[:, 3:4],
                                        op0=ALU.mult, op1=ALU.add)
                g["sVY"].inc(ins)
                for bt in range(4):
                    nb3 = i * 4 + bt
                    eng.wait_ge(g["sTY"].h, nb3 + 1)
                    dst = y3t_sb.rearrange("p (hc c) -> p hc c", hc=NH)[
                        :, 4 * bt:4 * bt + 4, :]
                    src = ptp3[bt % 2][:, 0:512].rearrange(
                        "p (hc c) -> p hc c", hc=4)
                    ins = eng.tensor_copy(out=dst, in_=src)
                    g["sVYE"].inc(ins)
                eng.wait_ge(g["sSQ3"].h, 2 * (i + 1))
                bview = sq_sb[:].rearrange("p (a r) -> p a r", a=64)
                ins = eng.reduce_sum(out=arrows, in_=bview, axis=AX.X)
                g["sVK"].inc(ins)
                eng.wait_ge(g["sVK"].h, g["sVK"].n)
                ins = eng.max(out=m8, in_=arrows)
                g["sVK"].inc(ins)
                eng.wait_ge(g["sVK"].h, g["sVK"].n)
                ins = eng.memset(m8[:, 4:8], -1.0)
                g["sVK"].inc(ins)
                eng.wait_ge(g["sVK"].h, g["sVK"].n)
                ins = eng.match_replace(out=scr64, in_to_replace=m8,
                                        in_values=arrows, imm_value=-1.0)
                g["sVK"].inc(ins)
                eng.wait_ge(g["sVK"].h, g["sVK"].n)
                ins = eng.tensor_tensor(out=mask64, in0=arrows, in1=scr64,
                                        op=ALU.not_equal)
                g["sVK"].inc(ins)
                eng.wait_ge(g["sVK"].h, g["sVK"].n)
                mb = mask64.unsqueeze(2).to_broadcast([128, 64, 16])
                ins = eng.tensor_tensor(
                    out=sq_sb[:].rearrange("p (a r) -> p a r", a=64),
                    in0=vec_sb[:].rearrange("p (a r) -> p a r", a=64),
                    in1=mb, op=ALU.mult)
                g["sVK"].inc(ins)
                eng.wait_ge(g["sVK"].h, g["sVK"].n)
                swap = sq_sb[:].rearrange("p (a r) -> p a r", a=64).transpose(
                    [0, 2, 1])
                ins = eng.reduce_sum(out=v4, in_=swap, axis=AX.X)
                g["sVK"].inc(ins)
                eng.wait_ge(g["sVK"].h, g["sVK"].n)
                ins = eng.tensor_scalar(out=v4s, in0=v4, scalar1=0.0625,
                                        scalar2=None, op0=ALU.mult)
                g["sVK"].inc(ins)
                eng.wait_ge(g["sVK"].h, g["sVK"].n)
                vb = v4s.unsqueeze(1).to_broadcast([128, 64, 16])
                ins = eng.tensor_copy(
                    out=sq_sb[:].rearrange("p (a r) -> p a r", a=64), in_=vb)
                g["sVK"].inc(ins)
                eng.wait_ge(g["sVK"].h, g["sVK"].n)
                ins = eng.tensor_tensor(
                    out=G_sb[:].rearrange("p (a r) -> p a r", a=64),
                    in0=sq_sb[:].rearrange("p (a r) -> p a r", a=64),
                    in1=mb, op=ALU.mult)
                g["sVG"].inc(ins)
                if "dvec" in dbg_aps and i == 0:
                    dsnap = V(WG, 2192, F32)
                    eng.tensor_copy(out=dsnap[:, 0:1024], in_=vec_sb)
                    eng.tensor_copy(out=dsnap[:, 1024:2048], in_=sq_sb)
                    eng.tensor_copy(out=dsnap[:, 2048:2112], in_=arrows)
                    eng.tensor_copy(out=dsnap[:, 2112:2176], in_=mask64)
                    ins = eng.tensor_copy(out=dsnap[:, 2176:2192], in_=v4)
                    g["sDBG"].inc(ins)
                for bt in range(2):
                    nb3 = i * 2 + bt
                    eng.wait_ge(g["sTG"].h, nb3 + 1)
                    if i >= 1 and bt == 0:
                        eng.wait_ge(g["sGW"].h, 16 * i)
                    dst = gt_sb.rearrange("p (arc c) -> p arc c", arc=8)[
                        :, 4 * bt:4 * bt + 4, :]
                    src = ptp3[bt % 2][:, 0:512].rearrange(
                        "p (arc c) -> p arc c", arc=4)
                    ins = eng.tensor_copy(out=dst, in_=src)
                    g["sVGE"].inc(ins)

        def ph3a_scalar(eng):
            for i in range(NT):
                eng.wait_ge(g["sRc3"].h, i + 1)
                ins = eng.activation(out=sc3[:, 1:2], in_=sc3[:, 0:1], func=ACT.Sqrt)
                g["sSq3"].inc(ins)
                eng.wait_ge(g["sTVc"].h, i + 1)
                ins = eng.square(out=sq_sb, in_=pvv[:])
                g["sSQ3"].inc(ins)
                ins = eng.copy(out=vec_sb, in_=pvv[:])
                g["sSQ3"].inc(ins)

        block.gpsimd(ph3a_gpsimd)
        block.sync(ph3a_sync)
        block.tensor(ph3a_tensor)
        block.vector(ph3a_vector)
        block.scalar(ph3a_scalar)
        ph3a_ps.close()

        if "dvec" in dbg_aps:
            def dbg_dvec2(eng):
                eng.wait_ge(g["sDBG"].h, 1)
                dsnap = V(WG, 2192, F32)
                eng.dma_start(out=dbg_aps["dvec"], in_=dsnap).then_inc(
                    g["sCON"].h, 16)
                g["sCON"].n += 16
                eng.wait_ge(g["sCON"].h, g["sCON"].n)
            block.sync(dbg_dvec2)
        if dbg_aps and ("x1" in dbg_aps or "y3" in dbg_aps or "gsp" in dbg_aps):
            def dbg_ph3a(eng):
                eng.wait_ge(g["sX1W"].h, NT * 16)
                eng.wait_ge(g["sY3W"].h, NT * 16)
                eng.wait_ge(g["sGW"].h, NT * 16)
                for nm, src in (("x1", x1sp_h), ("y3", y3sp_h), ("gsp", gsp_h)):
                    if nm in dbg_aps:
                        eng.dma_start(out=dbg_aps[nm], in_=src).then_inc(
                            g["sCON"].h, 16)
                        g["sCON"].n += 16
                eng.wait_ge(g["sCON"].h, g["sCON"].n)
            block.sync(dbg_ph3a)

        if NPHASE == 4:
            return nc, dbg_aps

        y3r = [V(A2 + i * 4096, NH * 128, BF16) for i in range(2)]
        gtr = [V(A2 + 8192 + i * 2048, 8 * 128, BF16) for i in range(2)]
        x1r = [V(A2 + 12288 + i * 8192, D, F32) for i in range(2)]
        out_sb = [V(A2 + 28672 + i * 8192, D, F32) for i in range(2)]
        brep3 = V(A2 + 45056, D, F32)

        ph3b_ps = ExitStack()
        pout = [ph3b_ps.enter_context(nc.psum_tensor(f"pout{i}", [128, 2048], F32))
                for i in range(2)]

        def ph3b_sync(eng):
            eng.wait_ge(g["sWC"].h, 32)
            eng.wait_ge(g["sTO"].h, 64)
            for hc in range(NH):
                eng.dma_start_transpose(out=WT[:, hc * D:(hc + 1) * D],
                                        in_=wbf_h[:, hc * 128:(hc + 1) * 128]
                                        ).then_inc(g["sWT"].h, 16)
            g["sWT"].n += NH * 16
            eng.wait_ge(g["sVGE"].h, 32)
            eng.wait_ge(g["sVX"].h, 16)
            eng.wait_ge(g["sSQ3"].h, 32)
            eng.dma_start(out=brep3, in_=bias_h.partition_broadcast(128)).then_inc(
                g["sL3"].h, 16)
            g["sL3"].n += 16
            for i in range(NT):
                if i >= 2:
                    eng.wait_ge(g["sVO"].h, i - 1)
                eng.wait_ge(g["sY3W"].h, 16 * (i + 1))
                src = y3sp_h.rearrange("(hc p) tc -> p hc tc", p=128)[
                    :, :, i * 128:(i + 1) * 128]
                eng.dma_start(out=y3r[i % 2].rearrange("p (hc c) -> p hc c", hc=NH),
                              in_=src).then_inc(g["sL3"].h, 16)
                eng.wait_ge(g["sGW"].h, 16 * (i + 1))
                src = gsp_h.rearrange("(arc p) tc -> p arc tc", p=128)[
                    :, :, i * 128:(i + 1) * 128]
                eng.dma_start(out=gtr[i % 2].rearrange("p (arc c) -> p arc c",
                                                       arc=8),
                              in_=src).then_inc(g["sL3"].h, 16)
                eng.wait_ge(g["sX1W"].h, 16 * (i + 1))
                eng.dma_start(out=x1r[i % 2],
                              in_=x1sp_h[i * 128:(i + 1) * 128, :]).then_inc(
                    g["sL3"].h, 16)
                g["sL3"].n += 48

        def ph3b_tensor(eng):
            eng.wait_ge(g["sBF"].h, 8 * 16)
            eng.wait_ge(g["sVGE"].h, 32)
            eng.wait_ge(g["sVX"].h, 16)
            eng.wait_ge(g["sSQ3"].h, 32)
            for i in range(NT):
                b = i % 2
                eng.wait_ge(g["sL3"].h, 16 + 48 * (i + 1))
                if i >= 2:
                    eng.wait_ge(g["sVO"].h, i - 1)
                for nb in range(4):
                    for arc in range(8):
                        ins = eng.matmul(
                            pout[b][:, nb * 512:(nb + 1) * 512],
                            gtr[b][:, arc * 128:(arc + 1) * 128],
                            Bfl[:, arc * D + nb * 512:arc * D + (nb + 1) * 512],
                            start=(arc == 0), stop=False)
                    for hc in range(NH):
                        if i == 0 and nb == 0:
                            eng.wait_ge(g["sWT"].h, 16 * (hc + 1))
                        ins = eng.matmul(
                            pout[b][:, nb * 512:(nb + 1) * 512],
                            y3r[b][:, hc * 128:(hc + 1) * 128],
                            WT[:, hc * D + nb * 512:hc * D + (nb + 1) * 512],
                            start=False, stop=(hc == NH - 1))
                    g["sT3"].inc(ins)

        def ph3b_vector(eng):
            for i in range(NT):
                b = i % 2
                eng.wait_ge(g["sT3"].h, 4 * (i + 1))
                if i >= 2:
                    eng.wait_ge(g["sOW"].h, 16 * (i - 1))
                eng.tensor_add(out_sb[b], pout[b][:], x1r[b])
                ins = eng.tensor_add(out_sb[b], out_sb[b], brep3)
                g["sVO"].inc(ins)

        def ph3b_scalar(eng):
            for i in range(NT):
                eng.wait_ge(g["sVO"].h, i + 1)
                eng.dma_start(out=out_h[i * 128:(i + 1) * 128, :],
                              in_=out_sb[i % 2]).then_inc(g["sOW"].h, 16)
                g["sOW"].n += 16
            eng.wait_ge(g["sOW"].h, NT * 16)

        block.sync(ph3b_sync)
        block.tensor(ph3b_tensor)
        block.vector(ph3b_vector)
        block.scalar(ph3b_scalar)
        ph3b_ps.close()

        return nc, dbg_aps


def _prep_inputs(L0_attn_x, L0_attn_qkv, L0_attn_o_proj, L0_attn_gamma,
                 L0_attn_beta, L0_ffn_A, L0_ffn_B, L0_ffn_W, L0_ffn_bias):
    common = {
        "qkv": np.ascontiguousarray(L0_attn_qkv, np.float32),
        "o_proj": np.ascontiguousarray(L0_attn_o_proj, np.float32),
        "gamma": np.ascontiguousarray(L0_attn_gamma, np.float32).reshape(1, D),
        "beta": np.ascontiguousarray(L0_attn_beta, np.float32).reshape(1, D),
        "A": np.ascontiguousarray(
            np.asarray(L0_ffn_A, np.float32).reshape(1024, D).T),
        "B": np.ascontiguousarray(
            np.asarray(L0_ffn_B, np.float32).reshape(8, 8, 2048, 16)
            .transpose(0, 2, 1, 3).reshape(8 * 2048, 128)),
        "W": np.ascontiguousarray(L0_ffn_W, np.float32),
        "bias": np.ascontiguousarray(L0_ffn_bias, np.float32).reshape(1, D),
    }
    in_maps = []
    for c in range(8):
        m = dict(common)
        m["x"] = np.ascontiguousarray(L0_attn_x[c], np.float32)
        in_maps.append(m)
    return in_maps


def kernel(L0_attn_x, L0_attn_qkv, L0_attn_o_proj, L0_attn_gamma, L0_attn_beta,
           L0_ffn_A, L0_ffn_B, L0_ffn_W, L0_ffn_bias):
    in_maps = _prep_inputs(L0_attn_x, L0_attn_qkv, L0_attn_o_proj, L0_attn_gamma,
                           L0_attn_beta, L0_ffn_A, L0_ffn_B, L0_ffn_W, L0_ffn_bias)
    nc, _ = build()
    res = run_bass_kernel_spmd(nc, in_maps, core_ids=list(range(8)))
    out = np.stack([np.asarray(r["out"]) for r in res.results], axis=0)
    return out.astype(np.float32)


# version 04
# speedup vs baseline: 1.2224x; 1.2224x over previous
import numpy as np
from contextlib import ExitStack

import concourse.bass as bass
import concourse.mybir as mybir
from concourse.bass_utils import run_bass_kernel_spmd

F32 = mybir.dt.float32
BF16 = mybir.dt.bfloat16
U8 = mybir.dt.uint8
AX = mybir.AxisListType
ALU = mybir.AluOpType
ACT = mybir.ActivationFunctionType

D = 2048
S = 2048
NT = 16
NH = 16
NG = 4
EPS = 1e-5
QSCALE = 1.0 / float(np.sqrt(2048.0))
NEG = -1.0e30

DBG = ()
NPHASE = 5

A1 = 0
A2 = 65536
A3 = 131072
WG = 196608
CN = 210944
TOTAL = 212832


def _sz(dt):
    return mybir.dt.size(dt)


class Ctr:
    def __init__(self, stack, nc, name):
        self.h = stack.enter_context(nc.semaphore(name))
        self.n = 0

    def inc(self, ins, k=1):
        ins.then_inc(self.h, k)
        self.n += k
        return self.n


def build():
    nc = bass.Bass("TRN2", target_bir_lowering=False, debug=False)

    x_h = nc.dram_tensor("x", [S, D], F32, kind="ExternalInput").ap()
    qkv_h = nc.dram_tensor("qkv", [D, 3 * D], F32, kind="ExternalInput").ap()
    oproj_h = nc.dram_tensor("o_proj", [D, D], F32, kind="ExternalInput").ap()
    gamma_h = nc.dram_tensor("gamma", [1, D], F32, kind="ExternalInput").ap()
    beta_h = nc.dram_tensor("beta", [1, D], F32, kind="ExternalInput").ap()
    A_h = nc.dram_tensor("A", [D, 1024], F32, kind="ExternalInput").ap()
    B_h = nc.dram_tensor("B", [8 * 2048, 128], F32, kind="ExternalInput").ap()
    W_h = nc.dram_tensor("W", [D, D], F32, kind="ExternalInput").ap()
    bias_h = nc.dram_tensor("bias", [1, D], F32, kind="ExternalInput").ap()

    out_h = nc.dram_tensor("out", [S, D], F32, kind="ExternalOutput").ap()

    qsp_h = nc.dram_tensor("q_sp", [NH * 128, NG * 512], BF16, kind="Internal").ap()
    aosp_h = nc.dram_tensor("ao_sp", [NH * 128, NG * 512], BF16, kind="Internal").ap()
    x1sp_h = nc.dram_tensor("x1_sp", [S, D], F32, kind="Internal").ap()
    y3sp_h = nc.dram_tensor("y3_sp", [NH * 128, NT * 128], BF16, kind="Internal").ap()
    gsp_h = nc.dram_tensor("g_sp", [8 * 128, NT * 128], BF16, kind="Internal").ap()
    wbf_h = nc.dram_tensor("w_bf", [D, D], BF16, kind="Internal").ap()
    bbf_h = nc.dram_tensor("b_bf", [8 * 2048, 128], BF16, kind="Internal").ap()

    dbg_aps = {}
    def _dbg(name, shape, dtype):
        if name in DBG:
            dbg_aps[name] = nc.dram_tensor("dbg_" + name, shape, dtype,
                                           kind="ExternalOutput").ap()
    _dbg("xn", [128, NH * D], BF16)
    _dbg("stats", [128, 2 * NT], F32)
    _dbg("kf", [128, NH * D], BF16)
    _dbg("vt", [128, NT * D], BF16)
    _dbg("lsum", [128, NT], F32)
    _dbg("qsp", [NH * 128, NG * 512], BF16)
    _dbg("ao", [NH * 128, NG * 512], BF16)
    _dbg("x1", [2048, 2048], F32)
    _dbg("y3", [NH * 128, NT * 128], BF16)
    _dbg("gsp", [8 * 128, NT * 128], BF16)
    _dbg("dvec", [128, 2192], F32)

    with ExitStack() as stack:
        arena = stack.enter_context(nc.sbuf_tensor("arena", [128, TOTAL], U8))

        def V(off, cols, dtype, parts=128):
            nb = cols * _sz(dtype)
            assert off % 32 == 0 and off + nb <= TOTAL, (off, nb)
            return arena[0:parts, off:off + nb].bitcast(dtype)

        id_bf = V(CN, 128, BF16)
        mask_f = V(CN + 256, 128, F32)
        is1p1 = V(CN + 768, NT, F32)
        nm1s = V(CN + 832, NT, F32)
        l_sums = V(CN + 896, NT, F32)
        inv_l = V(CN + 960, NT, F32)
        id_f32 = V(CN + 1024, 128, F32)

        xin = [V(A2 + i * 8192, D, F32) for i in range(2)]
        y1f = V(A2 + 16384, D, F32)
        y2f = V(A2 + 24576, D, F32)
        ttmp = V(A2 + 32768, D, F32)
        xnt_bf = V(A2 + 16384, D, BF16)
        grep = V(A2 + 40960, D, F32)
        brep = V(A2 + 49152, D, F32)
        st4 = V(A2 + 57344, 24, F32)
        mv2 = V(A2 + 57440, 2, F32)
        sc1 = V(A2 + 57472, 8, F32)

        xn = V(A3, NH * D, BF16)
        kf = V(A1, NH * D, BF16)
        vt = V(A2, NT * D, BF16)

        wkb = [V(WG + i * 4096, NH * 128, BF16) for i in range(2)]
        qstage = V(WG + 8192, D, BF16)
        wvb = [V(A1 + i * 16384, NH * 512, BF16) for i in range(2)]

        sems = {}
        sems_extra = ("sDBG",)
        for name in ("sSET", "sCON", "sX", "sSq", "sRc", "sV1", "sXN", "sT1",
                     "sE1", "sWV", "sTV", "sEV", "sWK", "sTK", "sEK",
                     "sWQ", "sTQ", "sEQ", "sQS",
                     "sQL", "sTS", "sVM", "sSP", "sTT", "sVT", "sPZ",
                     "sTA", "sEA", "sAO",
                     "sWC", "sOW3", "sAW", "sAL", "sXL", "sTO", "sVX", "sX1W",
                     "sY3W", "sGW", "sRc3", "sSq3", "sSQ3", "sVY", "sTY",
                     "sVYE", "sTVc", "sVG", "sTG", "sVGE", "sVK", "sV3",
                     "sWT", "sBF", "sL3", "sT3", "sVO", "sOW", "sTR") + sems_extra:
            sems[name] = Ctr(stack, nc, name)
        g = sems

        block = stack.enter_context(nc.Block())

        def setup_sync(eng):
            eng.dma_start(out=grep, in_=gamma_h.partition_broadcast(128)).then_inc(
                g["sCON"].h, 16)
            eng.dma_start(out=brep, in_=beta_h.partition_broadcast(128)).then_inc(
                g["sCON"].h, 16)
            g["sCON"].n += 32

        def setup_gpsimd(eng):
            eng.memset(mask_f, 0.0)
            ins = eng.affine_select(out=mask_f, in_=mask_f, pattern=[[-1, 128]],
                                    compare_op=ALU.is_ge, fill=NEG, base=0,
                                    channel_multiplier=1)
            g["sSET"].inc(ins)
            eng.memset(id_f32, 0.0)
            ins = eng.affine_select(out=id_f32, in_=id_f32, pattern=[[-1, 128]],
                                    compare_op=ALU.not_equal, fill=1.0, base=0,
                                    channel_multiplier=1)
            g["sSET"].inc(ins)

        block.sync(setup_sync)
        block.gpsimd(setup_gpsimd)

        def setup_vector(eng):
            eng.wait_ge(g["sSET"].h, 2)
            ins = eng.tensor_copy(out=id_bf, in_=id_f32)
            g["sSET"].inc(ins)

        block.vector(setup_vector)

        ph1_ps = ExitStack()
        tp_ps = [ph1_ps.enter_context(nc.psum_tensor(f"tp{i}", [128, 1024], BF16))
                 for i in range(2)]

        def ph1_sync(eng):
            for i in range(NT):
                if i >= 2:
                    eng.wait_ge(g["sXN"].h, i - 1)
                eng.dma_start(out=xin[i % 2],
                              in_=x_h[i * 128:(i + 1) * 128, :]).then_inc(
                    g["sX"].h, 16)
            g["sX"].n += NT * 16

        def ph1_vector(eng):
            for i in range(NT):
                xt = xin[i % 2]
                eng.wait_ge(g["sX"].h, 16 * (i + 1))
                if i >= 1:
                    eng.wait_ge(g["sT1"].h, 4 * i)
                for k in range(4):
                    ins = eng.bn_stats(out=st4[:, 6 * k:6 * k + 6],
                                       in_=xt[:, 512 * k:512 * (k + 1)])
                g["sV1"].inc(ins)
                eng.wait_ge(g["sV1"].h, g["sV1"].n)
                ins = eng.bn_aggr(out=mv2, in_=st4)
                g["sV1"].inc(ins)
                eng.wait_ge(g["sV1"].h, g["sV1"].n)
                ins = eng.tensor_scalar(out=sc1[:, 0:1], in0=mv2[:, 1:2],
                                        scalar1=EPS, scalar2=None, op0=ALU.add)
                g["sRc"].inc(ins)
                eng.wait_ge(g["sSq"].h, 2 * i + 1)
                ins = eng.reciprocal(out=sc1[:, 6:7], in_=sc1[:, 4:5])
                g["sV1"].inc(ins)
                eng.wait_ge(g["sV1"].h, g["sV1"].n)
                eng.tensor_scalar(out=is1p1[:, i:i + 1], in0=sc1[:, 6:7],
                                  scalar1=1.0, scalar2=None, op0=ALU.add)
                eng.tensor_scalar(out=nm1s[:, i:i + 1], in0=mv2[:, 0:1],
                                  scalar1=sc1[:, 6:7], scalar2=-1.0,
                                  op0=ALU.mult, op1=ALU.mult)
                ins = eng.tensor_scalar(out=sc1[:, 1:2], in0=mv2[:, 1:2],
                                        scalar1=sc1[:, 6:7], scalar2=sc1[:, 6:7],
                                        op0=ALU.mult, op1=ALU.mult)
                g["sV1"].inc(ins)
                eng.wait_ge(g["sV1"].h, g["sV1"].n)
                ins = eng.tensor_scalar(out=sc1[:, 1:2], in0=sc1[:, 1:2],
                                        scalar1=EPS, scalar2=None, op0=ALU.add)
                g["sRc"].inc(ins)
                eng.wait_ge(g["sSq"].h, 2 * i + 2)
                ins = eng.reciprocal(out=sc1[:, 7:8], in_=sc1[:, 5:6])
                g["sV1"].inc(ins)
                eng.wait_ge(g["sV1"].h, g["sV1"].n)
                eng.tensor_scalar(out=sc1[:, 3:4], in0=sc1[:, 6:7],
                                  scalar1=sc1[:, 7:8], scalar2=None, op0=ALU.mult)
                ins = eng.tensor_scalar(out=sc1[:, 2:3], in0=mv2[:, 0:1],
                                        scalar1=-1.0, scalar2=None, op0=ALU.mult)
                g["sV1"].inc(ins)
                eng.wait_ge(g["sV1"].h, g["sV1"].n)
                if i == 0:
                    eng.wait_ge(g["sCON"].h, 32)
                eng.scalar_tensor_tensor(out=ttmp, in0=xt, scalar=sc1[:, 2:3],
                                         in1=grep, op0=ALU.add, op1=ALU.mult)
                ins = eng.scalar_tensor_tensor(out=xnt_bf, in0=ttmp,
                                               scalar=sc1[:, 3:4], in1=brep,
                                               op0=ALU.mult, op1=ALU.add)
                g["sXN"].inc(ins)

        def ph1_tensor(eng):
            eng.wait_ge(g["sSET"].h, 3)
            for i in range(NT):
                eng.wait_ge(g["sXN"].h, i + 1)
                for gq in range(4):
                    b = (i * 4 + gq) % 2
                    if i * 4 + gq >= 2:
                        eng.wait_ge(g["sE1"].h, i * 4 + gq - 1)
                    for k in range(4):
                        c = gq * 4 + k
                        ins = eng.transpose(tp_ps[b][:, k * 128:(k + 1) * 128],
                                            xnt_bf[:, c * 128:(c + 1) * 128], id_bf)
                    g["sT1"].inc(ins)

        def ph1_scalar(eng):
            for i in range(NT):
                for k in range(2):
                    eng.wait_ge(g["sRc"].h, 2 * i + k + 1)
                    ins = eng.activation(out=sc1[:, 4 + k:5 + k],
                                         in_=sc1[:, k:k + 1], func=ACT.Sqrt)
                    g["sSq"].inc(ins)
                for gq in range(4):
                    b = (i * 4 + gq) % 2
                    eng.wait_ge(g["sT1"].h, i * 4 + gq + 1)
                    dst = xn.rearrange("p (hc t) -> p hc t", hc=NH)[
                        :, 4 * gq:4 * gq + 4, i * 128:(i + 1) * 128]
                    src = tp_ps[b][:, 0:512].rearrange("p (hc t) -> p hc t", hc=4)
                    ins = eng.copy(out=dst, in_=src)
                    g["sE1"].inc(ins)

        block.sync(ph1_sync)
        block.vector(ph1_vector)
        block.tensor(ph1_tensor)
        block.scalar(ph1_scalar)

        if "xn" in dbg_aps or "stats" in dbg_aps:
            def dbg_ph1(eng):
                eng.wait_ge(g["sE1"].h, NT * 4)
                eng.wait_ge(g["sXN"].h, NT)
                if "xn" in dbg_aps:
                    eng.dma_start(out=dbg_aps["xn"], in_=xn).then_inc(g["sCON"].h, 16)
                    g["sCON"].n += 16
                if "stats" in dbg_aps:
                    st = dbg_aps["stats"]
                    eng.dma_start(out=st[:, 0:NT], in_=is1p1).then_inc(g["sCON"].h, 16)
                    eng.dma_start(out=st[:, NT:2 * NT], in_=nm1s).then_inc(
                        g["sCON"].h, 16)
                    g["sCON"].n += 32
                eng.wait_ge(g["sCON"].h, g["sCON"].n)
            block.sync(dbg_ph1)

        ph1_ps.close()
        if NPHASE == 1:
            return nc, dbg_aps

        genv_ps = ExitStack()
        pv = [genv_ps.enter_context(nc.psum_tensor(f"pv{i}", [128, 512], F32))
              for i in range(2)]

        def genv_gpsimd(eng):
            for dvb in range(4):
                if dvb >= 2:
                    eng.wait_ge(g["sTV"].h, 16 * (dvb - 1))
                src = qkv_h[:, 2 * D + dvb * 512:2 * D + (dvb + 1) * 512]
                src = src.rearrange("(hc p) n -> p hc n", p=128)
                dst = wvb[dvb % 2].rearrange("p (hc n) -> p hc n", hc=NH)
                eng.dma_start(out=dst, in_=src).then_inc(g["sWV"].h, 16)
            g["sWV"].n += 4 * 16

        def genv_tensor(eng):
            eng.wait_ge(g["sE1"].h, NT * 4)
            for dvb in range(4):
                eng.wait_ge(g["sWV"].h, 16 * (dvb + 1))
                for tc in range(NT):
                    n = dvb * NT + tc
                    b = n % 2
                    if n >= 2:
                        eng.wait_ge(g["sEV"].h, n - 1)
                    for hc in range(NH):
                        ins = eng.matmul(
                            pv[b][:],
                            xn[:, hc * D + tc * 128:hc * D + (tc + 1) * 128],
                            wvb[dvb % 2][:, hc * 512:(hc + 1) * 512],
                            start=(hc == 0), stop=(hc == NH - 1))
                    g["sTV"].inc(ins)

        def genv_vector(eng):
            for dvb in range(4):
                for tc in range(NT):
                    n = dvb * NT + tc
                    eng.wait_ge(g["sTV"].h, n + 1)
                    ins = eng.tensor_copy(
                        out=vt[:, tc * D + dvb * 512:tc * D + (dvb + 1) * 512],
                        in_=pv[n % 2][:])
                    g["sEV"].inc(ins)

        def wcast_gpsimd(eng):
            eng.dma_start(out=wbf_h, in_=W_h).then_inc(g["sWC"].h, 16)
            eng.dma_start(out=bbf_h, in_=B_h).then_inc(g["sWC"].h, 16)
            g["sWC"].n += 32

        block.gpsimd(genv_gpsimd)
        block.gpsimd(wcast_gpsimd)
        block.tensor(genv_tensor)
        block.vector(genv_vector)
        genv_ps.close()

        genk_ps = ExitStack()
        pg = [genk_ps.enter_context(nc.psum_tensor(f"pg{i}", [128, 2048], F32))
              for i in range(2)]

        def genkq_gpsimd(eng):
            for phase, base in (("k", 1 * D), ("q", 0)):
                for dkc in range(NH):
                    n = (0 if phase == "k" else NH) + dkc
                    if n >= 2:
                        sem = g["sTK"] if n - 2 < NH else g["sTQ"]
                        eng.wait_ge(sem.h, (n - 2) % NH + 1)
                    src = qkv_h[:, base + dkc * 128:base + (dkc + 1) * 128]
                    src = src.rearrange("(hc p) c -> p hc c", p=128)
                    dst = wkb[n % 2].rearrange("p (hc c) -> p hc c", hc=NH)
                    eng.dma_start(out=dst, in_=src).then_inc(g["sWK"].h, 16)
            g["sWK"].n += 2 * NH * 16

        def genkq_tensor(eng):
            eng.wait_ge(g["sEV"].h, 64)
            for dkc in range(NH):
                eng.wait_ge(g["sWK"].h, 16 * (dkc + 1))
                b = dkc % 2
                if dkc >= 2:
                    eng.wait_ge(g["sEK"].h, dkc - 1)
                for tb in range(4):
                    for hc in range(NH):
                        ins = eng.matmul(
                            pg[b][:, tb * 512:(tb + 1) * 512],
                            wkb[dkc % 2][:, hc * 128:(hc + 1) * 128],
                            xn[:, hc * D + tb * 512:hc * D + (tb + 1) * 512],
                            start=(hc == 0), stop=(hc == NH - 1))
                g["sTK"].inc(ins)
            for dkc in range(NH):
                eng.wait_ge(g["sWK"].h, 16 * (NH + dkc + 1))
                b = dkc % 2
                if dkc >= 2:
                    eng.wait_ge(g["sEQ"].h, dkc - 1)
                else:
                    eng.wait_ge(g["sEK"].h, NH)
                for tb in range(4):
                    for hc in range(NH):
                        ins = eng.matmul(
                            pg[b][:, tb * 512:(tb + 1) * 512],
                            wkb[dkc % 2][:, hc * 128:(hc + 1) * 128],
                            xn[:, hc * D + tb * 512:hc * D + (tb + 1) * 512],
                            start=(hc == 0), stop=(hc == NH - 1))
                g["sTQ"].inc(ins)

        def genkq_vector(eng):
            for dkc in range(NH):
                eng.wait_ge(g["sTK"].h, dkc + 1)
                ins = eng.tensor_copy(out=kf[:, dkc * D:(dkc + 1) * D],
                                      in_=pg[dkc % 2][:])
                g["sEK"].inc(ins)
            for dkc in range(NH):
                eng.wait_ge(g["sTQ"].h, dkc + 1)
                if dkc >= 1:
                    eng.wait_ge(g["sQS"].h, 16 * dkc)
                ins = eng.tensor_scalar(out=qstage, in0=pg[dkc % 2][:],
                                        scalar1=QSCALE, scalar2=None,
                                        op0=ALU.mult)
                g["sEQ"].inc(ins)

        def genkq_scalar(eng):
            for dkc in range(NH):
                eng.wait_ge(g["sEQ"].h, dkc + 1)
                eng.dma_start(out=qsp_h[dkc * 128:(dkc + 1) * 128, :],
                              in_=qstage).then_inc(g["sQS"].h, 16)
            g["sQS"].n += NH * 16

        block.gpsimd(genkq_gpsimd)
        block.tensor(genkq_tensor)
        block.vector(genkq_vector)
        block.scalar(genkq_scalar)
        genk_ps.close()

        if dbg_aps:
            def dbg_gen(eng):
                eng.wait_ge(g["sEK"].h, NH)
                eng.wait_ge(g["sEV"].h, 64)
                eng.wait_ge(g["sQS"].h, NH * 16)
                if "kf" in dbg_aps:
                    eng.dma_start(out=dbg_aps["kf"], in_=kf).then_inc(g["sCON"].h, 16)
                    g["sCON"].n += 16
                if "vt" in dbg_aps:
                    eng.dma_start(out=dbg_aps["vt"], in_=vt).then_inc(g["sCON"].h, 16)
                    g["sCON"].n += 16
                if "qsp" in dbg_aps:
                    eng.dma_start(out=dbg_aps["qsp"], in_=qsp_h).then_inc(
                        g["sCON"].h, 16)
                    g["sCON"].n += 16
                eng.wait_ge(g["sCON"].h, g["sCON"].n)
            block.sync(dbg_gen)

        if NPHASE == 2:
            return nc, dbg_aps

        q_sb = V(A3, NH * 512, BF16)
        pT = [V(A3 + 16384 + i * 16384, NT * 512, BF16) for i in range(2)]
        p_sb = [V(A3 + 49152 + i * 4096, D, BF16) for i in range(2)]
        ao_st = [V(A3 + 57344 + i * 1024, 512, BF16) for i in range(2)]
        negm = V(A3 + 59392, 2, F32)
        dum2 = V(A3 + 59424, 1, F32)

        ph2_ps = ExitStack()
        psc = ph2_ps.enter_context(nc.psum_tensor("psc", [128, 2048], F32))
        ptp2 = [ph2_ps.enter_context(nc.psum_tensor(f"ptp2{i}", [128, 1024], BF16))
                for i in range(2)]
        pav = [ph2_ps.enter_context(nc.psum_tensor(f"pav{i}", [128, 512], F32))
               for i in range(2)]

        def _nbt(s_tile):
            return (s_tile + 1 + 3) // 4

        def ph2_sync(eng):
            eng.wait_ge(g["sQS"].h, NH * 16)
            for gI in range(NG):
                if gI >= 1:
                    eng.wait_ge(g["sTS"].h, 4 * gI)
                src = qsp_h[:, gI * 512:(gI + 1) * 512].rearrange(
                    "(dkc p) n -> p dkc n", p=128)
                dst = q_sb.rearrange("p (dkc n) -> p dkc n", dkc=NH)
                eng.dma_start(out=dst, in_=src).then_inc(g["sQL"].h, 16)
            g["sQL"].n += NG * 16

        def ph2_tensor(eng):
            av_idx = 0
            gbt = 0
            for gI in range(NG):
                eng.wait_ge(g["sQL"].h, 16 * (gI + 1))
                for j in range(4):
                    s_tile = 4 * gI + j
                    T = (s_tile + 1) * 128
                    if gI == 0:
                        sc_base = (j % 2) * 512
                        eng.wait_ge(g["sSP"].h, max(0, s_tile - 1))
                    elif gI == 1:
                        sc_base = (j % 2) * 1024
                        eng.wait_ge(g["sSP"].h, 4 if j < 2 else s_tile - 1)
                    else:
                        sc_base = 0
                        eng.wait_ge(g["sSP"].h, s_tile)
                    nb = (T + 511) // 512
                    for tb in range(nb):
                        N = min(512, T - tb * 512)
                        for dkc in range(NH):
                            ins = eng.matmul(
                                psc[:, sc_base + tb * 512:sc_base + tb * 512 + N],
                                q_sb[:, dkc * 512 + j * 128:dkc * 512 + (j + 1) * 128],
                                kf[:, dkc * D + tb * 512:dkc * D + tb * 512 + N],
                                start=(dkc == 0), stop=(dkc == NH - 1))
                    g["sTS"].inc(ins)
                    if gI >= 1:
                        if j == 0:
                            eng.wait_ge(g["sPZ"].h, gI)
                        ntc = 4 * gI
                        for dvc in range(4 * j, 4 * j + 4):
                            b = av_idx % 2
                            if av_idx >= 2:
                                eng.wait_ge(g["sEA"].h, av_idx - 1)
                            for tc in range(ntc):
                                ins = eng.matmul(
                                    pav[b][:],
                                    vt[:, tc * D + dvc * 128:tc * D + (dvc + 1) * 128],
                                    pT[(gI - 1) % 2][:, tc * 512:(tc + 1) * 512],
                                    start=(tc == 0), stop=(tc == ntc - 1))
                            g["sTA"].inc(ins)
                            av_idx += 1
                    eng.wait_ge(g["sSP"].h, s_tile + 1)
                    ntp = s_tile + 1
                    for bt in range(_nbt(s_tile)):
                        b = gbt % 2
                        if gbt >= 2:
                            eng.wait_ge(g["sVT"].h, gbt - 1)
                        for k in range(min(4, ntp - 4 * bt)):
                            tc = 4 * bt + k
                            ins = eng.transpose(
                                ptp2[b][:, k * 128:(k + 1) * 128],
                                p_sb[j % 2][:, tc * 128:(tc + 1) * 128], id_bf)
                        g["sTT"].inc(ins)
                        gbt += 1
            eng.wait_ge(g["sPZ"].h, NG)
            for dvc in range(16):
                b = av_idx % 2
                if av_idx >= 2:
                    eng.wait_ge(g["sEA"].h, av_idx - 1)
                for tc in range(16):
                    ins = eng.matmul(
                        pav[b][:],
                        vt[:, tc * D + dvc * 128:tc * D + (dvc + 1) * 128],
                        pT[1][:, tc * 512:(tc + 1) * 512],
                        start=(tc == 0), stop=(tc == 15))
                g["sTA"].inc(ins)
                av_idx += 1

        def ph2_vector(eng):
            gbt = 0
            for gI in range(NG):
                if gI >= 2:
                    eng.wait_ge(g["sTA"].h, 16 * (gI - 1))
                for j in range(4):
                    s_tile = 4 * gI + j
                    T = (s_tile + 1) * 128
                    sc_base = ((j % 2) * 512 if gI == 0 else
                               (j % 2) * 1024 if gI == 1 else 0)
                    eng.wait_ge(g["sTS"].h, s_tile + 1)
                    eng.tensor_add(psc[:, sc_base + s_tile * 128:sc_base + T],
                                   psc[:, sc_base + s_tile * 128:sc_base + T],
                                   mask_f)
                    ins = eng.reduce_max(out=negm[:, j % 2:j % 2 + 1],
                                         in_=psc[:, sc_base:sc_base + T],
                                         axis=AX.X, negate=True)
                    g["sVM"].inc(ins)
                    for bt in range(_nbt(s_tile)):
                        eng.wait_ge(g["sTT"].h, gbt + 1)
                        nk = min(4, s_tile + 1 - 4 * bt)
                        dst = pT[gI % 2].rearrange("p (tc c) -> p tc c", tc=NT)[
                            :, 4 * bt:4 * bt + nk, j * 128:(j + 1) * 128]
                        src = ptp2[gbt % 2][:, 0:nk * 128].rearrange(
                            "p (tc c) -> p tc c", tc=nk)
                        ins = eng.tensor_copy(out=dst, in_=src)
                        g["sVT"].inc(ins)
                        gbt += 1
                    hi = 4 * (gI + 1)
                    if s_tile + 1 < hi:
                        zv = pT[gI % 2].rearrange("p (tc c) -> p tc c", tc=NT)[
                            :, s_tile + 1:hi, j * 128:(j + 1) * 128]
                        eng.memset(zv, 0.0)
                ins = eng.tensor_copy(out=dum2, in_=negm[:, 0:1])
                g["sPZ"].inc(ins)

        def ph2_scalar(eng):
            av_idx = 0
            for gI in range(NG):
                for j in range(4):
                    s_tile = 4 * gI + j
                    T = (s_tile + 1) * 128
                    sc_base = ((j % 2) * 512 if gI == 0 else
                               (j % 2) * 1024 if gI == 1 else 0)
                    eng.wait_ge(g["sVM"].h, s_tile + 1)
                    ins = eng.activation(out=p_sb[j % 2][:, 0:T],
                                         in_=psc[:, sc_base:sc_base + T],
                                         func=ACT.Exp, bias=negm[:, j % 2:j % 2 + 1],
                                         scale=1.0,
                                         accum_out=l_sums[:, s_tile:s_tile + 1])
                    g["sSP"].inc(ins)
                    if gI >= 1:
                        for dvc in range(4 * j, 4 * j + 4):
                            eng.wait_ge(g["sTA"].h, av_idx + 1)
                            if av_idx >= 2:
                                eng.wait_ge(g["sAO"].h, 16 * (av_idx - 1))
                            b2 = av_idx % 2
                            ins = eng.copy(out=ao_st[b2], in_=pav[b2][:])
                            g["sEA"].inc(ins)
                            eng.wait_ge(g["sEA"].h, g["sEA"].n)
                            eng.dma_start(
                                out=aosp_h[dvc * 128:(dvc + 1) * 128,
                                           (gI - 1) * 512:gI * 512],
                                in_=ao_st[b2]).then_inc(g["sAO"].h, 16)
                            g["sAO"].n += 16
                            av_idx += 1
            for dvc in range(16):
                eng.wait_ge(g["sTA"].h, av_idx + 1)
                if av_idx >= 2:
                    eng.wait_ge(g["sAO"].h, 16 * (av_idx - 1))
                b2 = av_idx % 2
                ins = eng.copy(out=ao_st[b2], in_=pav[b2][:])
                g["sEA"].inc(ins)
                eng.wait_ge(g["sEA"].h, g["sEA"].n)
                eng.dma_start(out=aosp_h[dvc * 128:(dvc + 1) * 128,
                                         3 * 512:4 * 512],
                              in_=ao_st[b2]).then_inc(g["sAO"].h, 16)
                g["sAO"].n += 16
                av_idx += 1

        block.sync(ph2_sync)
        block.tensor(ph2_tensor)
        block.vector(ph2_vector)
        block.scalar(ph2_scalar)
        ph2_ps.close()

        if dbg_aps and ("ao" in dbg_aps or "lsum" in dbg_aps):
            def dbg_ph2(eng):
                eng.wait_ge(g["sAO"].h, g["sAO"].n)
                if "ao" in dbg_aps:
                    eng.dma_start(out=dbg_aps["ao"], in_=aosp_h).then_inc(
                        g["sCON"].h, 16)
                    g["sCON"].n += 16
                if "lsum" in dbg_aps:
                    eng.dma_start(out=dbg_aps["lsum"], in_=l_sums).then_inc(
                        g["sCON"].h, 16)
                    g["sCON"].n += 16
                eng.wait_ge(g["sCON"].h, g["sCON"].n)
            block.sync(dbg_ph2)

        if NPHASE == 3:
            return nc, dbg_aps

        opj_bf = V(A1, NH * D, BF16)
        WT = V(A1, NH * D, BF16)
        Bfl = V(A3 + 32768, 8 * D, BF16)
        A_bf = V(A3, NH * 1024, BF16)
        xi3 = [V(A2 + i * 8192, D, F32) for i in range(2)]
        ao_rd = V(A2 + 16384, NH * 512, BF16)
        x1_sb = V(A2 + 32768, D, F32)
        y3_sb = V(A2 + 40960, D, BF16)
        y3t_sb = V(A2 + 45056, NH * 128, BF16)
        vec_sb = V(A2 + 49152, 1024, F32)
        sq_sb = V(A2 + 53248, 1024, F32)
        G_sb = V(A2 + 57344, 1024, BF16)
        gt_sb = V(A2 + 59392, 8 * 128, BF16)
        arrows = V(A2 + 61440, 64, F32)
        scr64 = V(A2 + 61696, 64, F32)
        mask64 = V(A2 + 61952, 64, F32)
        m8 = V(A2 + 62208, 8, F32)
        v4 = V(A2 + 62240, 16, F32)
        v4s = V(A2 + 62304, 16, F32)
        st4b = V(A2 + 62368, 24, F32)
        mv2b = V(A2 + 62464, 2, F32)
        sc3 = V(A2 + 62496, 8, F32)

        ph3a_ps = ExitStack()
        po = ph3a_ps.enter_context(nc.psum_tensor("po", [128, 2048], F32))
        pvv = ph3a_ps.enter_context(nc.psum_tensor("pvv", [128, 1024], F32))
        ptp3 = [ph3a_ps.enter_context(nc.psum_tensor(f"ptp3{i}", [128, 1024], BF16))
                for i in range(2)]

        def ph3a_gpsimd(eng):
            eng.wait_ge(g["sTS"].h, 16)
            for q4 in range(4):
                src = oproj_h[q4 * 512:(q4 + 1) * 512, :].rearrange(
                    "(dvc p) h -> p dvc h", p=128)
                dst = opj_bf.rearrange("p (dvc h) -> p dvc h", dvc=NH)[
                    :, q4 * 4:(q4 + 1) * 4, :]
                eng.dma_start(out=dst, in_=src).then_inc(g["sOW3"].h, 16)
            g["sOW3"].n += 64
            eng.wait_ge(g["sTA"].h, 64)
            src = A_h.rearrange("(hc p) ar -> p hc ar", p=128)
            eng.dma_start(out=A_bf.rearrange("p (hc ar) -> p hc ar", hc=NH),
                          in_=src).then_inc(g["sAW"].h, 16)
            g["sAW"].n += 16
            for i in range(NT):
                eng.wait_ge(g["sVX"].h, i + 1)
                eng.dma_start(out=x1sp_h[i * 128:(i + 1) * 128, :],
                              in_=x1_sb).then_inc(g["sX1W"].h, 16)
                g["sX1W"].n += 16
                eng.wait_ge(g["sVYE"].h, 4 * (i + 1))
                dst = y3sp_h.rearrange("(hc p) tc -> p hc tc", p=128)[
                    :, :, i * 128:(i + 1) * 128]
                eng.dma_start(out=dst,
                              in_=y3t_sb.rearrange("p (hc c) -> p hc c", hc=NH)
                              ).then_inc(g["sY3W"].h, 16)
                g["sY3W"].n += 16
                eng.wait_ge(g["sVGE"].h, 2 * (i + 1))
                dst = gsp_h.rearrange("(arc p) tc -> p arc tc", p=128)[
                    :, :, i * 128:(i + 1) * 128]
                eng.dma_start(out=dst,
                              in_=gt_sb.rearrange("p (arc c) -> p arc c", arc=8)
                              ).then_inc(g["sGW"].h, 16)
                g["sGW"].n += 16

        def ph3a_sync(eng):
            eng.wait_ge(g["sTA"].h, 64)
            eng.wait_ge(g["sAO"].h, 64 * 16)
            eng.wait_ge(g["sWC"].h, 32)
            eng.wait_ge(g["sTT"].h, 40)
            eng.wait_ge(g["sSP"].h, 16)
            for c in range(8):
                eng.dma_start_transpose(
                    out=Bfl[:, c * D:(c + 1) * D],
                    in_=bbf_h[c * D:(c + 1) * D, :]).then_inc(g["sBF"].h, 16)
            g["sBF"].n += 8 * 16
            for i in range(NT):
                gi = i // 4
                if i % 4 == 0:
                    if gi >= 1:
                        eng.wait_ge(g["sTO"].h, 16 * gi)
                    src = aosp_h[:, gi * 512:(gi + 1) * 512].rearrange(
                        "(dvc p) n -> p dvc n", p=128)
                    eng.dma_start(out=ao_rd.rearrange("p (dvc n) -> p dvc n",
                                                      dvc=NH),
                                  in_=src).then_inc(g["sAL"].h, 16)
                    g["sAL"].n += 16
                if i >= 2:
                    eng.wait_ge(g["sVX"].h, i - 1)
                eng.dma_start(out=xi3[i % 2],
                              in_=x_h[i * 128:(i + 1) * 128, :]).then_inc(
                    g["sXL"].h, 16)
                g["sXL"].n += 16

        def ph3a_tensor(eng):
            eng.wait_ge(g["sEA"].h, 64)
            eng.wait_ge(g["sOW3"].h, 64)
            for it in range(NT + 1):
                if it < NT:
                    i = it
                    gi, j = i // 4, i % 4
                    eng.wait_ge(g["sAL"].h, 16 * (gi + 1))
                    if i >= 1:
                        eng.wait_ge(g["sVX"].h, i)
                    for nb in range(4):
                        for dvc in range(NH):
                            ins = eng.matmul(
                                po[:, nb * 512:(nb + 1) * 512],
                                ao_rd[:, dvc * 512 + j * 128:dvc * 512 + (j + 1) * 128],
                                opj_bf[:, dvc * D + nb * 512:dvc * D + (nb + 1) * 512],
                                start=(dvc == 0), stop=(dvc == NH - 1))
                        g["sTO"].inc(ins)
                if it >= 1:
                    i = it - 1
                    eng.wait_ge(g["sVY"].h, i + 1)
                    for bt in range(4):
                        b = bt % 2
                        if bt < 2:
                            if i >= 1:
                                eng.wait_ge(g["sVGE"].h, 2 * (i - 1) + bt + 1)
                        else:
                            eng.wait_ge(g["sVYE"].h, 4 * i + bt - 1)
                        for k in range(4):
                            c = bt * 4 + k
                            ins = eng.transpose(ptp3[b][:, k * 128:(k + 1) * 128],
                                                y3_sb[:, c * 128:(c + 1) * 128],
                                                id_bf)
                        g["sTY"].inc(ins)
                    if i == 0:
                        eng.wait_ge(g["sAW"].h, 16)
                    eng.wait_ge(g["sVYE"].h, 4 * (i + 1))
                    if i >= 1:
                        eng.wait_ge(g["sSQ3"].h, 2 * i)
                    for hc in range(NH):
                        for n2 in range(2):
                            ins = eng.matmul(
                                pvv[:, n2 * 512:(n2 + 1) * 512],
                                y3t_sb[:, hc * 128:(hc + 1) * 128],
                                A_bf[:, hc * 1024 + n2 * 512:hc * 1024 + (n2 + 1) * 512],
                                start=(hc == 0), stop=(hc == NH - 1))
                    g["sTVc"].inc(ins)
                    eng.wait_ge(g["sVG"].h, i + 1)
                    for bt in range(2):
                        b = bt % 2
                        eng.wait_ge(g["sVYE"].h, 4 * i + bt + 3)
                        for k in range(4):
                            c = bt * 4 + k
                            ins = eng.transpose(ptp3[b][:, k * 128:(k + 1) * 128],
                                                G_sb[:, c * 128:(c + 1) * 128],
                                                id_bf)
                        g["sTG"].inc(ins)

        def ph3a_vector(eng):
            eng.wait_ge(g["sSP"].h, 16)
            ins = eng.reciprocal(out=inv_l, in_=l_sums)
            g["sV3"].inc(ins)
            eng.wait_ge(g["sV3"].h, g["sV3"].n)
            for i in range(NT):
                eng.wait_ge(g["sTR"].h, i + 1)
                eng.wait_ge(g["sTO"].h, 4 * (i + 1))
                if i >= 1:
                    eng.wait_ge(g["sX1W"].h, 16 * i)
                ins = eng.scalar_tensor_tensor(out=x1_sb, in0=po[:],
                                               scalar=inv_l[:, i:i + 1],
                                               in1=xi3[i % 2],
                                               op0=ALU.mult, op1=ALU.add)
                g["sVX"].inc(ins)
                for k in range(4):
                    ins = eng.bn_stats(out=st4b[:, 6 * k:6 * k + 6],
                                       in_=x1_sb[:, 512 * k:512 * (k + 1)])
                g["sV3"].inc(ins)
                eng.wait_ge(g["sV3"].h, g["sV3"].n)
                ins = eng.bn_aggr(out=mv2b, in_=st4b)
                g["sV3"].inc(ins)
                eng.wait_ge(g["sV3"].h, g["sV3"].n)
                ins = eng.tensor_scalar(out=sc3[:, 0:1], in0=mv2b[:, 1:2],
                                        scalar1=EPS, scalar2=None, op0=ALU.add)
                g["sRc3"].inc(ins)
                eng.wait_ge(g["sSq3"].h, i + 1)
                ins = eng.reciprocal(out=sc3[:, 2:3], in_=sc3[:, 1:2])
                g["sV3"].inc(ins)
                eng.wait_ge(g["sV3"].h, g["sV3"].n)
                ins = eng.tensor_scalar(out=sc3[:, 3:4], in0=mv2b[:, 0:1],
                                        scalar1=sc3[:, 2:3], scalar2=-1.0,
                                        op0=ALU.mult, op1=ALU.mult)
                g["sV3"].inc(ins)
                eng.wait_ge(g["sV3"].h, g["sV3"].n)
                if i >= 1:
                    eng.wait_ge(g["sTY"].h, 4 * i)
                ins = eng.tensor_scalar(out=y3_sb, in0=x1_sb,
                                        scalar1=sc3[:, 2:3], scalar2=sc3[:, 3:4],
                                        op0=ALU.mult, op1=ALU.add)
                g["sVY"].inc(ins)
                if "dvec" in dbg_aps and i == 1:
                    pass
                eng.wait_ge(g["sSQ3"].h, 2 * (i + 1))
                bview = sq_sb[:].rearrange("p (a r) -> p a r", a=64)
                ins = eng.reduce_sum(out=arrows, in_=bview, axis=AX.X)
                g["sVK"].inc(ins)
                eng.wait_ge(g["sVK"].h, g["sVK"].n)
                ins = eng.max(out=m8, in_=arrows)
                g["sVK"].inc(ins)
                eng.wait_ge(g["sVK"].h, g["sVK"].n)
                ins = eng.memset(m8[:, 4:8], -1.0)
                g["sVK"].inc(ins)
                eng.wait_ge(g["sVK"].h, g["sVK"].n)
                ins = eng.match_replace(out=scr64, in_to_replace=m8,
                                        in_values=arrows, imm_value=-1.0)
                g["sVK"].inc(ins)
                eng.wait_ge(g["sVK"].h, g["sVK"].n)
                ins = eng.tensor_tensor(out=mask64, in0=arrows, in1=scr64,
                                        op=ALU.not_equal)
                g["sVK"].inc(ins)
                eng.wait_ge(g["sVK"].h, g["sVK"].n)
                mb = mask64.unsqueeze(2).to_broadcast([128, 64, 16])
                ins = eng.tensor_tensor(
                    out=sq_sb[:].rearrange("p (a r) -> p a r", a=64),
                    in0=vec_sb[:].rearrange("p (a r) -> p a r", a=64),
                    in1=mb, op=ALU.mult)
                g["sVK"].inc(ins)
                eng.wait_ge(g["sVK"].h, g["sVK"].n)
                swap = sq_sb[:].rearrange("p (a r) -> p a r", a=64).transpose(
                    [0, 2, 1])
                ins = eng.reduce_sum(out=v4, in_=swap, axis=AX.X)
                g["sVK"].inc(ins)
                eng.wait_ge(g["sVK"].h, g["sVK"].n)
                ins = eng.tensor_scalar(out=v4s, in0=v4, scalar1=0.0625,
                                        scalar2=None, op0=ALU.mult)
                g["sVK"].inc(ins)
                eng.wait_ge(g["sVK"].h, g["sVK"].n)
                vb = v4s.unsqueeze(1).to_broadcast([128, 64, 16])
                ins = eng.tensor_copy(
                    out=sq_sb[:].rearrange("p (a r) -> p a r", a=64), in_=vb)
                g["sVK"].inc(ins)
                eng.wait_ge(g["sVK"].h, g["sVK"].n)
                if i >= 1:
                    eng.wait_ge(g["sTG"].h, 2 * i)
                ins = eng.tensor_tensor(
                    out=G_sb[:].rearrange("p (a r) -> p a r", a=64),
                    in0=sq_sb[:].rearrange("p (a r) -> p a r", a=64),
                    in1=mb, op=ALU.mult)
                g["sVG"].inc(ins)
                if "dvec" in dbg_aps and i == 0:
                    dsnap = V(WG, 2192, F32)
                    eng.tensor_copy(out=dsnap[:, 0:1024], in_=vec_sb)
                    eng.tensor_copy(out=dsnap[:, 1024:2048], in_=sq_sb)
                    eng.tensor_copy(out=dsnap[:, 2048:2112], in_=arrows)
                    eng.tensor_copy(out=dsnap[:, 2112:2176], in_=mask64)
                    ins = eng.tensor_copy(out=dsnap[:, 2176:2192], in_=v4)
                    g["sDBG"].inc(ins)

        def ph3a_scalar(eng):
            for i in range(NT):
                eng.wait_ge(g["sXL"].h, 16 * (i + 1))
                ins = eng.activation(out=xi3[i % 2], in_=xi3[i % 2],
                                     func=ACT.Identity,
                                     scale=is1p1[:, i:i + 1],
                                     bias=nm1s[:, i:i + 1])
                g["sTR"].inc(ins)
                eng.wait_ge(g["sRc3"].h, i + 1)
                ins = eng.activation(out=sc3[:, 1:2], in_=sc3[:, 0:1], func=ACT.Sqrt)
                g["sSq3"].inc(ins)
                for bt in range(4):
                    nb3 = i * 4 + bt
                    eng.wait_ge(g["sTY"].h, nb3 + 1)
                    if i >= 1 and bt == 0:
                        eng.wait_ge(g["sY3W"].h, 16 * i)
                    dst = y3t_sb.rearrange("p (hc c) -> p hc c", hc=NH)[
                        :, 4 * bt:4 * bt + 4, :]
                    srcp = ptp3[bt % 2][:, 0:512].rearrange(
                        "p (hc c) -> p hc c", hc=4)
                    ins = eng.copy(out=dst, in_=srcp)
                    g["sVYE"].inc(ins)
                eng.wait_ge(g["sTVc"].h, i + 1)
                ins = eng.square(out=sq_sb, in_=pvv[:])
                g["sSQ3"].inc(ins)
                ins = eng.copy(out=vec_sb, in_=pvv[:])
                g["sSQ3"].inc(ins)
                for bt in range(2):
                    nb3 = i * 2 + bt
                    eng.wait_ge(g["sTG"].h, nb3 + 1)
                    if i >= 1 and bt == 0:
                        eng.wait_ge(g["sGW"].h, 16 * i)
                    dst = gt_sb.rearrange("p (arc c) -> p arc c", arc=8)[
                        :, 4 * bt:4 * bt + 4, :]
                    srcp = ptp3[bt % 2][:, 0:512].rearrange(
                        "p (arc c) -> p arc c", arc=4)
                    ins = eng.copy(out=dst, in_=srcp)
                    g["sVGE"].inc(ins)

        block.gpsimd(ph3a_gpsimd)
        block.sync(ph3a_sync)
        block.tensor(ph3a_tensor)
        block.vector(ph3a_vector)
        block.scalar(ph3a_scalar)
        ph3a_ps.close()

        if "dvec" in dbg_aps:
            def dbg_dvec2(eng):
                eng.wait_ge(g["sDBG"].h, 1)
                dsnap = V(WG, 2192, F32)
                eng.dma_start(out=dbg_aps["dvec"], in_=dsnap).then_inc(
                    g["sCON"].h, 16)
                g["sCON"].n += 16
                eng.wait_ge(g["sCON"].h, g["sCON"].n)
            block.sync(dbg_dvec2)
        if dbg_aps and ("x1" in dbg_aps or "y3" in dbg_aps or "gsp" in dbg_aps):
            def dbg_ph3a(eng):
                eng.wait_ge(g["sX1W"].h, NT * 16)
                eng.wait_ge(g["sY3W"].h, NT * 16)
                eng.wait_ge(g["sGW"].h, NT * 16)
                for nm, src in (("x1", x1sp_h), ("y3", y3sp_h), ("gsp", gsp_h)):
                    if nm in dbg_aps:
                        eng.dma_start(out=dbg_aps[nm], in_=src).then_inc(
                            g["sCON"].h, 16)
                        g["sCON"].n += 16
                eng.wait_ge(g["sCON"].h, g["sCON"].n)
            block.sync(dbg_ph3a)

        if NPHASE == 4:
            return nc, dbg_aps

        y3r = [V(A2 + i * 4096, NH * 128, BF16) for i in range(2)]
        gtr = [V(A2 + 8192 + i * 2048, 8 * 128, BF16) for i in range(2)]
        x1r = [V(A2 + 12288 + i * 8192, D, F32) for i in range(2)]
        out_sb = [V(A2 + 28672 + i * 8192, D, F32) for i in range(2)]
        brep3 = V(A2 + 45056, D, F32)

        ph3b_ps = ExitStack()
        pout = [ph3b_ps.enter_context(nc.psum_tensor(f"pout{i}", [128, 2048], F32))
                for i in range(2)]

        def ph3b_sync(eng):
            eng.wait_ge(g["sWC"].h, 32)
            eng.wait_ge(g["sTO"].h, 64)
            for hc in range(NH):
                eng.dma_start_transpose(out=WT[:, hc * D:(hc + 1) * D],
                                        in_=wbf_h[:, hc * 128:(hc + 1) * 128]
                                        ).then_inc(g["sWT"].h, 16)
            g["sWT"].n += NH * 16
            eng.wait_ge(g["sVGE"].h, 32)
            eng.wait_ge(g["sVX"].h, 16)
            eng.wait_ge(g["sSQ3"].h, 32)
            eng.dma_start(out=brep3, in_=bias_h.partition_broadcast(128)).then_inc(
                g["sL3"].h, 16)
            g["sL3"].n += 16
            for i in range(NT):
                if i >= 2:
                    eng.wait_ge(g["sVO"].h, i - 1)
                eng.wait_ge(g["sY3W"].h, 16 * (i + 1))
                src = y3sp_h.rearrange("(hc p) tc -> p hc tc", p=128)[
                    :, :, i * 128:(i + 1) * 128]
                eng.dma_start(out=y3r[i % 2].rearrange("p (hc c) -> p hc c", hc=NH),
                              in_=src).then_inc(g["sL3"].h, 16)
                eng.wait_ge(g["sGW"].h, 16 * (i + 1))
                src = gsp_h.rearrange("(arc p) tc -> p arc tc", p=128)[
                    :, :, i * 128:(i + 1) * 128]
                eng.dma_start(out=gtr[i % 2].rearrange("p (arc c) -> p arc c",
                                                       arc=8),
                              in_=src).then_inc(g["sL3"].h, 16)
                eng.wait_ge(g["sX1W"].h, 16 * (i + 1))
                eng.dma_start(out=x1r[i % 2],
                              in_=x1sp_h[i * 128:(i + 1) * 128, :]).then_inc(
                    g["sL3"].h, 16)
                g["sL3"].n += 48

        def ph3b_tensor(eng):
            eng.wait_ge(g["sBF"].h, 8 * 16)
            eng.wait_ge(g["sVGE"].h, 32)
            eng.wait_ge(g["sVX"].h, 16)
            eng.wait_ge(g["sSQ3"].h, 32)
            for i in range(NT):
                b = i % 2
                eng.wait_ge(g["sL3"].h, 16 + 48 * (i + 1))
                if i >= 2:
                    eng.wait_ge(g["sVO"].h, i - 1)
                for nb in range(4):
                    for arc in range(8):
                        ins = eng.matmul(
                            pout[b][:, nb * 512:(nb + 1) * 512],
                            gtr[b][:, arc * 128:(arc + 1) * 128],
                            Bfl[:, arc * D + nb * 512:arc * D + (nb + 1) * 512],
                            start=(arc == 0), stop=False)
                    for hc in range(NH):
                        if i == 0 and nb == 0:
                            eng.wait_ge(g["sWT"].h, 16 * (hc + 1))
                        ins = eng.matmul(
                            pout[b][:, nb * 512:(nb + 1) * 512],
                            y3r[b][:, hc * 128:(hc + 1) * 128],
                            WT[:, hc * D + nb * 512:hc * D + (nb + 1) * 512],
                            start=False, stop=(hc == NH - 1))
                    g["sT3"].inc(ins)

        def ph3b_vector(eng):
            for i in range(NT):
                b = i % 2
                eng.wait_ge(g["sT3"].h, 4 * (i + 1))
                if i >= 2:
                    eng.wait_ge(g["sOW"].h, 16 * (i - 1))
                eng.tensor_add(out_sb[b], pout[b][:], x1r[b])
                ins = eng.tensor_add(out_sb[b], out_sb[b], brep3)
                g["sVO"].inc(ins)

        def ph3b_scalar(eng):
            for i in range(NT):
                eng.wait_ge(g["sVO"].h, i + 1)
                eng.dma_start(out=out_h[i * 128:(i + 1) * 128, :],
                              in_=out_sb[i % 2]).then_inc(g["sOW"].h, 16)
                g["sOW"].n += 16
            eng.wait_ge(g["sOW"].h, NT * 16)

        block.sync(ph3b_sync)
        block.tensor(ph3b_tensor)
        block.vector(ph3b_vector)
        block.scalar(ph3b_scalar)
        ph3b_ps.close()

        return nc, dbg_aps


def _prep_inputs(L0_attn_x, L0_attn_qkv, L0_attn_o_proj, L0_attn_gamma,
                 L0_attn_beta, L0_ffn_A, L0_ffn_B, L0_ffn_W, L0_ffn_bias):
    common = {
        "qkv": np.ascontiguousarray(L0_attn_qkv, np.float32),
        "o_proj": np.ascontiguousarray(L0_attn_o_proj, np.float32),
        "gamma": np.ascontiguousarray(L0_attn_gamma, np.float32).reshape(1, D),
        "beta": np.ascontiguousarray(L0_attn_beta, np.float32).reshape(1, D),
        "A": np.ascontiguousarray(
            np.asarray(L0_ffn_A, np.float32).reshape(1024, D).T),
        "B": np.ascontiguousarray(
            np.asarray(L0_ffn_B, np.float32).reshape(8, 8, 2048, 16)
            .transpose(0, 2, 1, 3).reshape(8 * 2048, 128)),
        "W": np.ascontiguousarray(L0_ffn_W, np.float32),
        "bias": np.ascontiguousarray(L0_ffn_bias, np.float32).reshape(1, D),
    }
    in_maps = []
    for c in range(8):
        m = dict(common)
        m["x"] = np.ascontiguousarray(L0_attn_x[c], np.float32)
        in_maps.append(m)
    return in_maps


def kernel(L0_attn_x, L0_attn_qkv, L0_attn_o_proj, L0_attn_gamma, L0_attn_beta,
           L0_ffn_A, L0_ffn_B, L0_ffn_W, L0_ffn_bias):
    in_maps = _prep_inputs(L0_attn_x, L0_attn_qkv, L0_attn_o_proj, L0_attn_gamma,
                           L0_attn_beta, L0_ffn_A, L0_ffn_B, L0_ffn_W, L0_ffn_bias)
    nc, _ = build()
    res = run_bass_kernel_spmd(nc, in_maps, core_ids=list(range(8)))
    out = np.stack([np.asarray(r["out"]) for r in res.results], axis=0)
    return out.astype(np.float32)


# version 05
# speedup vs baseline: 1.2316x; 1.0075x over previous
import numpy as np
from contextlib import ExitStack

import concourse.bass as bass
import concourse.mybir as mybir
from concourse.bass_utils import run_bass_kernel_spmd

F32 = mybir.dt.float32
BF16 = mybir.dt.bfloat16
U8 = mybir.dt.uint8
AX = mybir.AxisListType
ALU = mybir.AluOpType
ACT = mybir.ActivationFunctionType

D = 2048
S = 2048
NT = 16
NH = 16
NG = 4
EPS = 1e-5
QSCALE = 1.0 / float(np.sqrt(2048.0))
NEG = -1.0e30

DBG = ()
NPHASE = 5

A1 = 0
A2 = 65536
A3 = 131072
WG = 196608
CN = 210944
TOTAL = 212832


def _sz(dt):
    return mybir.dt.size(dt)


class Ctr:
    def __init__(self, stack, nc, name):
        self.h = stack.enter_context(nc.semaphore(name))
        self.n = 0

    def inc(self, ins, k=1):
        ins.then_inc(self.h, k)
        self.n += k
        return self.n


def build():
    nc = bass.Bass("TRN2", target_bir_lowering=False, debug=False)

    x_h = nc.dram_tensor("x", [S, D], F32, kind="ExternalInput").ap()
    qkv_h = nc.dram_tensor("qkv", [D, 3 * D], F32, kind="ExternalInput").ap()
    oproj_h = nc.dram_tensor("o_proj", [D, D], F32, kind="ExternalInput").ap()
    gamma_h = nc.dram_tensor("gamma", [1, D], F32, kind="ExternalInput").ap()
    beta_h = nc.dram_tensor("beta", [1, D], F32, kind="ExternalInput").ap()
    A_h = nc.dram_tensor("A", [D, 1024], F32, kind="ExternalInput").ap()
    B_h = nc.dram_tensor("B", [8 * 2048, 128], F32, kind="ExternalInput").ap()
    W_h = nc.dram_tensor("W", [D, D], F32, kind="ExternalInput").ap()
    bias_h = nc.dram_tensor("bias", [1, D], F32, kind="ExternalInput").ap()

    out_h = nc.dram_tensor("out", [S, D], F32, kind="ExternalOutput").ap()

    qsp_h = nc.dram_tensor("q_sp", [NH * 128, NG * 512], BF16, kind="Internal").ap()
    aosp_h = nc.dram_tensor("ao_sp", [NH * 128, NG * 512], BF16, kind="Internal").ap()
    x1sp_h = nc.dram_tensor("x1_sp", [S, D], F32, kind="Internal").ap()
    y3sp_h = nc.dram_tensor("y3_sp", [NH * 128, NT * 128], BF16, kind="Internal").ap()
    gsp_h = nc.dram_tensor("g_sp", [8 * 128, NT * 128], BF16, kind="Internal").ap()
    wbf_h = nc.dram_tensor("w_bf", [D, D], BF16, kind="Internal").ap()
    bbf_h = nc.dram_tensor("b_bf", [8 * 2048, 128], BF16, kind="Internal").ap()

    dbg_aps = {}
    def _dbg(name, shape, dtype):
        if name in DBG:
            dbg_aps[name] = nc.dram_tensor("dbg_" + name, shape, dtype,
                                           kind="ExternalOutput").ap()
    _dbg("xn", [128, NH * D], BF16)
    _dbg("stats", [128, 2 * NT], F32)
    _dbg("kf", [128, NH * D], BF16)
    _dbg("vt", [128, NT * D], BF16)
    _dbg("lsum", [128, NT], F32)
    _dbg("qsp", [NH * 128, NG * 512], BF16)
    _dbg("ao", [NH * 128, NG * 512], BF16)
    _dbg("x1", [2048, 2048], F32)
    _dbg("y3", [NH * 128, NT * 128], BF16)
    _dbg("gsp", [8 * 128, NT * 128], BF16)
    _dbg("dvec", [128, 2192], F32)

    with ExitStack() as stack:
        arena = stack.enter_context(nc.sbuf_tensor("arena", [128, TOTAL], U8))

        def V(off, cols, dtype, parts=128):
            nb = cols * _sz(dtype)
            assert off % 32 == 0 and off + nb <= TOTAL, (off, nb)
            return arena[0:parts, off:off + nb].bitcast(dtype)

        id_bf = V(CN, 128, BF16)
        mask_f = V(CN + 256, 128, F32)
        is1p1 = V(CN + 768, NT, F32)
        nm1s = V(CN + 832, NT, F32)
        l_sums = V(CN + 896, NT, F32)
        inv_l = V(CN + 960, NT, F32)
        id_f32 = V(CN + 1024, 128, F32)

        xin = [V(A2 + i * 8192, D, F32) for i in range(2)]
        y1f = V(A2 + 16384, D, F32)
        y2f = V(A2 + 24576, D, F32)
        ttmp = V(A2 + 32768, D, F32)
        xnt_bf = V(A2 + 16384, D, BF16)
        grep = V(A2 + 40960, D, F32)
        brep = V(A2 + 49152, D, F32)
        st4 = V(A2 + 57344, 24, F32)
        mv2 = V(A2 + 57440, 2, F32)
        sc1 = V(A2 + 57472, 8, F32)

        xn = V(A3, NH * D, BF16)
        kf = V(A1, NH * D, BF16)
        vt = V(A2, NT * D, BF16)

        wkb = [V(WG + i * 4096, NH * 128, BF16) for i in range(2)]
        qstage = V(WG + 8192, D, BF16)
        wvb = [V(A1 + i * 16384, NH * 512, BF16) for i in range(2)]

        sems = {}
        sems_extra = ("sDBG",)
        for name in ("sSET", "sCON", "sX", "sSq", "sRc", "sV1", "sXN", "sT1",
                     "sE1", "sWV", "sTV", "sEV", "sWK", "sTK", "sEK",
                     "sWQ", "sTQ", "sEQ", "sQS",
                     "sQL", "sTS", "sVM", "sSP", "sTT", "sVT", "sPZ",
                     "sTA", "sEA", "sAO",
                     "sWC", "sOW3", "sAW", "sAL", "sXL", "sTO", "sVX", "sX1W",
                     "sY3W", "sGW", "sRc3", "sSq3", "sSQ3", "sVY", "sTY",
                     "sVYE", "sTVc", "sVG", "sTG", "sVGE", "sVK", "sV3",
                     "sWT", "sBF", "sL3", "sT3", "sVO", "sOW", "sTR", "sS1") + sems_extra:
            sems[name] = Ctr(stack, nc, name)
        g = sems

        block = stack.enter_context(nc.Block())

        def setup_sync(eng):
            eng.dma_start(out=grep, in_=gamma_h.partition_broadcast(128)).then_inc(
                g["sCON"].h, 16)
            eng.dma_start(out=brep, in_=beta_h.partition_broadcast(128)).then_inc(
                g["sCON"].h, 16)
            g["sCON"].n += 32

        def setup_gpsimd(eng):
            eng.memset(mask_f, 0.0)
            ins = eng.affine_select(out=mask_f, in_=mask_f, pattern=[[-1, 128]],
                                    compare_op=ALU.is_ge, fill=NEG, base=0,
                                    channel_multiplier=1)
            g["sSET"].inc(ins)
            eng.memset(id_f32, 0.0)
            ins = eng.affine_select(out=id_f32, in_=id_f32, pattern=[[-1, 128]],
                                    compare_op=ALU.not_equal, fill=1.0, base=0,
                                    channel_multiplier=1)
            g["sSET"].inc(ins)

        block.sync(setup_sync)
        block.gpsimd(setup_gpsimd)

        def setup_vector(eng):
            eng.wait_ge(g["sSET"].h, 2)
            ins = eng.tensor_copy(out=id_bf, in_=id_f32)
            g["sSET"].inc(ins)

        block.vector(setup_vector)

        ph1_ps = ExitStack()
        tp_ps = [ph1_ps.enter_context(nc.psum_tensor(f"tp{i}", [128, 1024], BF16))
                 for i in range(2)]

        def ph1_sync(eng):
            for i in range(NT):
                if i >= 2:
                    eng.wait_ge(g["sXN"].h, i - 1)
                eng.dma_start(out=xin[i % 2],
                              in_=x_h[i * 128:(i + 1) * 128, :]).then_inc(
                    g["sX"].h, 16)
            g["sX"].n += NT * 16

        def ph1_vector(eng):
            INV_D = 1.0 / 2048.0
            for i in range(NT):
                xt = xin[i % 2]
                if i >= 1:
                    eng.wait_ge(g["sT1"].h, 4 * i)
                eng.wait_ge(g["sS1"].h, 2 * (i + 1))
                eng.tensor_scalar(out=sc1[:, 6:7], in0=sc1[:, 0:1],
                                  scalar1=INV_D, scalar2=None, op0=ALU.mult)
                ins = eng.tensor_scalar(out=sc1[:, 7:8], in0=sc1[:, 1:2],
                                        scalar1=INV_D, scalar2=None, op0=ALU.mult)
                g["sV1"].inc(ins)
                eng.wait_ge(g["sV1"].h, g["sV1"].n)
                ins = eng.tensor_scalar(out=sc1[:, 2:3], in0=sc1[:, 6:7],
                                        scalar1=sc1[:, 6:7], scalar2=None,
                                        op0=ALU.mult)
                g["sV1"].inc(ins)
                eng.wait_ge(g["sV1"].h, g["sV1"].n)
                ins = eng.tensor_tensor(out=sc1[:, 3:4], in0=sc1[:, 7:8],
                                        in1=sc1[:, 2:3], op=ALU.subtract)
                g["sV1"].inc(ins)
                eng.wait_ge(g["sV1"].h, g["sV1"].n)
                ins = eng.tensor_scalar(out=sc1[:, 2:3], in0=sc1[:, 3:4],
                                        scalar1=1.0 + EPS, scalar2=EPS * EPS,
                                        op0=ALU.mult, op1=ALU.add)
                g["sRc"].inc(ins)
                ins = eng.tensor_scalar(out=sc1[:, 3:4], in0=sc1[:, 3:4],
                                        scalar1=EPS, scalar2=None, op0=ALU.add)
                g["sRc"].inc(ins)
                eng.wait_ge(g["sSq"].h, 2 * i + 1)
                ins = eng.reciprocal(out=sc1[:, 7:8], in_=sc1[:, 4:5])
                g["sV1"].inc(ins)
                eng.wait_ge(g["sV1"].h, g["sV1"].n)
                eng.wait_ge(g["sSq"].h, 2 * i + 2)
                ins = eng.reciprocal(out=sc1[:, 4:5], in_=sc1[:, 5:6])
                g["sV1"].inc(ins)
                eng.wait_ge(g["sV1"].h, g["sV1"].n)
                eng.tensor_scalar(out=is1p1[:, i:i + 1], in0=sc1[:, 4:5],
                                  scalar1=1.0, scalar2=None, op0=ALU.add)
                eng.tensor_scalar(out=nm1s[:, i:i + 1], in0=sc1[:, 6:7],
                                  scalar1=sc1[:, 4:5], scalar2=-1.0,
                                  op0=ALU.mult, op1=ALU.mult)
                ins = eng.tensor_scalar(out=sc1[:, 2:3], in0=sc1[:, 6:7],
                                        scalar1=-1.0, scalar2=None, op0=ALU.mult)
                g["sV1"].inc(ins)
                eng.wait_ge(g["sV1"].h, g["sV1"].n)
                if i == 0:
                    eng.wait_ge(g["sCON"].h, 32)
                eng.scalar_tensor_tensor(out=ttmp, in0=xt, scalar=sc1[:, 2:3],
                                         in1=grep, op0=ALU.add, op1=ALU.mult)
                ins = eng.scalar_tensor_tensor(out=xnt_bf, in0=ttmp,
                                               scalar=sc1[:, 7:8], in1=brep,
                                               op0=ALU.mult, op1=ALU.add)
                g["sXN"].inc(ins)

        def ph1_tensor(eng):
            eng.wait_ge(g["sSET"].h, 3)
            for i in range(NT):
                eng.wait_ge(g["sXN"].h, i + 1)
                for gq in range(4):
                    b = (i * 4 + gq) % 2
                    if i * 4 + gq >= 2:
                        eng.wait_ge(g["sE1"].h, i * 4 + gq - 1)
                    for k in range(4):
                        c = gq * 4 + k
                        ins = eng.transpose(tp_ps[b][:, k * 128:(k + 1) * 128],
                                            xnt_bf[:, c * 128:(c + 1) * 128], id_bf)
                    g["sT1"].inc(ins)

        def ph1_scalar(eng):
            for i in range(NT):
                xt = xin[i % 2]
                eng.wait_ge(g["sX"].h, 16 * (i + 1))
                ins = eng.activation(out=y2f, in_=xt, func=ACT.Identity,
                                     accum_out=sc1[:, 0:1])
                g["sS1"].inc(ins)
                ins = eng.activation(out=y2f, in_=xt, func=ACT.Square,
                                     accum_out=sc1[:, 1:2])
                g["sS1"].inc(ins)
                eng.wait_ge(g["sRc"].h, 2 * i + 1)
                ins = eng.activation(out=sc1[:, 4:5], in_=sc1[:, 2:3], func=ACT.Sqrt)
                g["sSq"].inc(ins)
                eng.wait_ge(g["sRc"].h, 2 * i + 2)
                ins = eng.activation(out=sc1[:, 5:6], in_=sc1[:, 3:4], func=ACT.Sqrt)
                g["sSq"].inc(ins)
                for gq in range(4):
                    b = (i * 4 + gq) % 2
                    eng.wait_ge(g["sT1"].h, i * 4 + gq + 1)
                    dst = xn.rearrange("p (hc t) -> p hc t", hc=NH)[
                        :, 4 * gq:4 * gq + 4, i * 128:(i + 1) * 128]
                    srcp = tp_ps[b][:, 0:512].rearrange("p (hc t) -> p hc t", hc=4)
                    ins = eng.copy(out=dst, in_=srcp)
                    g["sE1"].inc(ins)

        block.sync(ph1_sync)
        block.vector(ph1_vector)
        block.tensor(ph1_tensor)
        block.scalar(ph1_scalar)

        if "xn" in dbg_aps or "stats" in dbg_aps:
            def dbg_ph1(eng):
                eng.wait_ge(g["sE1"].h, NT * 4)
                eng.wait_ge(g["sXN"].h, NT)
                if "xn" in dbg_aps:
                    eng.dma_start(out=dbg_aps["xn"], in_=xn).then_inc(g["sCON"].h, 16)
                    g["sCON"].n += 16
                if "stats" in dbg_aps:
                    st = dbg_aps["stats"]
                    eng.dma_start(out=st[:, 0:NT], in_=is1p1).then_inc(g["sCON"].h, 16)
                    eng.dma_start(out=st[:, NT:2 * NT], in_=nm1s).then_inc(
                        g["sCON"].h, 16)
                    g["sCON"].n += 32
                eng.wait_ge(g["sCON"].h, g["sCON"].n)
            block.sync(dbg_ph1)

        ph1_ps.close()
        if NPHASE == 1:
            return nc, dbg_aps

        genv_ps = ExitStack()
        pv = [genv_ps.enter_context(nc.psum_tensor(f"pv{i}", [128, 512], F32))
              for i in range(2)]

        def genv_gpsimd(eng):
            for dvb in range(4):
                if dvb >= 2:
                    eng.wait_ge(g["sTV"].h, 16 * (dvb - 1))
                src = qkv_h[:, 2 * D + dvb * 512:2 * D + (dvb + 1) * 512]
                src = src.rearrange("(hc p) n -> p hc n", p=128)
                dst = wvb[dvb % 2].rearrange("p (hc n) -> p hc n", hc=NH)
                eng.dma_start(out=dst, in_=src).then_inc(g["sWV"].h, 16)
            g["sWV"].n += 4 * 16

        def genv_tensor(eng):
            eng.wait_ge(g["sE1"].h, NT * 4)
            for dvb in range(4):
                eng.wait_ge(g["sWV"].h, 16 * (dvb + 1))
                for tc in range(NT):
                    n = dvb * NT + tc
                    b = n % 2
                    if n >= 2:
                        eng.wait_ge(g["sEV"].h, n - 1)
                    for hc in range(NH):
                        ins = eng.matmul(
                            pv[b][:],
                            xn[:, hc * D + tc * 128:hc * D + (tc + 1) * 128],
                            wvb[dvb % 2][:, hc * 512:(hc + 1) * 512],
                            start=(hc == 0), stop=(hc == NH - 1))
                    g["sTV"].inc(ins)

        def genv_vector(eng):
            for dvb in range(4):
                for tc in range(NT):
                    n = dvb * NT + tc
                    eng.wait_ge(g["sTV"].h, n + 1)
                    ins = eng.tensor_copy(
                        out=vt[:, tc * D + dvb * 512:tc * D + (dvb + 1) * 512],
                        in_=pv[n % 2][:])
                    g["sEV"].inc(ins)

        def wcast_gpsimd(eng):
            eng.dma_start(out=wbf_h, in_=W_h).then_inc(g["sWC"].h, 16)
            eng.dma_start(out=bbf_h, in_=B_h).then_inc(g["sWC"].h, 16)
            g["sWC"].n += 32

        block.gpsimd(genv_gpsimd)
        block.gpsimd(wcast_gpsimd)
        block.tensor(genv_tensor)
        block.vector(genv_vector)
        genv_ps.close()

        genk_ps = ExitStack()
        pg = [genk_ps.enter_context(nc.psum_tensor(f"pg{i}", [128, 2048], F32))
              for i in range(2)]

        def genkq_gpsimd(eng):
            for phase, base in (("k", 1 * D), ("q", 0)):
                for dkc in range(NH):
                    n = (0 if phase == "k" else NH) + dkc
                    if n >= 2:
                        sem = g["sTK"] if n - 2 < NH else g["sTQ"]
                        eng.wait_ge(sem.h, (n - 2) % NH + 1)
                    src = qkv_h[:, base + dkc * 128:base + (dkc + 1) * 128]
                    src = src.rearrange("(hc p) c -> p hc c", p=128)
                    dst = wkb[n % 2].rearrange("p (hc c) -> p hc c", hc=NH)
                    eng.dma_start(out=dst, in_=src).then_inc(g["sWK"].h, 16)
            g["sWK"].n += 2 * NH * 16

        def genkq_tensor(eng):
            eng.wait_ge(g["sEV"].h, 64)
            for dkc in range(NH):
                eng.wait_ge(g["sWK"].h, 16 * (dkc + 1))
                b = dkc % 2
                if dkc >= 2:
                    eng.wait_ge(g["sEK"].h, dkc - 1)
                for tb in range(4):
                    for hc in range(NH):
                        ins = eng.matmul(
                            pg[b][:, tb * 512:(tb + 1) * 512],
                            wkb[dkc % 2][:, hc * 128:(hc + 1) * 128],
                            xn[:, hc * D + tb * 512:hc * D + (tb + 1) * 512],
                            start=(hc == 0), stop=(hc == NH - 1))
                g["sTK"].inc(ins)
            for dkc in range(NH):
                eng.wait_ge(g["sWK"].h, 16 * (NH + dkc + 1))
                b = dkc % 2
                if dkc >= 2:
                    eng.wait_ge(g["sEQ"].h, dkc - 1)
                else:
                    eng.wait_ge(g["sEK"].h, NH)
                for tb in range(4):
                    for hc in range(NH):
                        ins = eng.matmul(
                            pg[b][:, tb * 512:(tb + 1) * 512],
                            wkb[dkc % 2][:, hc * 128:(hc + 1) * 128],
                            xn[:, hc * D + tb * 512:hc * D + (tb + 1) * 512],
                            start=(hc == 0), stop=(hc == NH - 1))
                g["sTQ"].inc(ins)

        def genkq_vector(eng):
            for dkc in range(NH):
                eng.wait_ge(g["sTK"].h, dkc + 1)
                ins = eng.tensor_copy(out=kf[:, dkc * D:(dkc + 1) * D],
                                      in_=pg[dkc % 2][:])
                g["sEK"].inc(ins)
            for dkc in range(NH):
                eng.wait_ge(g["sTQ"].h, dkc + 1)
                if dkc >= 1:
                    eng.wait_ge(g["sQS"].h, 16 * dkc)
                ins = eng.tensor_scalar(out=qstage, in0=pg[dkc % 2][:],
                                        scalar1=QSCALE, scalar2=None,
                                        op0=ALU.mult)
                g["sEQ"].inc(ins)

        def genkq_scalar(eng):
            for dkc in range(NH):
                eng.wait_ge(g["sEQ"].h, dkc + 1)
                eng.dma_start(out=qsp_h[dkc * 128:(dkc + 1) * 128, :],
                              in_=qstage).then_inc(g["sQS"].h, 16)
            g["sQS"].n += NH * 16

        block.gpsimd(genkq_gpsimd)
        block.tensor(genkq_tensor)
        block.vector(genkq_vector)
        block.scalar(genkq_scalar)
        genk_ps.close()

        if dbg_aps:
            def dbg_gen(eng):
                eng.wait_ge(g["sEK"].h, NH)
                eng.wait_ge(g["sEV"].h, 64)
                eng.wait_ge(g["sQS"].h, NH * 16)
                if "kf" in dbg_aps:
                    eng.dma_start(out=dbg_aps["kf"], in_=kf).then_inc(g["sCON"].h, 16)
                    g["sCON"].n += 16
                if "vt" in dbg_aps:
                    eng.dma_start(out=dbg_aps["vt"], in_=vt).then_inc(g["sCON"].h, 16)
                    g["sCON"].n += 16
                if "qsp" in dbg_aps:
                    eng.dma_start(out=dbg_aps["qsp"], in_=qsp_h).then_inc(
                        g["sCON"].h, 16)
                    g["sCON"].n += 16
                eng.wait_ge(g["sCON"].h, g["sCON"].n)
            block.sync(dbg_gen)

        if NPHASE == 2:
            return nc, dbg_aps

        q_sb = V(A3, NH * 512, BF16)
        pT = [V(A3 + 16384 + i * 16384, NT * 512, BF16) for i in range(2)]
        p_sb = [V(A3 + 49152 + i * 4096, D, BF16) for i in range(2)]
        ao_st = [V(WG + i * 1024, 512, BF16) for i in range(2)]
        negm = V(WG + 2048, 2, F32)
        dum2 = V(WG + 2080, 1, F32)

        ph2_ps = ExitStack()
        psc = ph2_ps.enter_context(nc.psum_tensor("psc", [128, 2048], F32))
        ptp2 = [ph2_ps.enter_context(nc.psum_tensor(f"ptp2{i}", [128, 1024], BF16))
                for i in range(2)]
        pav = [ph2_ps.enter_context(nc.psum_tensor(f"pav{i}", [128, 512], F32))
               for i in range(2)]

        def _nbt(s_tile):
            return (s_tile + 1 + 3) // 4

        def ph2_sync(eng):
            eng.wait_ge(g["sTQ"].h, NH)
            for dkc in range(NH):
                eng.wait_ge(g["sQS"].h, 16 * (dkc + 1))
                eng.dma_start(
                    out=q_sb[:, dkc * 512:(dkc + 1) * 512],
                    in_=qsp_h[dkc * 128:(dkc + 1) * 128, 0:512]).then_inc(
                    g["sQL"].h, 16)
            g["sQL"].n += NH * 16
            for gI in range(1, NG):
                eng.wait_ge(g["sTS"].h, 4 * gI)
                src = qsp_h[:, gI * 512:(gI + 1) * 512].rearrange(
                    "(dkc p) n -> p dkc n", p=128)
                dst = q_sb.rearrange("p (dkc n) -> p dkc n", dkc=NH)
                eng.dma_start(out=dst, in_=src).then_inc(g["sQL"].h, 16)
            g["sQL"].n += (NG - 1) * 16

        def ph2_tensor(eng):
            av_idx = 0
            gbt = 0
            for gI in range(NG):
                eng.wait_ge(g["sQL"].h, 16 * (NH + gI) if gI >= 1 else 16 * NH)
                for j in range(4):
                    s_tile = 4 * gI + j
                    T = (s_tile + 1) * 128
                    if gI == 0:
                        sc_base = (j % 2) * 512
                        eng.wait_ge(g["sSP"].h, max(0, s_tile - 1))
                    elif gI == 1:
                        sc_base = (j % 2) * 1024
                        eng.wait_ge(g["sSP"].h, 4 if j < 2 else s_tile - 1)
                    else:
                        sc_base = 0
                        eng.wait_ge(g["sSP"].h, s_tile)
                    nb = (T + 511) // 512
                    for tb in range(nb):
                        N = min(512, T - tb * 512)
                        for dkc in range(NH):
                            ins = eng.matmul(
                                psc[:, sc_base + tb * 512:sc_base + tb * 512 + N],
                                q_sb[:, dkc * 512 + j * 128:dkc * 512 + (j + 1) * 128],
                                kf[:, dkc * D + tb * 512:dkc * D + tb * 512 + N],
                                start=(dkc == 0), stop=(dkc == NH - 1))
                    g["sTS"].inc(ins)
                    if gI >= 1:
                        if j == 0:
                            eng.wait_ge(g["sPZ"].h, gI)
                        ntc = 4 * gI
                        for dvc in range(4 * j, 4 * j + 4):
                            b = av_idx % 2
                            if av_idx >= 2:
                                eng.wait_ge(g["sEA"].h, av_idx - 1)
                            for tc in range(ntc):
                                ins = eng.matmul(
                                    pav[b][:],
                                    vt[:, tc * D + dvc * 128:tc * D + (dvc + 1) * 128],
                                    pT[gI % 2][:, tc * 512:(tc + 1) * 512],
                                    start=(tc == 0), stop=(tc == ntc - 1))
                            g["sTA"].inc(ins)
                            av_idx += 1
                    eng.wait_ge(g["sSP"].h, s_tile + 1)
                    ntp = s_tile + 1
                    for bt in range(_nbt(s_tile)):
                        b = gbt % 2
                        if gbt >= 2:
                            eng.wait_ge(g["sVT"].h, gbt - 1)
                        for k in range(min(4, ntp - 4 * bt)):
                            tc = 4 * bt + k
                            ins = eng.transpose(
                                ptp2[b][:, k * 128:(k + 1) * 128],
                                p_sb[j % 2][:, tc * 128:(tc + 1) * 128], id_bf)
                        g["sTT"].inc(ins)
                        gbt += 1
            eng.wait_ge(g["sPZ"].h, NG)
            for dvc in range(16):
                b = av_idx % 2
                if av_idx >= 2:
                    eng.wait_ge(g["sEA"].h, av_idx - 1)
                for tc in range(16):
                    ins = eng.matmul(
                        pav[b][:],
                        vt[:, tc * D + dvc * 128:tc * D + (dvc + 1) * 128],
                        pT[0][:, tc * 512:(tc + 1) * 512],
                        start=(tc == 0), stop=(tc == 15))
                g["sTA"].inc(ins)
                av_idx += 1

        def ph2_vector(eng):
            gbt = 0
            for gI in range(NG):
                if gI >= 2:
                    eng.wait_ge(g["sTA"].h, 16 * (gI - 1))
                for j in range(4):
                    s_tile = 4 * gI + j
                    T = (s_tile + 1) * 128
                    sc_base = ((j % 2) * 512 if gI == 0 else
                               (j % 2) * 1024 if gI == 1 else 0)
                    eng.wait_ge(g["sTS"].h, s_tile + 1)
                    eng.tensor_add(psc[:, sc_base + s_tile * 128:sc_base + T],
                                   psc[:, sc_base + s_tile * 128:sc_base + T],
                                   mask_f)
                    ins = eng.reduce_max(out=negm[:, j % 2:j % 2 + 1],
                                         in_=psc[:, sc_base:sc_base + T],
                                         axis=AX.X, negate=True)
                    g["sVM"].inc(ins)
                    for bt in range(_nbt(s_tile)):
                        eng.wait_ge(g["sTT"].h, gbt + 1)
                        nk = min(4, s_tile + 1 - 4 * bt)
                        dst = pT[(gI + 1) % 2].rearrange("p (tc c) -> p tc c", tc=NT)[
                            :, 4 * bt:4 * bt + nk, j * 128:(j + 1) * 128]
                        src = ptp2[gbt % 2][:, 0:nk * 128].rearrange(
                            "p (tc c) -> p tc c", tc=nk)
                        ins = eng.tensor_copy(out=dst, in_=src)
                        g["sVT"].inc(ins)
                        gbt += 1
                    hi = 4 * (gI + 1)
                    if s_tile + 1 < hi:
                        zv = pT[(gI + 1) % 2].rearrange("p (tc c) -> p tc c", tc=NT)[
                            :, s_tile + 1:hi, j * 128:(j + 1) * 128]
                        eng.memset(zv, 0.0)
                ins = eng.tensor_copy(out=dum2, in_=negm[:, 0:1])
                g["sPZ"].inc(ins)

        def ph2_scalar(eng):
            av_idx = 0
            for gI in range(NG):
                for j in range(4):
                    s_tile = 4 * gI + j
                    T = (s_tile + 1) * 128
                    sc_base = ((j % 2) * 512 if gI == 0 else
                               (j % 2) * 1024 if gI == 1 else 0)
                    eng.wait_ge(g["sVM"].h, s_tile + 1)
                    ins = eng.activation(out=p_sb[j % 2][:, 0:T],
                                         in_=psc[:, sc_base:sc_base + T],
                                         func=ACT.Exp, bias=negm[:, j % 2:j % 2 + 1],
                                         scale=1.0,
                                         accum_out=l_sums[:, s_tile:s_tile + 1])
                    g["sSP"].inc(ins)
                    if gI >= 1:
                        for dvc in range(4 * j, 4 * j + 4):
                            eng.wait_ge(g["sTA"].h, av_idx + 1)
                            if av_idx >= 2:
                                eng.wait_ge(g["sAO"].h, 16 * (av_idx - 1))
                            b2 = av_idx % 2
                            ins = eng.copy(out=ao_st[b2], in_=pav[b2][:])
                            g["sEA"].inc(ins)
                            eng.wait_ge(g["sEA"].h, g["sEA"].n)
                            eng.dma_start(
                                out=aosp_h[dvc * 128:(dvc + 1) * 128,
                                           (gI - 1) * 512:gI * 512],
                                in_=ao_st[b2]).then_inc(g["sAO"].h, 16)
                            g["sAO"].n += 16
                            av_idx += 1
            for dvc in range(16):
                eng.wait_ge(g["sTA"].h, av_idx + 1)
                if av_idx >= 2:
                    eng.wait_ge(g["sAO"].h, 16 * (av_idx - 1))
                b2 = av_idx % 2
                ins = eng.copy(out=ao_st[b2], in_=pav[b2][:])
                g["sEA"].inc(ins)
                eng.wait_ge(g["sEA"].h, g["sEA"].n)
                eng.dma_start(out=aosp_h[dvc * 128:(dvc + 1) * 128,
                                         3 * 512:4 * 512],
                              in_=ao_st[b2]).then_inc(g["sAO"].h, 16)
                g["sAO"].n += 16
                av_idx += 1

        block.sync(ph2_sync)
        block.tensor(ph2_tensor)
        block.vector(ph2_vector)
        block.scalar(ph2_scalar)
        ph2_ps.close()

        if dbg_aps and ("ao" in dbg_aps or "lsum" in dbg_aps):
            def dbg_ph2(eng):
                eng.wait_ge(g["sAO"].h, g["sAO"].n)
                if "ao" in dbg_aps:
                    eng.dma_start(out=dbg_aps["ao"], in_=aosp_h).then_inc(
                        g["sCON"].h, 16)
                    g["sCON"].n += 16
                if "lsum" in dbg_aps:
                    eng.dma_start(out=dbg_aps["lsum"], in_=l_sums).then_inc(
                        g["sCON"].h, 16)
                    g["sCON"].n += 16
                eng.wait_ge(g["sCON"].h, g["sCON"].n)
            block.sync(dbg_ph2)

        if NPHASE == 3:
            return nc, dbg_aps

        opj_bf = V(A1, NH * D, BF16)
        WT = V(A1, NH * D, BF16)
        Bfl = V(A3 + 32768, 8 * D, BF16)
        A_bf = V(A3, NH * 1024, BF16)
        xi3 = [V(A2 + i * 8192, D, F32) for i in range(2)]
        ao_rd = V(A2 + 16384, NH * 512, BF16)
        x1_sb = V(A2 + 32768, D, F32)
        y3_sb = V(A2 + 40960, D, BF16)
        y3t_sb = V(A2 + 45056, NH * 128, BF16)
        vec_sb = V(A2 + 49152, 1024, F32)
        sq_sb = V(A2 + 53248, 1024, F32)
        G_sb = V(A2 + 57344, 1024, BF16)
        gt_sb = V(A2 + 59392, 8 * 128, BF16)
        arrows = V(A2 + 61440, 64, F32)
        scr64 = V(A2 + 61696, 64, F32)
        mask64 = V(A2 + 61952, 64, F32)
        m8 = V(A2 + 62208, 8, F32)
        v4 = V(A2 + 62240, 16, F32)
        v4s = V(A2 + 62304, 16, F32)
        st4b = V(A2 + 62368, 24, F32)
        mv2b = V(A2 + 62464, 2, F32)
        sc3 = V(A2 + 62496, 8, F32)

        ph3a_ps = ExitStack()
        po = ph3a_ps.enter_context(nc.psum_tensor("po", [128, 2048], F32))
        pvv = ph3a_ps.enter_context(nc.psum_tensor("pvv", [128, 1024], F32))
        ptp3 = [ph3a_ps.enter_context(nc.psum_tensor(f"ptp3{i}", [128, 1024], BF16))
                for i in range(2)]

        def bfl_prefetch(eng):
            for c in range(8):
                eng.dma_start_transpose(
                    out=Bfl[:, c * D:(c + 1) * D],
                    in_=bbf_h[c * D:(c + 1) * D, :]).then_inc(g["sBF"].h, 16)
            g["sBF"].n += 8 * 16

        def ph3a_gpsimd(eng):
            eng.wait_ge(g["sTS"].h, 16)
            for q4 in range(4):
                src = oproj_h[q4 * 512:(q4 + 1) * 512, :].rearrange(
                    "(dvc p) h -> p dvc h", p=128)
                dst = opj_bf.rearrange("p (dvc h) -> p dvc h", dvc=NH)[
                    :, q4 * 4:(q4 + 1) * 4, :]
                eng.dma_start(out=dst, in_=src).then_inc(g["sOW3"].h, 16)
            g["sOW3"].n += 64
            eng.wait_ge(g["sTA"].h, 64)
            src = A_h.rearrange("(hc p) ar -> p hc ar", p=128)
            eng.dma_start(out=A_bf.rearrange("p (hc ar) -> p hc ar", hc=NH),
                          in_=src).then_inc(g["sAW"].h, 16)
            g["sAW"].n += 16
            for i in range(NT):
                eng.wait_ge(g["sVX"].h, i + 1)
                eng.dma_start(out=x1sp_h[i * 128:(i + 1) * 128, :],
                              in_=x1_sb).then_inc(g["sX1W"].h, 16)
                g["sX1W"].n += 16
                eng.wait_ge(g["sVYE"].h, 4 * (i + 1))
                dst = y3sp_h.rearrange("(hc p) tc -> p hc tc", p=128)[
                    :, :, i * 128:(i + 1) * 128]
                eng.dma_start(out=dst,
                              in_=y3t_sb.rearrange("p (hc c) -> p hc c", hc=NH)
                              ).then_inc(g["sY3W"].h, 16)
                g["sY3W"].n += 16
                eng.wait_ge(g["sVGE"].h, 2 * (i + 1))
                dst = gsp_h.rearrange("(arc p) tc -> p arc tc", p=128)[
                    :, :, i * 128:(i + 1) * 128]
                eng.dma_start(out=dst,
                              in_=gt_sb.rearrange("p (arc c) -> p arc c", arc=8)
                              ).then_inc(g["sGW"].h, 16)
                g["sGW"].n += 16

        def ph3a_sync(eng):
            eng.wait_ge(g["sWC"].h, 32)
            eng.wait_ge(g["sTT"].h, 40)
            bfl_prefetch(eng)
            eng.wait_ge(g["sTA"].h, 64)
            eng.wait_ge(g["sAO"].h, 64 * 16)
            pass
            for i in range(NT):
                gi = i // 4
                if i % 4 == 0:
                    if gi >= 1:
                        eng.wait_ge(g["sTO"].h, 16 * gi)
                    src = aosp_h[:, gi * 512:(gi + 1) * 512].rearrange(
                        "(dvc p) n -> p dvc n", p=128)
                    eng.dma_start(out=ao_rd.rearrange("p (dvc n) -> p dvc n",
                                                      dvc=NH),
                                  in_=src).then_inc(g["sAL"].h, 16)
                    g["sAL"].n += 16
                if i >= 2:
                    eng.wait_ge(g["sVX"].h, i - 1)
                eng.dma_start(out=xi3[i % 2],
                              in_=x_h[i * 128:(i + 1) * 128, :]).then_inc(
                    g["sXL"].h, 16)
                g["sXL"].n += 16

        def ph3a_tensor(eng):
            eng.wait_ge(g["sEA"].h, 64)
            eng.wait_ge(g["sOW3"].h, 64)
            for it in range(NT + 2):
                if it < NT:
                    i = it
                    gi, j = i // 4, i % 4
                    eng.wait_ge(g["sAL"].h, 16 * (gi + 1))
                    if i >= 1:
                        eng.wait_ge(g["sVX"].h, i)
                    for nb in range(4):
                        for dvc in range(NH):
                            ins = eng.matmul(
                                po[:, nb * 512:(nb + 1) * 512],
                                ao_rd[:, dvc * 512 + j * 128:dvc * 512 + (j + 1) * 128],
                                opj_bf[:, dvc * D + nb * 512:dvc * D + (nb + 1) * 512],
                                start=(dvc == 0), stop=(dvc == NH - 1))
                        g["sTO"].inc(ins)
                if 1 <= it <= NT:
                    i = it - 1
                    eng.wait_ge(g["sVY"].h, i + 1)
                    for bt in range(4):
                        b = bt % 2
                        if bt < 2:
                            if i >= 2:
                                eng.wait_ge(g["sVGE"].h, 2 * (i - 2) + bt + 1)
                        else:
                            eng.wait_ge(g["sVYE"].h, 4 * i + bt - 1)
                        for k in range(4):
                            c = bt * 4 + k
                            ins = eng.transpose(ptp3[b][:, k * 128:(k + 1) * 128],
                                                y3_sb[:, c * 128:(c + 1) * 128],
                                                id_bf)
                        g["sTY"].inc(ins)
                    if i == 0:
                        eng.wait_ge(g["sAW"].h, 16)
                    eng.wait_ge(g["sVYE"].h, 4 * (i + 1))
                    if i >= 1:
                        eng.wait_ge(g["sSQ3"].h, 2 * i)
                    for hc in range(NH):
                        for n2 in range(2):
                            ins = eng.matmul(
                                pvv[:, n2 * 512:(n2 + 1) * 512],
                                y3t_sb[:, hc * 128:(hc + 1) * 128],
                                A_bf[:, hc * 1024 + n2 * 512:hc * 1024 + (n2 + 1) * 512],
                                start=(hc == 0), stop=(hc == NH - 1))
                    g["sTVc"].inc(ins)
                if it >= 2:
                    i = it - 2
                    eng.wait_ge(g["sVG"].h, i + 1)
                    for bt in range(2):
                        b = bt % 2
                        if it <= NT:
                            eng.wait_ge(g["sVYE"].h, 4 * (it - 1) + bt + 3)
                        else:
                            eng.wait_ge(g["sVYE"].h, 4 * (it - 2) + bt + 3)
                        for k in range(4):
                            c = bt * 4 + k
                            ins = eng.transpose(ptp3[b][:, k * 128:(k + 1) * 128],
                                                G_sb[:, c * 128:(c + 1) * 128],
                                                id_bf)
                        g["sTG"].inc(ins)

        def ph3a_vector(eng):
            eng.wait_ge(g["sSP"].h, 16)
            ins = eng.reciprocal(out=inv_l, in_=l_sums)
            g["sV3"].inc(ins)
            eng.wait_ge(g["sV3"].h, g["sV3"].n)
            for i in range(NT):
                eng.wait_ge(g["sTR"].h, i + 1)
                eng.wait_ge(g["sTO"].h, 4 * (i + 1))
                if i >= 1:
                    eng.wait_ge(g["sX1W"].h, 16 * i)
                ins = eng.scalar_tensor_tensor(out=x1_sb, in0=po[:],
                                               scalar=inv_l[:, i:i + 1],
                                               in1=xi3[i % 2],
                                               op0=ALU.mult, op1=ALU.add)
                g["sVX"].inc(ins)
                for k in range(4):
                    ins = eng.bn_stats(out=st4b[:, 6 * k:6 * k + 6],
                                       in_=x1_sb[:, 512 * k:512 * (k + 1)])
                g["sV3"].inc(ins)
                eng.wait_ge(g["sV3"].h, g["sV3"].n)
                ins = eng.bn_aggr(out=mv2b, in_=st4b)
                g["sV3"].inc(ins)
                eng.wait_ge(g["sV3"].h, g["sV3"].n)
                ins = eng.tensor_scalar(out=sc3[:, 0:1], in0=mv2b[:, 1:2],
                                        scalar1=EPS, scalar2=None, op0=ALU.add)
                g["sRc3"].inc(ins)
                eng.wait_ge(g["sSq3"].h, i + 1)
                ins = eng.reciprocal(out=sc3[:, 2:3], in_=sc3[:, 1:2])
                g["sV3"].inc(ins)
                eng.wait_ge(g["sV3"].h, g["sV3"].n)
                ins = eng.tensor_scalar(out=sc3[:, 3:4], in0=mv2b[:, 0:1],
                                        scalar1=sc3[:, 2:3], scalar2=-1.0,
                                        op0=ALU.mult, op1=ALU.mult)
                g["sV3"].inc(ins)
                eng.wait_ge(g["sV3"].h, g["sV3"].n)
                if i >= 1:
                    eng.wait_ge(g["sTY"].h, 4 * i)
                ins = eng.tensor_scalar(out=y3_sb, in0=x1_sb,
                                        scalar1=sc3[:, 2:3], scalar2=sc3[:, 3:4],
                                        op0=ALU.mult, op1=ALU.add)
                g["sVY"].inc(ins)
                if "dvec" in dbg_aps and i == 1:
                    pass
                eng.wait_ge(g["sSQ3"].h, 2 * (i + 1))
                bview = sq_sb[:].rearrange("p (a r) -> p a r", a=64)
                ins = eng.reduce_sum(out=arrows, in_=bview, axis=AX.X)
                g["sVK"].inc(ins)
                eng.wait_ge(g["sVK"].h, g["sVK"].n)
                ins = eng.max(out=m8, in_=arrows)
                g["sVK"].inc(ins)
                eng.wait_ge(g["sVK"].h, g["sVK"].n)
                ins = eng.memset(m8[:, 4:8], -1.0)
                g["sVK"].inc(ins)
                eng.wait_ge(g["sVK"].h, g["sVK"].n)
                ins = eng.match_replace(out=scr64, in_to_replace=m8,
                                        in_values=arrows, imm_value=-1.0)
                g["sVK"].inc(ins)
                eng.wait_ge(g["sVK"].h, g["sVK"].n)
                ins = eng.tensor_tensor(out=mask64, in0=arrows, in1=scr64,
                                        op=ALU.not_equal)
                g["sVK"].inc(ins)
                eng.wait_ge(g["sVK"].h, g["sVK"].n)
                mb = mask64.unsqueeze(2).to_broadcast([128, 64, 16])
                ins = eng.tensor_tensor(
                    out=sq_sb[:].rearrange("p (a r) -> p a r", a=64),
                    in0=vec_sb[:].rearrange("p (a r) -> p a r", a=64),
                    in1=mb, op=ALU.mult)
                g["sVK"].inc(ins)
                eng.wait_ge(g["sVK"].h, g["sVK"].n)
                swap = sq_sb[:].rearrange("p (a r) -> p a r", a=64).transpose(
                    [0, 2, 1])
                ins = eng.reduce_sum(out=v4, in_=swap, axis=AX.X)
                g["sVK"].inc(ins)
                eng.wait_ge(g["sVK"].h, g["sVK"].n)
                ins = eng.tensor_scalar(out=v4s, in0=v4, scalar1=0.0625,
                                        scalar2=None, op0=ALU.mult)
                g["sVK"].inc(ins)
                eng.wait_ge(g["sVK"].h, g["sVK"].n)
                vb = v4s.unsqueeze(1).to_broadcast([128, 64, 16])
                ins = eng.tensor_copy(
                    out=sq_sb[:].rearrange("p (a r) -> p a r", a=64), in_=vb)
                g["sVK"].inc(ins)
                eng.wait_ge(g["sVK"].h, g["sVK"].n)
                if i >= 1:
                    eng.wait_ge(g["sTG"].h, 2 * i)
                ins = eng.tensor_tensor(
                    out=G_sb[:].rearrange("p (a r) -> p a r", a=64),
                    in0=sq_sb[:].rearrange("p (a r) -> p a r", a=64),
                    in1=mb, op=ALU.mult)
                g["sVG"].inc(ins)
                if "dvec" in dbg_aps and i == 0:
                    dsnap = V(WG, 2192, F32)
                    eng.tensor_copy(out=dsnap[:, 0:1024], in_=vec_sb)
                    eng.tensor_copy(out=dsnap[:, 1024:2048], in_=sq_sb)
                    eng.tensor_copy(out=dsnap[:, 2048:2112], in_=arrows)
                    eng.tensor_copy(out=dsnap[:, 2112:2176], in_=mask64)
                    ins = eng.tensor_copy(out=dsnap[:, 2176:2192], in_=v4)
                    g["sDBG"].inc(ins)

        def ph3a_scalar(eng):
            for it in range(NT + 1):
                if it < NT:
                    i = it
                    eng.wait_ge(g["sXL"].h, 16 * (i + 1))
                    ins = eng.activation(out=xi3[i % 2], in_=xi3[i % 2],
                                         func=ACT.Identity,
                                         scale=is1p1[:, i:i + 1],
                                         bias=nm1s[:, i:i + 1])
                    g["sTR"].inc(ins)
                    eng.wait_ge(g["sRc3"].h, i + 1)
                    ins = eng.activation(out=sc3[:, 1:2], in_=sc3[:, 0:1],
                                         func=ACT.Sqrt)
                    g["sSq3"].inc(ins)
                    for bt in range(4):
                        nb3 = i * 4 + bt
                        eng.wait_ge(g["sTY"].h, nb3 + 1)
                        if i >= 1 and bt == 0:
                            eng.wait_ge(g["sY3W"].h, 16 * i)
                        dst = y3t_sb.rearrange("p (hc c) -> p hc c", hc=NH)[
                            :, 4 * bt:4 * bt + 4, :]
                        srcp = ptp3[bt % 2][:, 0:512].rearrange(
                            "p (hc c) -> p hc c", hc=4)
                        ins = eng.copy(out=dst, in_=srcp)
                        g["sVYE"].inc(ins)
                    eng.wait_ge(g["sTVc"].h, i + 1)
                    ins = eng.square(out=sq_sb, in_=pvv[:])
                    g["sSQ3"].inc(ins)
                    ins = eng.copy(out=vec_sb, in_=pvv[:])
                    g["sSQ3"].inc(ins)
                if it >= 1:
                    i = it - 1
                    for bt in range(2):
                        nb3 = i * 2 + bt
                        eng.wait_ge(g["sTG"].h, nb3 + 1)
                        if i >= 1 and bt == 0:
                            eng.wait_ge(g["sGW"].h, 16 * i)
                        dst = gt_sb.rearrange("p (arc c) -> p arc c", arc=8)[
                            :, 4 * bt:4 * bt + 4, :]
                        srcp = ptp3[bt % 2][:, 0:512].rearrange(
                            "p (arc c) -> p arc c", arc=4)
                        ins = eng.copy(out=dst, in_=srcp)
                        g["sVGE"].inc(ins)

        block.gpsimd(ph3a_gpsimd)
        block.sync(ph3a_sync)
        block.tensor(ph3a_tensor)
        block.vector(ph3a_vector)
        block.scalar(ph3a_scalar)
        ph3a_ps.close()

        if "dvec" in dbg_aps:
            def dbg_dvec2(eng):
                eng.wait_ge(g["sDBG"].h, 1)
                dsnap = V(WG, 2192, F32)
                eng.dma_start(out=dbg_aps["dvec"], in_=dsnap).then_inc(
                    g["sCON"].h, 16)
                g["sCON"].n += 16
                eng.wait_ge(g["sCON"].h, g["sCON"].n)
            block.sync(dbg_dvec2)
        if dbg_aps and ("x1" in dbg_aps or "y3" in dbg_aps or "gsp" in dbg_aps):
            def dbg_ph3a(eng):
                eng.wait_ge(g["sX1W"].h, NT * 16)
                eng.wait_ge(g["sY3W"].h, NT * 16)
                eng.wait_ge(g["sGW"].h, NT * 16)
                for nm, src in (("x1", x1sp_h), ("y3", y3sp_h), ("gsp", gsp_h)):
                    if nm in dbg_aps:
                        eng.dma_start(out=dbg_aps[nm], in_=src).then_inc(
                            g["sCON"].h, 16)
                        g["sCON"].n += 16
                eng.wait_ge(g["sCON"].h, g["sCON"].n)
            block.sync(dbg_ph3a)

        if NPHASE == 4:
            return nc, dbg_aps

        y3r = [V(A2 + i * 4096, NH * 128, BF16) for i in range(2)]
        gtr = [V(A2 + 8192 + i * 2048, 8 * 128, BF16) for i in range(2)]
        x1r = [V(A2 + 12288 + i * 8192, D, F32) for i in range(2)]
        out_sb = [V(A2 + 28672 + i * 8192, D, F32) for i in range(2)]
        brep3 = V(A2 + 45056, D, F32)

        ph3b_ps = ExitStack()
        pout = [ph3b_ps.enter_context(nc.psum_tensor(f"pout{i}", [128, 2048], F32))
                for i in range(2)]

        def ph3b_sync(eng):
            eng.wait_ge(g["sWC"].h, 32)
            eng.wait_ge(g["sTO"].h, 64)
            for hc in range(NH):
                eng.dma_start_transpose(out=WT[:, hc * D:(hc + 1) * D],
                                        in_=wbf_h[:, hc * 128:(hc + 1) * 128]
                                        ).then_inc(g["sWT"].h, 16)
            g["sWT"].n += NH * 16
            eng.wait_ge(g["sVGE"].h, 32)
            eng.wait_ge(g["sVX"].h, 16)
            eng.wait_ge(g["sSQ3"].h, 32)
            eng.dma_start(out=brep3, in_=bias_h.partition_broadcast(128)).then_inc(
                g["sL3"].h, 16)
            g["sL3"].n += 16
            for i in range(NT):
                if i >= 2:
                    eng.wait_ge(g["sVO"].h, i - 1)
                eng.wait_ge(g["sY3W"].h, 16 * (i + 1))
                src = y3sp_h.rearrange("(hc p) tc -> p hc tc", p=128)[
                    :, :, i * 128:(i + 1) * 128]
                eng.dma_start(out=y3r[i % 2].rearrange("p (hc c) -> p hc c", hc=NH),
                              in_=src).then_inc(g["sL3"].h, 16)
                eng.wait_ge(g["sGW"].h, 16 * (i + 1))
                src = gsp_h.rearrange("(arc p) tc -> p arc tc", p=128)[
                    :, :, i * 128:(i + 1) * 128]
                eng.dma_start(out=gtr[i % 2].rearrange("p (arc c) -> p arc c",
                                                       arc=8),
                              in_=src).then_inc(g["sL3"].h, 16)
                eng.wait_ge(g["sX1W"].h, 16 * (i + 1))
                eng.dma_start(out=x1r[i % 2],
                              in_=x1sp_h[i * 128:(i + 1) * 128, :]).then_inc(
                    g["sL3"].h, 16)
                g["sL3"].n += 48

        def ph3b_tensor(eng):
            eng.wait_ge(g["sBF"].h, 8 * 16)
            eng.wait_ge(g["sVGE"].h, 32)
            eng.wait_ge(g["sVX"].h, 16)
            eng.wait_ge(g["sSQ3"].h, 32)
            for i in range(NT):
                b = i % 2
                eng.wait_ge(g["sL3"].h, 16 + 48 * (i + 1))
                if i >= 2:
                    eng.wait_ge(g["sVO"].h, i - 1)
                for nb in range(4):
                    for arc in range(8):
                        ins = eng.matmul(
                            pout[b][:, nb * 512:(nb + 1) * 512],
                            gtr[b][:, arc * 128:(arc + 1) * 128],
                            Bfl[:, arc * D + nb * 512:arc * D + (nb + 1) * 512],
                            start=(arc == 0), stop=False)
                    for hc in range(NH):
                        if i == 0 and nb == 0:
                            eng.wait_ge(g["sWT"].h, 16 * (hc + 1))
                        ins = eng.matmul(
                            pout[b][:, nb * 512:(nb + 1) * 512],
                            y3r[b][:, hc * 128:(hc + 1) * 128],
                            WT[:, hc * D + nb * 512:hc * D + (nb + 1) * 512],
                            start=False, stop=(hc == NH - 1))
                    g["sT3"].inc(ins)

        def ph3b_vector(eng):
            for i in range(NT):
                b = i % 2
                eng.wait_ge(g["sT3"].h, 4 * (i + 1))
                if i >= 2:
                    eng.wait_ge(g["sOW"].h, 16 * (i - 1))
                eng.tensor_add(out_sb[b], pout[b][:], x1r[b])
                ins = eng.tensor_add(out_sb[b], out_sb[b], brep3)
                g["sVO"].inc(ins)

        def ph3b_scalar(eng):
            for i in range(NT):
                eng.wait_ge(g["sVO"].h, i + 1)
                eng.dma_start(out=out_h[i * 128:(i + 1) * 128, :],
                              in_=out_sb[i % 2]).then_inc(g["sOW"].h, 16)
                g["sOW"].n += 16
            eng.wait_ge(g["sOW"].h, NT * 16)

        block.sync(ph3b_sync)
        block.tensor(ph3b_tensor)
        block.vector(ph3b_vector)
        block.scalar(ph3b_scalar)
        ph3b_ps.close()

        return nc, dbg_aps


def _prep_inputs(L0_attn_x, L0_attn_qkv, L0_attn_o_proj, L0_attn_gamma,
                 L0_attn_beta, L0_ffn_A, L0_ffn_B, L0_ffn_W, L0_ffn_bias):
    common = {
        "qkv": np.ascontiguousarray(L0_attn_qkv, np.float32),
        "o_proj": np.ascontiguousarray(L0_attn_o_proj, np.float32),
        "gamma": np.ascontiguousarray(L0_attn_gamma, np.float32).reshape(1, D),
        "beta": np.ascontiguousarray(L0_attn_beta, np.float32).reshape(1, D),
        "A": np.ascontiguousarray(
            np.asarray(L0_ffn_A, np.float32).reshape(1024, D).T),
        "B": np.ascontiguousarray(
            np.asarray(L0_ffn_B, np.float32).reshape(8, 8, 2048, 16)
            .transpose(0, 2, 1, 3).reshape(8 * 2048, 128)),
        "W": np.ascontiguousarray(L0_ffn_W, np.float32),
        "bias": np.ascontiguousarray(L0_ffn_bias, np.float32).reshape(1, D),
    }
    in_maps = []
    for c in range(8):
        m = dict(common)
        m["x"] = np.ascontiguousarray(L0_attn_x[c], np.float32)
        in_maps.append(m)
    return in_maps


def kernel(L0_attn_x, L0_attn_qkv, L0_attn_o_proj, L0_attn_gamma, L0_attn_beta,
           L0_ffn_A, L0_ffn_B, L0_ffn_W, L0_ffn_bias):
    in_maps = _prep_inputs(L0_attn_x, L0_attn_qkv, L0_attn_o_proj, L0_attn_gamma,
                           L0_attn_beta, L0_ffn_A, L0_ffn_B, L0_ffn_W, L0_ffn_bias)
    nc, _ = build()
    res = run_bass_kernel_spmd(nc, in_maps, core_ids=list(range(8)))
    out = np.stack([np.asarray(r["out"]) for r in res.results], axis=0)
    return out.astype(np.float32)


# version 06
# speedup vs baseline: 1.2518x; 1.0164x over previous
import numpy as np
from contextlib import ExitStack

import concourse.bass as bass
import concourse.mybir as mybir
from concourse.bass_utils import run_bass_kernel_spmd

F32 = mybir.dt.float32
BF16 = mybir.dt.bfloat16
U8 = mybir.dt.uint8
AX = mybir.AxisListType
ALU = mybir.AluOpType
ACT = mybir.ActivationFunctionType

D = 2048
S = 2048
NT = 16
NH = 16
NG = 4
EPS = 1e-5
QSCALE = 1.0 / float(np.sqrt(2048.0))
NEG = -1.0e30

DBG = ()
NPHASE = 5

A1 = 0
A2 = 65536
A3 = 131072
WG = 196608
CN = 210944
TOTAL = 212832


def _sz(dt):
    return mybir.dt.size(dt)


class Ctr:
    def __init__(self, stack, nc, name):
        self.h = stack.enter_context(nc.semaphore(name))
        self.n = 0

    def inc(self, ins, k=1):
        ins.then_inc(self.h, k)
        self.n += k
        return self.n


def build():
    nc = bass.Bass("TRN2", target_bir_lowering=False, debug=False)

    x_h = nc.dram_tensor("x", [S, D], F32, kind="ExternalInput").ap()
    qkv_h = nc.dram_tensor("qkv", [D, 3 * D], F32, kind="ExternalInput").ap()
    oproj_h = nc.dram_tensor("o_proj", [D, D], F32, kind="ExternalInput").ap()
    gamma_h = nc.dram_tensor("gamma", [1, D], F32, kind="ExternalInput").ap()
    beta_h = nc.dram_tensor("beta", [1, D], F32, kind="ExternalInput").ap()
    A_h = nc.dram_tensor("A", [D, 1024], F32, kind="ExternalInput").ap()
    B_h = nc.dram_tensor("B", [8 * 2048, 128], F32, kind="ExternalInput").ap()
    W_h = nc.dram_tensor("W", [D, D], F32, kind="ExternalInput").ap()
    bias_h = nc.dram_tensor("bias", [1, D], F32, kind="ExternalInput").ap()

    out_h = nc.dram_tensor("out", [S, D], F32, kind="ExternalOutput").ap()

    qsp_h = nc.dram_tensor("q_sp", [NH * 128, NG * 512], BF16, kind="Internal").ap()
    aosp_h = nc.dram_tensor("ao_sp", [NH * 128, NG * 512], BF16, kind="Internal").ap()
    x1sp_h = nc.dram_tensor("x1_sp", [S, D], F32, kind="Internal").ap()
    y3sp_h = nc.dram_tensor("y3_sp", [NH * 128, NT * 128], BF16, kind="Internal").ap()
    gsp_h = nc.dram_tensor("g_sp", [8 * 128, NT * 128], BF16, kind="Internal").ap()
    wbf_h = nc.dram_tensor("w_bf", [D, D], BF16, kind="Internal").ap()
    bbf_h = nc.dram_tensor("b_bf", [8 * 2048, 128], BF16, kind="Internal").ap()

    dbg_aps = {}
    def _dbg(name, shape, dtype):
        if name in DBG:
            dbg_aps[name] = nc.dram_tensor("dbg_" + name, shape, dtype,
                                           kind="ExternalOutput").ap()
    _dbg("xn", [128, NH * D], BF16)
    _dbg("stats", [128, 2 * NT], F32)
    _dbg("kf", [128, NH * D], BF16)
    _dbg("vt", [128, NT * D], BF16)
    _dbg("lsum", [128, NT], F32)
    _dbg("qsp", [NH * 128, NG * 512], BF16)
    _dbg("ao", [NH * 128, NG * 512], BF16)
    _dbg("x1", [2048, 2048], F32)
    _dbg("y3", [NH * 128, NT * 128], BF16)
    _dbg("gsp", [8 * 128, NT * 128], BF16)
    _dbg("dvec", [128, 2192], F32)

    with ExitStack() as stack:
        arena = stack.enter_context(nc.sbuf_tensor("arena", [128, TOTAL], U8))

        def V(off, cols, dtype, parts=128):
            nb = cols * _sz(dtype)
            assert off % 32 == 0 and off + nb <= TOTAL, (off, nb)
            return arena[0:parts, off:off + nb].bitcast(dtype)

        id_bf = V(CN, 128, BF16)
        mask_f = V(CN + 256, 128, F32)
        is1p1 = V(CN + 768, NT, F32)
        nm1s = V(CN + 832, NT, F32)
        l_sums = V(CN + 896, NT, F32)
        inv_l = V(CN + 960, NT, F32)
        id_f32 = V(CN + 1024, 128, F32)

        xin = [V(A2 + i * 8192, D, F32) for i in range(2)]
        y1f = V(A2 + 16384, D, F32)
        y2f = V(A2 + 24576, D, F32)
        ttmp = V(A2 + 32768, D, F32)
        xnt_bf = V(A2 + 16384, D, BF16)
        grep = V(A2 + 40960, D, F32)
        brep = V(A2 + 49152, D, F32)
        st4 = V(A2 + 57344, 24, F32)
        mv2 = V(A2 + 57440, 2, F32)
        sc1 = V(A2 + 57472, 8, F32)

        xn = V(A3, NH * D, BF16)
        kf = V(A1, NH * D, BF16)
        vt = V(A2, NT * D, BF16)

        wkb = [V(WG + i * 4096, NH * 128, BF16) for i in range(2)]
        qstage = V(WG + 8192, D, BF16)
        wvb = [V(A1 + i * 16384, NH * 512, BF16) for i in range(2)]

        sems = {}
        sems_extra = ("sDBG",)
        for name in ("sSET", "sCON", "sX", "sSq", "sRc", "sV1", "sXN", "sT1",
                     "sE1", "sWV", "sTV", "sEV", "sWK", "sTK", "sEK",
                     "sWQ", "sTQ", "sEQ", "sQS",
                     "sQL", "sTS", "sVM", "sSP", "sTT", "sVT", "sPZ",
                     "sTA", "sEA", "sAO",
                     "sWC", "sOW3", "sAW", "sAL", "sXL", "sTO", "sVX", "sX1W",
                     "sY3W", "sGW", "sRc3", "sSq3", "sSQ3", "sVY", "sTY",
                     "sVYE", "sTVc", "sVG", "sTG", "sVGE", "sVK", "sV3",
                     "sWT", "sBF", "sL3", "sT3", "sVO", "sOW", "sTR", "sS1") + sems_extra:
            sems[name] = Ctr(stack, nc, name)
        g = sems

        block = stack.enter_context(nc.Block())

        def setup_sync(eng):
            eng.dma_start(out=grep, in_=gamma_h.partition_broadcast(128)).then_inc(
                g["sCON"].h, 16)
            eng.dma_start(out=brep, in_=beta_h.partition_broadcast(128)).then_inc(
                g["sCON"].h, 16)
            g["sCON"].n += 32

        def setup_gpsimd(eng):
            eng.memset(mask_f, 0.0)
            ins = eng.affine_select(out=mask_f, in_=mask_f, pattern=[[-1, 128]],
                                    compare_op=ALU.is_ge, fill=NEG, base=0,
                                    channel_multiplier=1)
            g["sSET"].inc(ins)
            eng.memset(id_f32, 0.0)
            ins = eng.affine_select(out=id_f32, in_=id_f32, pattern=[[-1, 128]],
                                    compare_op=ALU.not_equal, fill=1.0, base=0,
                                    channel_multiplier=1)
            g["sSET"].inc(ins)

        block.sync(setup_sync)
        block.gpsimd(setup_gpsimd)

        def setup_vector(eng):
            eng.wait_ge(g["sSET"].h, 2)
            ins = eng.tensor_copy(out=id_bf, in_=id_f32)
            g["sSET"].inc(ins)

        block.vector(setup_vector)

        ph1_ps = ExitStack()
        tp_ps = [ph1_ps.enter_context(nc.psum_tensor(f"tp{i}", [128, 1024], BF16))
                 for i in range(2)]

        def ph1_sync(eng):
            for i in range(NT):
                if i >= 2:
                    eng.wait_ge(g["sXN"].h, i - 1)
                eng.dma_start(out=xin[i % 2],
                              in_=x_h[i * 128:(i + 1) * 128, :]).then_inc(
                    g["sX"].h, 16)
            g["sX"].n += NT * 16

        def ph1_vector(eng):
            INV_D = 1.0 / 2048.0
            for i in range(NT):
                xt = xin[i % 2]
                if i >= 1:
                    eng.wait_ge(g["sT1"].h, 4 * i)
                eng.wait_ge(g["sS1"].h, 2 * (i + 1))
                eng.tensor_scalar(out=sc1[:, 6:7], in0=sc1[:, 0:1],
                                  scalar1=INV_D, scalar2=None, op0=ALU.mult)
                ins = eng.tensor_scalar(out=sc1[:, 7:8], in0=sc1[:, 1:2],
                                        scalar1=INV_D, scalar2=None, op0=ALU.mult)
                g["sV1"].inc(ins)
                eng.wait_ge(g["sV1"].h, g["sV1"].n)
                ins = eng.tensor_scalar(out=sc1[:, 2:3], in0=sc1[:, 6:7],
                                        scalar1=sc1[:, 6:7], scalar2=None,
                                        op0=ALU.mult)
                g["sV1"].inc(ins)
                eng.wait_ge(g["sV1"].h, g["sV1"].n)
                ins = eng.tensor_tensor(out=sc1[:, 3:4], in0=sc1[:, 7:8],
                                        in1=sc1[:, 2:3], op=ALU.subtract)
                g["sV1"].inc(ins)
                eng.wait_ge(g["sV1"].h, g["sV1"].n)
                ins = eng.tensor_scalar(out=sc1[:, 2:3], in0=sc1[:, 3:4],
                                        scalar1=1.0 + EPS, scalar2=EPS * EPS,
                                        op0=ALU.mult, op1=ALU.add)
                g["sRc"].inc(ins)
                ins = eng.tensor_scalar(out=sc1[:, 3:4], in0=sc1[:, 3:4],
                                        scalar1=EPS, scalar2=None, op0=ALU.add)
                g["sRc"].inc(ins)
                eng.wait_ge(g["sSq"].h, 2 * i + 1)
                ins = eng.reciprocal(out=sc1[:, 7:8], in_=sc1[:, 4:5])
                g["sV1"].inc(ins)
                eng.wait_ge(g["sV1"].h, g["sV1"].n)
                eng.wait_ge(g["sSq"].h, 2 * i + 2)
                ins = eng.reciprocal(out=sc1[:, 4:5], in_=sc1[:, 5:6])
                g["sV1"].inc(ins)
                eng.wait_ge(g["sV1"].h, g["sV1"].n)
                eng.tensor_scalar(out=is1p1[:, i:i + 1], in0=sc1[:, 4:5],
                                  scalar1=1.0, scalar2=None, op0=ALU.add)
                eng.tensor_scalar(out=nm1s[:, i:i + 1], in0=sc1[:, 6:7],
                                  scalar1=sc1[:, 4:5], scalar2=-1.0,
                                  op0=ALU.mult, op1=ALU.mult)
                ins = eng.tensor_scalar(out=sc1[:, 2:3], in0=sc1[:, 6:7],
                                        scalar1=-1.0, scalar2=None, op0=ALU.mult)
                g["sV1"].inc(ins)
                eng.wait_ge(g["sV1"].h, g["sV1"].n)
                if i == 0:
                    eng.wait_ge(g["sCON"].h, 32)
                eng.scalar_tensor_tensor(out=ttmp, in0=xt, scalar=sc1[:, 2:3],
                                         in1=grep, op0=ALU.add, op1=ALU.mult)
                ins = eng.scalar_tensor_tensor(out=xnt_bf, in0=ttmp,
                                               scalar=sc1[:, 7:8], in1=brep,
                                               op0=ALU.mult, op1=ALU.add)
                g["sXN"].inc(ins)

        def ph1_tensor(eng):
            eng.wait_ge(g["sSET"].h, 3)
            for i in range(NT):
                eng.wait_ge(g["sXN"].h, i + 1)
                for gq in range(4):
                    b = (i * 4 + gq) % 2
                    if i * 4 + gq >= 2:
                        eng.wait_ge(g["sE1"].h, i * 4 + gq - 1)
                    for k in range(4):
                        c = gq * 4 + k
                        ins = eng.transpose(tp_ps[b][:, k * 128:(k + 1) * 128],
                                            xnt_bf[:, c * 128:(c + 1) * 128], id_bf)
                    g["sT1"].inc(ins)

        def ph1_scalar(eng):
            for i in range(NT):
                xt = xin[i % 2]
                eng.wait_ge(g["sX"].h, 16 * (i + 1))
                ins = eng.activation(out=y2f, in_=xt, func=ACT.Identity,
                                     accum_out=sc1[:, 0:1])
                g["sS1"].inc(ins)
                ins = eng.activation(out=y2f, in_=xt, func=ACT.Square,
                                     accum_out=sc1[:, 1:2])
                g["sS1"].inc(ins)
                eng.wait_ge(g["sRc"].h, 2 * i + 1)
                ins = eng.activation(out=sc1[:, 4:5], in_=sc1[:, 2:3], func=ACT.Sqrt)
                g["sSq"].inc(ins)
                eng.wait_ge(g["sRc"].h, 2 * i + 2)
                ins = eng.activation(out=sc1[:, 5:6], in_=sc1[:, 3:4], func=ACT.Sqrt)
                g["sSq"].inc(ins)
                for gq in range(4):
                    b = (i * 4 + gq) % 2
                    eng.wait_ge(g["sT1"].h, i * 4 + gq + 1)
                    dst = xn.rearrange("p (hc t) -> p hc t", hc=NH)[
                        :, 4 * gq:4 * gq + 4, i * 128:(i + 1) * 128]
                    srcp = tp_ps[b][:, 0:512].rearrange("p (hc t) -> p hc t", hc=4)
                    ins = eng.copy(out=dst, in_=srcp)
                    g["sE1"].inc(ins)

        block.sync(ph1_sync)
        block.vector(ph1_vector)
        block.tensor(ph1_tensor)
        block.scalar(ph1_scalar)

        if "xn" in dbg_aps or "stats" in dbg_aps:
            def dbg_ph1(eng):
                eng.wait_ge(g["sE1"].h, NT * 4)
                eng.wait_ge(g["sXN"].h, NT)
                if "xn" in dbg_aps:
                    eng.dma_start(out=dbg_aps["xn"], in_=xn).then_inc(g["sCON"].h, 16)
                    g["sCON"].n += 16
                if "stats" in dbg_aps:
                    st = dbg_aps["stats"]
                    eng.dma_start(out=st[:, 0:NT], in_=is1p1).then_inc(g["sCON"].h, 16)
                    eng.dma_start(out=st[:, NT:2 * NT], in_=nm1s).then_inc(
                        g["sCON"].h, 16)
                    g["sCON"].n += 32
                eng.wait_ge(g["sCON"].h, g["sCON"].n)
            block.sync(dbg_ph1)

        ph1_ps.close()
        if NPHASE == 1:
            return nc, dbg_aps

        genv_ps = ExitStack()
        pv = [genv_ps.enter_context(nc.psum_tensor(f"pv{i}", [128, 512], F32))
              for i in range(2)]

        def genv_gpsimd(eng):
            for dvb in range(4):
                if dvb >= 2:
                    eng.wait_ge(g["sTV"].h, 16 * (dvb - 1))
                src = qkv_h[:, 2 * D + dvb * 512:2 * D + (dvb + 1) * 512]
                src = src.rearrange("(hc p) n -> p hc n", p=128)
                dst = wvb[dvb % 2].rearrange("p (hc n) -> p hc n", hc=NH)
                eng.dma_start(out=dst, in_=src).then_inc(g["sWV"].h, 16)
            g["sWV"].n += 4 * 16

        def genv_tensor(eng):
            eng.wait_ge(g["sE1"].h, NT * 4)
            for dvb in range(4):
                eng.wait_ge(g["sWV"].h, 16 * (dvb + 1))
                for tc in range(NT):
                    n = dvb * NT + tc
                    b = n % 2
                    if n >= 2:
                        eng.wait_ge(g["sEV"].h, n - 1)
                    for hc in range(NH):
                        ins = eng.matmul(
                            pv[b][:],
                            xn[:, hc * D + tc * 128:hc * D + (tc + 1) * 128],
                            wvb[dvb % 2][:, hc * 512:(hc + 1) * 512],
                            start=(hc == 0), stop=(hc == NH - 1))
                    g["sTV"].inc(ins)

        def genv_vector(eng):
            for dvb in range(4):
                for tc in range(NT):
                    n = dvb * NT + tc
                    eng.wait_ge(g["sTV"].h, n + 1)
                    ins = eng.tensor_copy(
                        out=vt[:, tc * D + dvb * 512:tc * D + (dvb + 1) * 512],
                        in_=pv[n % 2][:])
                    g["sEV"].inc(ins)

        def wcast_gpsimd(eng):
            eng.dma_start(out=wbf_h, in_=W_h).then_inc(g["sWC"].h, 16)
            eng.dma_start(out=bbf_h, in_=B_h).then_inc(g["sWC"].h, 16)
            g["sWC"].n += 32

        block.gpsimd(genv_gpsimd)
        block.gpsimd(wcast_gpsimd)
        block.tensor(genv_tensor)
        block.vector(genv_vector)
        genv_ps.close()

        genk_ps = ExitStack()
        pg = [genk_ps.enter_context(nc.psum_tensor(f"pg{i}", [128, 2048], F32))
              for i in range(2)]

        def genkq_gpsimd(eng):
            for phase, base in (("k", 1 * D), ("q", 0)):
                for dkc in range(NH):
                    n = (0 if phase == "k" else NH) + dkc
                    if n >= 2:
                        sem = g["sTK"] if n - 2 < NH else g["sTQ"]
                        eng.wait_ge(sem.h, (n - 2) % NH + 1)
                    src = qkv_h[:, base + dkc * 128:base + (dkc + 1) * 128]
                    src = src.rearrange("(hc p) c -> p hc c", p=128)
                    dst = wkb[n % 2].rearrange("p (hc c) -> p hc c", hc=NH)
                    eng.dma_start(out=dst, in_=src).then_inc(g["sWK"].h, 16)
            g["sWK"].n += 2 * NH * 16

        def genkq_tensor(eng):
            eng.wait_ge(g["sEV"].h, 64)
            for dkc in range(NH):
                eng.wait_ge(g["sWK"].h, 16 * (dkc + 1))
                b = dkc % 2
                if dkc >= 2:
                    eng.wait_ge(g["sEK"].h, dkc - 1)
                for tb in range(4):
                    for hc in range(NH):
                        ins = eng.matmul(
                            pg[b][:, tb * 512:(tb + 1) * 512],
                            wkb[dkc % 2][:, hc * 128:(hc + 1) * 128],
                            xn[:, hc * D + tb * 512:hc * D + (tb + 1) * 512],
                            start=(hc == 0), stop=(hc == NH - 1))
                g["sTK"].inc(ins)
            for dkc in range(NH):
                eng.wait_ge(g["sWK"].h, 16 * (NH + dkc + 1))
                b = dkc % 2
                if dkc >= 2:
                    eng.wait_ge(g["sEQ"].h, dkc - 1)
                else:
                    eng.wait_ge(g["sEK"].h, NH)
                for tb in range(4):
                    for hc in range(NH):
                        ins = eng.matmul(
                            pg[b][:, tb * 512:(tb + 1) * 512],
                            wkb[dkc % 2][:, hc * 128:(hc + 1) * 128],
                            xn[:, hc * D + tb * 512:hc * D + (tb + 1) * 512],
                            start=(hc == 0), stop=(hc == NH - 1))
                g["sTQ"].inc(ins)

        def genkq_vector(eng):
            for dkc in range(NH):
                eng.wait_ge(g["sTK"].h, dkc + 1)
                ins = eng.tensor_copy(out=kf[:, dkc * D:(dkc + 1) * D],
                                      in_=pg[dkc % 2][:])
                g["sEK"].inc(ins)
            for dkc in range(NH):
                eng.wait_ge(g["sTQ"].h, dkc + 1)
                if dkc >= 1:
                    eng.wait_ge(g["sQS"].h, 16 * dkc)
                ins = eng.tensor_scalar(out=qstage, in0=pg[dkc % 2][:],
                                        scalar1=QSCALE, scalar2=None,
                                        op0=ALU.mult)
                g["sEQ"].inc(ins)

        def genkq_scalar(eng):
            for dkc in range(NH):
                eng.wait_ge(g["sEQ"].h, dkc + 1)
                eng.dma_start(out=qsp_h[dkc * 128:(dkc + 1) * 128, :],
                              in_=qstage).then_inc(g["sQS"].h, 16)
            g["sQS"].n += NH * 16

        block.gpsimd(genkq_gpsimd)
        block.tensor(genkq_tensor)
        block.vector(genkq_vector)
        block.scalar(genkq_scalar)
        genk_ps.close()

        if dbg_aps:
            def dbg_gen(eng):
                eng.wait_ge(g["sEK"].h, NH)
                eng.wait_ge(g["sEV"].h, 64)
                eng.wait_ge(g["sQS"].h, NH * 16)
                if "kf" in dbg_aps:
                    eng.dma_start(out=dbg_aps["kf"], in_=kf).then_inc(g["sCON"].h, 16)
                    g["sCON"].n += 16
                if "vt" in dbg_aps:
                    eng.dma_start(out=dbg_aps["vt"], in_=vt).then_inc(g["sCON"].h, 16)
                    g["sCON"].n += 16
                if "qsp" in dbg_aps:
                    eng.dma_start(out=dbg_aps["qsp"], in_=qsp_h).then_inc(
                        g["sCON"].h, 16)
                    g["sCON"].n += 16
                eng.wait_ge(g["sCON"].h, g["sCON"].n)
            block.sync(dbg_gen)

        if NPHASE == 2:
            return nc, dbg_aps

        q_sb = V(A3, NH * 512, BF16)
        pT = [V(A3 + 16384 + i * 16384, NT * 512, BF16) for i in range(2)]
        p_sb = [V(A3 + 49152 + i * 4096, D, BF16) for i in range(2)]
        ao_st = [V(WG + i * 1024, 512, BF16) for i in range(2)]
        negm = V(WG + 2048, 2, F32)
        dum2 = V(WG + 2080, 1, F32)

        ph2_ps = ExitStack()
        psc = ph2_ps.enter_context(nc.psum_tensor("psc", [128, 2048], F32))
        ptp2 = [ph2_ps.enter_context(nc.psum_tensor(f"ptp2{i}", [128, 1024], BF16))
                for i in range(2)]
        pav = [ph2_ps.enter_context(nc.psum_tensor(f"pav{i}", [128, 512], F32))
               for i in range(2)]

        def _nbt(s_tile):
            return (s_tile + 1 + 3) // 4

        def ph2_sync(eng):
            eng.wait_ge(g["sTQ"].h, NH)
            for dkc in range(NH):
                eng.wait_ge(g["sQS"].h, 16 * (dkc + 1))
                eng.dma_start(
                    out=q_sb[:, dkc * 512:(dkc + 1) * 512],
                    in_=qsp_h[dkc * 128:(dkc + 1) * 128, 0:512]).then_inc(
                    g["sQL"].h, 16)
            g["sQL"].n += NH * 16
            for gI in range(1, NG):
                eng.wait_ge(g["sTS"].h, 4 * gI)
                src = qsp_h[:, gI * 512:(gI + 1) * 512].rearrange(
                    "(dkc p) n -> p dkc n", p=128)
                dst = q_sb.rearrange("p (dkc n) -> p dkc n", dkc=NH)
                eng.dma_start(out=dst, in_=src).then_inc(g["sQL"].h, 16)
            g["sQL"].n += (NG - 1) * 16

        def ph2_tensor(eng):
            eng.wait_ge(g["sEQ"].h, NH)
            av_idx = 0
            gbt = 0
            for gI in range(NG):
                eng.wait_ge(g["sQL"].h, 16 * (NH + gI) if gI >= 1 else 16 * NH)
                for j in range(4):
                    s_tile = 4 * gI + j
                    T = (s_tile + 1) * 128
                    if gI == 0:
                        sc_base = (j % 2) * 512
                        eng.wait_ge(g["sSP"].h, max(0, s_tile - 1))
                    elif gI == 1:
                        sc_base = (j % 2) * 1024
                        eng.wait_ge(g["sSP"].h, 4 if j < 2 else s_tile - 1)
                    else:
                        sc_base = 0
                        eng.wait_ge(g["sSP"].h, s_tile)
                    nb = (T + 511) // 512
                    for tb in range(nb):
                        N = min(512, T - tb * 512)
                        for dkc in range(NH):
                            ins = eng.matmul(
                                psc[:, sc_base + tb * 512:sc_base + tb * 512 + N],
                                q_sb[:, dkc * 512 + j * 128:dkc * 512 + (j + 1) * 128],
                                kf[:, dkc * D + tb * 512:dkc * D + tb * 512 + N],
                                start=(dkc == 0), stop=(dkc == NH - 1))
                    g["sTS"].inc(ins)
                    if gI >= 1:
                        if j == 0:
                            eng.wait_ge(g["sPZ"].h, gI)
                        ntc = 4 * gI
                        for dvc in range(4 * j, 4 * j + 4):
                            b = av_idx % 2
                            if av_idx >= 2:
                                eng.wait_ge(g["sEA"].h, av_idx - 1)
                            for tc in range(ntc):
                                ins = eng.matmul(
                                    pav[b][:],
                                    vt[:, tc * D + dvc * 128:tc * D + (dvc + 1) * 128],
                                    pT[gI % 2][:, tc * 512:(tc + 1) * 512],
                                    start=(tc == 0), stop=(tc == ntc - 1))
                            g["sTA"].inc(ins)
                            av_idx += 1
                    eng.wait_ge(g["sSP"].h, s_tile + 1)
                    ntp = s_tile + 1
                    for bt in range(_nbt(s_tile)):
                        b = gbt % 2
                        if gbt >= 2:
                            eng.wait_ge(g["sVT"].h, gbt - 1)
                        for k in range(min(4, ntp - 4 * bt)):
                            tc = 4 * bt + k
                            ins = eng.transpose(
                                ptp2[b][:, k * 128:(k + 1) * 128],
                                p_sb[j % 2][:, tc * 128:(tc + 1) * 128], id_bf)
                        g["sTT"].inc(ins)
                        gbt += 1
            eng.wait_ge(g["sPZ"].h, NG)
            for dvc in range(16):
                b = av_idx % 2
                if av_idx >= 2:
                    eng.wait_ge(g["sEA"].h, av_idx - 1)
                for tc in range(16):
                    ins = eng.matmul(
                        pav[b][:],
                        vt[:, tc * D + dvc * 128:tc * D + (dvc + 1) * 128],
                        pT[0][:, tc * 512:(tc + 1) * 512],
                        start=(tc == 0), stop=(tc == 15))
                g["sTA"].inc(ins)
                av_idx += 1

        def ph2_vector(eng):
            gbt = 0
            for gI in range(NG):
                if gI >= 2:
                    eng.wait_ge(g["sTA"].h, 16 * (gI - 1))
                for j in range(4):
                    s_tile = 4 * gI + j
                    T = (s_tile + 1) * 128
                    sc_base = ((j % 2) * 512 if gI == 0 else
                               (j % 2) * 1024 if gI == 1 else 0)
                    eng.wait_ge(g["sTS"].h, s_tile + 1)
                    eng.tensor_add(psc[:, sc_base + s_tile * 128:sc_base + T],
                                   psc[:, sc_base + s_tile * 128:sc_base + T],
                                   mask_f)
                    ins = eng.reduce_max(out=negm[:, j % 2:j % 2 + 1],
                                         in_=psc[:, sc_base:sc_base + T],
                                         axis=AX.X, negate=True)
                    g["sVM"].inc(ins)
                    for bt in range(_nbt(s_tile)):
                        eng.wait_ge(g["sTT"].h, gbt + 1)
                        nk = min(4, s_tile + 1 - 4 * bt)
                        dst = pT[(gI + 1) % 2].rearrange("p (tc c) -> p tc c", tc=NT)[
                            :, 4 * bt:4 * bt + nk, j * 128:(j + 1) * 128]
                        src = ptp2[gbt % 2][:, 0:nk * 128].rearrange(
                            "p (tc c) -> p tc c", tc=nk)
                        ins = eng.tensor_copy(out=dst, in_=src)
                        g["sVT"].inc(ins)
                        gbt += 1
                    hi = 4 * (gI + 1)
                    if s_tile + 1 < hi:
                        zv = pT[(gI + 1) % 2].rearrange("p (tc c) -> p tc c", tc=NT)[
                            :, s_tile + 1:hi, j * 128:(j + 1) * 128]
                        eng.memset(zv, 0.0)
                ins = eng.tensor_copy(out=dum2, in_=negm[:, 0:1])
                g["sPZ"].inc(ins)

        def ph2_scalar(eng):
            av_idx = 0
            for gI in range(NG):
                for j in range(4):
                    s_tile = 4 * gI + j
                    T = (s_tile + 1) * 128
                    sc_base = ((j % 2) * 512 if gI == 0 else
                               (j % 2) * 1024 if gI == 1 else 0)
                    eng.wait_ge(g["sVM"].h, s_tile + 1)
                    ins = eng.activation(out=p_sb[j % 2][:, 0:T],
                                         in_=psc[:, sc_base:sc_base + T],
                                         func=ACT.Exp, bias=negm[:, j % 2:j % 2 + 1],
                                         scale=1.0,
                                         accum_out=l_sums[:, s_tile:s_tile + 1])
                    g["sSP"].inc(ins)
                    if gI >= 1:
                        for dvc in range(4 * j, 4 * j + 4):
                            eng.wait_ge(g["sTA"].h, av_idx + 1)
                            if av_idx >= 2:
                                eng.wait_ge(g["sAO"].h, 16 * (av_idx - 1))
                            b2 = av_idx % 2
                            ins = eng.copy(out=ao_st[b2], in_=pav[b2][:])
                            g["sEA"].inc(ins)
                            eng.wait_ge(g["sEA"].h, g["sEA"].n)
                            eng.dma_start(
                                out=aosp_h[dvc * 128:(dvc + 1) * 128,
                                           (gI - 1) * 512:gI * 512],
                                in_=ao_st[b2]).then_inc(g["sAO"].h, 16)
                            g["sAO"].n += 16
                            av_idx += 1
            for dvc in range(16):
                eng.wait_ge(g["sTA"].h, av_idx + 1)
                if av_idx >= 2:
                    eng.wait_ge(g["sAO"].h, 16 * (av_idx - 1))
                b2 = av_idx % 2
                ins = eng.copy(out=ao_st[b2], in_=pav[b2][:])
                g["sEA"].inc(ins)
                eng.wait_ge(g["sEA"].h, g["sEA"].n)
                eng.dma_start(out=aosp_h[dvc * 128:(dvc + 1) * 128,
                                         3 * 512:4 * 512],
                              in_=ao_st[b2]).then_inc(g["sAO"].h, 16)
                g["sAO"].n += 16
                av_idx += 1

        block.sync(ph2_sync)
        block.tensor(ph2_tensor)
        block.vector(ph2_vector)
        block.scalar(ph2_scalar)
        ph2_ps.close()

        if dbg_aps and ("ao" in dbg_aps or "lsum" in dbg_aps):
            def dbg_ph2(eng):
                eng.wait_ge(g["sAO"].h, g["sAO"].n)
                if "ao" in dbg_aps:
                    eng.dma_start(out=dbg_aps["ao"], in_=aosp_h).then_inc(
                        g["sCON"].h, 16)
                    g["sCON"].n += 16
                if "lsum" in dbg_aps:
                    eng.dma_start(out=dbg_aps["lsum"], in_=l_sums).then_inc(
                        g["sCON"].h, 16)
                    g["sCON"].n += 16
                eng.wait_ge(g["sCON"].h, g["sCON"].n)
            block.sync(dbg_ph2)

        if NPHASE == 3:
            return nc, dbg_aps

        opj_bf = V(A1, NH * D, BF16)
        WT = V(A1, NH * D, BF16)
        Bfl = V(A3 + 32768, 8 * D, BF16)
        A_bf = V(A3, NH * 1024, BF16)
        xi3 = [V(A2 + i * 8192, D, F32) for i in range(2)]
        ao_rd = V(A2 + 16384, NH * 512, BF16)
        x1_sb = V(A2 + 32768, D, F32)
        y3_sb = V(A2 + 40960, D, BF16)
        y3t_sb = V(A2 + 45056, NH * 128, BF16)
        vec_sb = V(A2 + 49152, 1024, F32)
        sq_sb = V(A2 + 53248, 1024, F32)
        G_sb = V(A2 + 57344, 1024, BF16)
        gt_sb = V(A2 + 59392, 8 * 128, BF16)
        arrows = V(A2 + 61440, 64, F32)
        scr64 = V(A2 + 61696, 64, F32)
        mask64 = V(A2 + 61952, 64, F32)
        m8 = V(A2 + 62208, 8, F32)
        v4 = V(A2 + 62240, 16, F32)
        v4s = V(A2 + 62304, 16, F32)
        st4b = V(A2 + 62368, 24, F32)
        mv2b = V(A2 + 62464, 2, F32)
        sc3 = V(A2 + 62496, 8, F32)

        ph3a_ps = ExitStack()
        po = ph3a_ps.enter_context(nc.psum_tensor("po", [128, 2048], F32))
        pvv = ph3a_ps.enter_context(nc.psum_tensor("pvv", [128, 1024], F32))
        ptp3 = [ph3a_ps.enter_context(nc.psum_tensor(f"ptp3{i}", [128, 1024], BF16))
                for i in range(2)]

        def bfl_prefetch(eng):
            for c in range(8):
                eng.dma_start_transpose(
                    out=Bfl[:, c * D:(c + 1) * D],
                    in_=bbf_h[c * D:(c + 1) * D, :]).then_inc(g["sBF"].h, 16)
            g["sBF"].n += 8 * 16

        def ph3a_gpsimd(eng):
            eng.wait_ge(g["sTS"].h, 16)
            for q4 in range(4):
                src = oproj_h[q4 * 512:(q4 + 1) * 512, :].rearrange(
                    "(dvc p) h -> p dvc h", p=128)
                dst = opj_bf.rearrange("p (dvc h) -> p dvc h", dvc=NH)[
                    :, q4 * 4:(q4 + 1) * 4, :]
                eng.dma_start(out=dst, in_=src).then_inc(g["sOW3"].h, 16)
            g["sOW3"].n += 64
            eng.wait_ge(g["sTA"].h, 64)
            src = A_h.rearrange("(hc p) ar -> p hc ar", p=128)
            eng.dma_start(out=A_bf.rearrange("p (hc ar) -> p hc ar", hc=NH),
                          in_=src).then_inc(g["sAW"].h, 16)
            g["sAW"].n += 16
            for i in range(NT):
                eng.wait_ge(g["sVX"].h, i + 1)
                eng.dma_start(out=x1sp_h[i * 128:(i + 1) * 128, :],
                              in_=x1_sb).then_inc(g["sX1W"].h, 16)
                g["sX1W"].n += 16
                eng.wait_ge(g["sVYE"].h, 4 * (i + 1))
                dst = y3sp_h.rearrange("(hc p) tc -> p hc tc", p=128)[
                    :, :, i * 128:(i + 1) * 128]
                eng.dma_start(out=dst,
                              in_=y3t_sb.rearrange("p (hc c) -> p hc c", hc=NH)
                              ).then_inc(g["sY3W"].h, 16)
                g["sY3W"].n += 16
                eng.wait_ge(g["sVGE"].h, 2 * (i + 1))
                dst = gsp_h.rearrange("(arc p) tc -> p arc tc", p=128)[
                    :, :, i * 128:(i + 1) * 128]
                eng.dma_start(out=dst,
                              in_=gt_sb.rearrange("p (arc c) -> p arc c", arc=8)
                              ).then_inc(g["sGW"].h, 16)
                g["sGW"].n += 16

        def ph3a_sync(eng):
            eng.wait_ge(g["sWC"].h, 32)
            eng.wait_ge(g["sTT"].h, 40)
            bfl_prefetch(eng)
            eng.wait_ge(g["sTA"].h, 64)
            eng.wait_ge(g["sAO"].h, 64 * 16)
            pass
            for i in range(NT):
                gi = i // 4
                if i % 4 == 0:
                    if gi >= 1:
                        eng.wait_ge(g["sTO"].h, 16 * gi)
                    src = aosp_h[:, gi * 512:(gi + 1) * 512].rearrange(
                        "(dvc p) n -> p dvc n", p=128)
                    eng.dma_start(out=ao_rd.rearrange("p (dvc n) -> p dvc n",
                                                      dvc=NH),
                                  in_=src).then_inc(g["sAL"].h, 16)
                    g["sAL"].n += 16
                if i >= 2:
                    eng.wait_ge(g["sVX"].h, i - 1)
                eng.dma_start(out=xi3[i % 2],
                              in_=x_h[i * 128:(i + 1) * 128, :]).then_inc(
                    g["sXL"].h, 16)
                g["sXL"].n += 16

        def ph3a_tensor(eng):
            eng.wait_ge(g["sEA"].h, 64)
            eng.wait_ge(g["sOW3"].h, 64)
            for it in range(NT + 2):
                if it < NT:
                    i = it
                    gi, j = i // 4, i % 4
                    eng.wait_ge(g["sAL"].h, 16 * (gi + 1))
                    if i >= 1:
                        eng.wait_ge(g["sVX"].h, i)
                    for nb in range(4):
                        for dvc in range(NH):
                            ins = eng.matmul(
                                po[:, nb * 512:(nb + 1) * 512],
                                ao_rd[:, dvc * 512 + j * 128:dvc * 512 + (j + 1) * 128],
                                opj_bf[:, dvc * D + nb * 512:dvc * D + (nb + 1) * 512],
                                start=(dvc == 0), stop=(dvc == NH - 1))
                        g["sTO"].inc(ins)
                if 1 <= it <= NT:
                    i = it - 1
                    eng.wait_ge(g["sVY"].h, i + 1)
                    for bt in range(4):
                        b = bt % 2
                        if bt < 2:
                            if i >= 2:
                                eng.wait_ge(g["sVGE"].h, 2 * (i - 2) + bt + 1)
                        else:
                            eng.wait_ge(g["sVYE"].h, 4 * i + bt - 1)
                        for k in range(4):
                            c = bt * 4 + k
                            ins = eng.transpose(ptp3[b][:, k * 128:(k + 1) * 128],
                                                y3_sb[:, c * 128:(c + 1) * 128],
                                                id_bf)
                        g["sTY"].inc(ins)
                    if i == 0:
                        eng.wait_ge(g["sAW"].h, 16)
                    eng.wait_ge(g["sVYE"].h, 4 * (i + 1))
                    if i >= 1:
                        eng.wait_ge(g["sSQ3"].h, 2 * i)
                    for hc in range(NH):
                        for n2 in range(2):
                            ins = eng.matmul(
                                pvv[:, n2 * 512:(n2 + 1) * 512],
                                y3t_sb[:, hc * 128:(hc + 1) * 128],
                                A_bf[:, hc * 1024 + n2 * 512:hc * 1024 + (n2 + 1) * 512],
                                start=(hc == 0), stop=(hc == NH - 1))
                    g["sTVc"].inc(ins)
                if it >= 2:
                    i = it - 2
                    eng.wait_ge(g["sVG"].h, i + 1)
                    for bt in range(2):
                        b = bt % 2
                        if it <= NT:
                            eng.wait_ge(g["sVYE"].h, 4 * (it - 1) + bt + 3)
                        else:
                            eng.wait_ge(g["sVYE"].h, 4 * (it - 2) + bt + 3)
                        for k in range(4):
                            c = bt * 4 + k
                            ins = eng.transpose(ptp3[b][:, k * 128:(k + 1) * 128],
                                                G_sb[:, c * 128:(c + 1) * 128],
                                                id_bf)
                        g["sTG"].inc(ins)

        def ph3a_vector(eng):
            eng.wait_ge(g["sSP"].h, 16)
            ins = eng.reciprocal(out=inv_l, in_=l_sums)
            g["sV3"].inc(ins)
            eng.wait_ge(g["sV3"].h, g["sV3"].n)
            for i in range(NT):
                eng.wait_ge(g["sTR"].h, i + 1)
                eng.wait_ge(g["sTO"].h, 4 * (i + 1))
                if i >= 1:
                    eng.wait_ge(g["sX1W"].h, 16 * i)
                ins = eng.scalar_tensor_tensor(out=x1_sb, in0=po[:],
                                               scalar=inv_l[:, i:i + 1],
                                               in1=xi3[i % 2],
                                               op0=ALU.mult, op1=ALU.add)
                g["sVX"].inc(ins)
                for k in range(4):
                    ins = eng.bn_stats(out=st4b[:, 6 * k:6 * k + 6],
                                       in_=x1_sb[:, 512 * k:512 * (k + 1)])
                g["sV3"].inc(ins)
                eng.wait_ge(g["sV3"].h, g["sV3"].n)
                ins = eng.bn_aggr(out=mv2b, in_=st4b)
                g["sV3"].inc(ins)
                eng.wait_ge(g["sV3"].h, g["sV3"].n)
                ins = eng.tensor_scalar(out=sc3[:, 0:1], in0=mv2b[:, 1:2],
                                        scalar1=EPS, scalar2=None, op0=ALU.add)
                g["sRc3"].inc(ins)
                eng.wait_ge(g["sSq3"].h, i + 1)
                ins = eng.reciprocal(out=sc3[:, 2:3], in_=sc3[:, 1:2])
                g["sV3"].inc(ins)
                eng.wait_ge(g["sV3"].h, g["sV3"].n)
                ins = eng.tensor_scalar(out=sc3[:, 3:4], in0=mv2b[:, 0:1],
                                        scalar1=sc3[:, 2:3], scalar2=-1.0,
                                        op0=ALU.mult, op1=ALU.mult)
                g["sV3"].inc(ins)
                eng.wait_ge(g["sV3"].h, g["sV3"].n)
                if i >= 1:
                    eng.wait_ge(g["sTY"].h, 4 * i)
                ins = eng.tensor_scalar(out=y3_sb, in0=x1_sb,
                                        scalar1=sc3[:, 2:3], scalar2=sc3[:, 3:4],
                                        op0=ALU.mult, op1=ALU.add)
                g["sVY"].inc(ins)
                if "dvec" in dbg_aps and i == 1:
                    pass
                eng.wait_ge(g["sSQ3"].h, 2 * (i + 1))
                bview = sq_sb[:].rearrange("p (a r) -> p a r", a=64)
                ins = eng.reduce_sum(out=arrows, in_=bview, axis=AX.X)
                g["sVK"].inc(ins)
                eng.wait_ge(g["sVK"].h, g["sVK"].n)
                ins = eng.max(out=m8, in_=arrows)
                g["sVK"].inc(ins)
                eng.wait_ge(g["sVK"].h, g["sVK"].n)
                ins = eng.memset(m8[:, 4:8], -1.0)
                g["sVK"].inc(ins)
                eng.wait_ge(g["sVK"].h, g["sVK"].n)
                ins = eng.match_replace(out=scr64, in_to_replace=m8,
                                        in_values=arrows, imm_value=-1.0)
                g["sVK"].inc(ins)
                eng.wait_ge(g["sVK"].h, g["sVK"].n)
                ins = eng.tensor_tensor(out=mask64, in0=arrows, in1=scr64,
                                        op=ALU.not_equal)
                g["sVK"].inc(ins)
                eng.wait_ge(g["sVK"].h, g["sVK"].n)
                mb = mask64.unsqueeze(2).to_broadcast([128, 64, 16])
                ins = eng.tensor_tensor(
                    out=sq_sb[:].rearrange("p (a r) -> p a r", a=64),
                    in0=vec_sb[:].rearrange("p (a r) -> p a r", a=64),
                    in1=mb, op=ALU.mult)
                g["sVK"].inc(ins)
                eng.wait_ge(g["sVK"].h, g["sVK"].n)
                swap = sq_sb[:].rearrange("p (a r) -> p a r", a=64).transpose(
                    [0, 2, 1])
                ins = eng.reduce_sum(out=v4, in_=swap, axis=AX.X)
                g["sVK"].inc(ins)
                eng.wait_ge(g["sVK"].h, g["sVK"].n)
                ins = eng.tensor_scalar(out=v4s, in0=v4, scalar1=0.0625,
                                        scalar2=None, op0=ALU.mult)
                g["sVK"].inc(ins)
                eng.wait_ge(g["sVK"].h, g["sVK"].n)
                vb = v4s.unsqueeze(1).to_broadcast([128, 64, 16])
                ins = eng.tensor_copy(
                    out=sq_sb[:].rearrange("p (a r) -> p a r", a=64), in_=vb)
                g["sVK"].inc(ins)
                eng.wait_ge(g["sVK"].h, g["sVK"].n)
                if i >= 1:
                    eng.wait_ge(g["sTG"].h, 2 * i)
                ins = eng.tensor_tensor(
                    out=G_sb[:].rearrange("p (a r) -> p a r", a=64),
                    in0=sq_sb[:].rearrange("p (a r) -> p a r", a=64),
                    in1=mb, op=ALU.mult)
                g["sVG"].inc(ins)
                if "dvec" in dbg_aps and i == 0:
                    dsnap = V(WG, 2192, F32)
                    eng.tensor_copy(out=dsnap[:, 0:1024], in_=vec_sb)
                    eng.tensor_copy(out=dsnap[:, 1024:2048], in_=sq_sb)
                    eng.tensor_copy(out=dsnap[:, 2048:2112], in_=arrows)
                    eng.tensor_copy(out=dsnap[:, 2112:2176], in_=mask64)
                    ins = eng.tensor_copy(out=dsnap[:, 2176:2192], in_=v4)
                    g["sDBG"].inc(ins)

        def ph3a_scalar(eng):
            for it in range(NT + 1):
                if it < NT:
                    i = it
                    eng.wait_ge(g["sXL"].h, 16 * (i + 1))
                    ins = eng.activation(out=xi3[i % 2], in_=xi3[i % 2],
                                         func=ACT.Identity,
                                         scale=is1p1[:, i:i + 1],
                                         bias=nm1s[:, i:i + 1])
                    g["sTR"].inc(ins)
                    eng.wait_ge(g["sRc3"].h, i + 1)
                    ins = eng.activation(out=sc3[:, 1:2], in_=sc3[:, 0:1],
                                         func=ACT.Sqrt)
                    g["sSq3"].inc(ins)
                    for bt in range(4):
                        nb3 = i * 4 + bt
                        eng.wait_ge(g["sTY"].h, nb3 + 1)
                        if i >= 1 and bt == 0:
                            eng.wait_ge(g["sY3W"].h, 16 * i)
                        dst = y3t_sb.rearrange("p (hc c) -> p hc c", hc=NH)[
                            :, 4 * bt:4 * bt + 4, :]
                        srcp = ptp3[bt % 2][:, 0:512].rearrange(
                            "p (hc c) -> p hc c", hc=4)
                        ins = eng.copy(out=dst, in_=srcp)
                        g["sVYE"].inc(ins)
                    eng.wait_ge(g["sTVc"].h, i + 1)
                    ins = eng.square(out=sq_sb, in_=pvv[:])
                    g["sSQ3"].inc(ins)
                    ins = eng.copy(out=vec_sb, in_=pvv[:])
                    g["sSQ3"].inc(ins)
                if it >= 1:
                    i = it - 1
                    for bt in range(2):
                        nb3 = i * 2 + bt
                        eng.wait_ge(g["sTG"].h, nb3 + 1)
                        if i >= 1 and bt == 0:
                            eng.wait_ge(g["sGW"].h, 16 * i)
                        dst = gt_sb.rearrange("p (arc c) -> p arc c", arc=8)[
                            :, 4 * bt:4 * bt + 4, :]
                        srcp = ptp3[bt % 2][:, 0:512].rearrange(
                            "p (arc c) -> p arc c", arc=4)
                        ins = eng.copy(out=dst, in_=srcp)
                        g["sVGE"].inc(ins)

        block.gpsimd(ph3a_gpsimd)
        block.sync(ph3a_sync)
        block.tensor(ph3a_tensor)
        block.vector(ph3a_vector)
        block.scalar(ph3a_scalar)
        ph3a_ps.close()

        if "dvec" in dbg_aps:
            def dbg_dvec2(eng):
                eng.wait_ge(g["sDBG"].h, 1)
                dsnap = V(WG, 2192, F32)
                eng.dma_start(out=dbg_aps["dvec"], in_=dsnap).then_inc(
                    g["sCON"].h, 16)
                g["sCON"].n += 16
                eng.wait_ge(g["sCON"].h, g["sCON"].n)
            block.sync(dbg_dvec2)
        if dbg_aps and ("x1" in dbg_aps or "y3" in dbg_aps or "gsp" in dbg_aps):
            def dbg_ph3a(eng):
                eng.wait_ge(g["sX1W"].h, NT * 16)
                eng.wait_ge(g["sY3W"].h, NT * 16)
                eng.wait_ge(g["sGW"].h, NT * 16)
                for nm, src in (("x1", x1sp_h), ("y3", y3sp_h), ("gsp", gsp_h)):
                    if nm in dbg_aps:
                        eng.dma_start(out=dbg_aps[nm], in_=src).then_inc(
                            g["sCON"].h, 16)
                        g["sCON"].n += 16
                eng.wait_ge(g["sCON"].h, g["sCON"].n)
            block.sync(dbg_ph3a)

        if NPHASE == 4:
            return nc, dbg_aps

        y3r = [V(A2 + i * 4096, NH * 128, BF16) for i in range(2)]
        gtr = [V(A2 + 8192 + i * 2048, 8 * 128, BF16) for i in range(2)]
        x1r = [V(A2 + 12288 + i * 8192, D, F32) for i in range(2)]
        out_sb = [V(A2 + 28672 + i * 8192, D, F32) for i in range(2)]
        brep3 = V(A2 + 45056, D, F32)

        ph3b_ps = ExitStack()
        pout = [ph3b_ps.enter_context(nc.psum_tensor(f"pout{i}", [128, 2048], F32))
                for i in range(2)]

        def ph3b_sync(eng):
            eng.wait_ge(g["sWC"].h, 32)
            eng.wait_ge(g["sTO"].h, 64)
            for hc in range(NH):
                eng.dma_start_transpose(out=WT[:, hc * D:(hc + 1) * D],
                                        in_=wbf_h[:, hc * 128:(hc + 1) * 128]
                                        ).then_inc(g["sWT"].h, 16)
            g["sWT"].n += NH * 16
            eng.wait_ge(g["sVGE"].h, 32)
            eng.wait_ge(g["sVX"].h, 16)
            eng.wait_ge(g["sSQ3"].h, 32)
            eng.dma_start(out=brep3, in_=bias_h.partition_broadcast(128)).then_inc(
                g["sL3"].h, 16)
            g["sL3"].n += 16
            for i in range(NT):
                if i >= 2:
                    eng.wait_ge(g["sVO"].h, i - 1)
                eng.wait_ge(g["sY3W"].h, 16 * (i + 1))
                src = y3sp_h.rearrange("(hc p) tc -> p hc tc", p=128)[
                    :, :, i * 128:(i + 1) * 128]
                eng.dma_start(out=y3r[i % 2].rearrange("p (hc c) -> p hc c", hc=NH),
                              in_=src).then_inc(g["sL3"].h, 16)
                eng.wait_ge(g["sGW"].h, 16 * (i + 1))
                src = gsp_h.rearrange("(arc p) tc -> p arc tc", p=128)[
                    :, :, i * 128:(i + 1) * 128]
                eng.dma_start(out=gtr[i % 2].rearrange("p (arc c) -> p arc c",
                                                       arc=8),
                              in_=src).then_inc(g["sL3"].h, 16)
                eng.wait_ge(g["sX1W"].h, 16 * (i + 1))
                eng.dma_start(out=x1r[i % 2],
                              in_=x1sp_h[i * 128:(i + 1) * 128, :]).then_inc(
                    g["sL3"].h, 16)
                g["sL3"].n += 48

        def ph3b_tensor(eng):
            eng.wait_ge(g["sBF"].h, 8 * 16)
            eng.wait_ge(g["sVGE"].h, 32)
            eng.wait_ge(g["sVX"].h, 16)
            eng.wait_ge(g["sSQ3"].h, 32)
            for i in range(NT):
                b = i % 2
                eng.wait_ge(g["sL3"].h, 16 + 48 * (i + 1))
                if i >= 2:
                    eng.wait_ge(g["sVO"].h, i - 1)
                for nb in range(4):
                    for arc in range(8):
                        ins = eng.matmul(
                            pout[b][:, nb * 512:(nb + 1) * 512],
                            gtr[b][:, arc * 128:(arc + 1) * 128],
                            Bfl[:, arc * D + nb * 512:arc * D + (nb + 1) * 512],
                            start=(arc == 0), stop=False)
                    for hc in range(NH):
                        if i == 0 and nb == 0:
                            eng.wait_ge(g["sWT"].h, 16 * (hc + 1))
                        ins = eng.matmul(
                            pout[b][:, nb * 512:(nb + 1) * 512],
                            y3r[b][:, hc * 128:(hc + 1) * 128],
                            WT[:, hc * D + nb * 512:hc * D + (nb + 1) * 512],
                            start=False, stop=(hc == NH - 1))
                    g["sT3"].inc(ins)

        def ph3b_vector(eng):
            for i in range(NT):
                b = i % 2
                eng.wait_ge(g["sT3"].h, 4 * (i + 1))
                if i >= 2:
                    eng.wait_ge(g["sOW"].h, 16 * (i - 1))
                eng.tensor_add(out_sb[b], pout[b][:], x1r[b])
                ins = eng.tensor_add(out_sb[b], out_sb[b], brep3)
                g["sVO"].inc(ins)

        def ph3b_scalar(eng):
            for i in range(NT):
                eng.wait_ge(g["sVO"].h, i + 1)
                eng.dma_start(out=out_h[i * 128:(i + 1) * 128, :],
                              in_=out_sb[i % 2]).then_inc(g["sOW"].h, 16)
                g["sOW"].n += 16
            eng.wait_ge(g["sOW"].h, NT * 16)

        block.sync(ph3b_sync)
        block.tensor(ph3b_tensor)
        block.vector(ph3b_vector)
        block.scalar(ph3b_scalar)
        ph3b_ps.close()

        return nc, dbg_aps


def _prep_inputs(L0_attn_x, L0_attn_qkv, L0_attn_o_proj, L0_attn_gamma,
                 L0_attn_beta, L0_ffn_A, L0_ffn_B, L0_ffn_W, L0_ffn_bias):
    common = {
        "qkv": np.ascontiguousarray(L0_attn_qkv, np.float32),
        "o_proj": np.ascontiguousarray(L0_attn_o_proj, np.float32),
        "gamma": np.ascontiguousarray(L0_attn_gamma, np.float32).reshape(1, D),
        "beta": np.ascontiguousarray(L0_attn_beta, np.float32).reshape(1, D),
        "A": np.ascontiguousarray(
            np.asarray(L0_ffn_A, np.float32).reshape(1024, D).T),
        "B": np.ascontiguousarray(
            np.asarray(L0_ffn_B, np.float32).reshape(8, 8, 2048, 16)
            .transpose(0, 2, 1, 3).reshape(8 * 2048, 128)),
        "W": np.ascontiguousarray(L0_ffn_W, np.float32),
        "bias": np.ascontiguousarray(L0_ffn_bias, np.float32).reshape(1, D),
    }
    in_maps = []
    for c in range(8):
        m = dict(common)
        m["x"] = np.ascontiguousarray(L0_attn_x[c], np.float32)
        in_maps.append(m)
    return in_maps


def kernel(L0_attn_x, L0_attn_qkv, L0_attn_o_proj, L0_attn_gamma, L0_attn_beta,
           L0_ffn_A, L0_ffn_B, L0_ffn_W, L0_ffn_bias):
    in_maps = _prep_inputs(L0_attn_x, L0_attn_qkv, L0_attn_o_proj, L0_attn_gamma,
                           L0_attn_beta, L0_ffn_A, L0_ffn_B, L0_ffn_W, L0_ffn_bias)
    nc, _ = build()
    res = run_bass_kernel_spmd(nc, in_maps, core_ids=list(range(8)))
    out = np.stack([np.asarray(r["out"]) for r in res.results], axis=0)
    return out.astype(np.float32)
